# version 09
# speedup vs baseline: 1.3366x; 1.0181x over previous
import sys
import numpy as np
import concourse.bass as bass
import concourse.mybir as mybir
from concourse.bass_utils import run_bass_kernel_spmd

F32 = mybir.dt.float32
BF16 = mybir.dt.bfloat16
AF = mybir.ActivationFunctionType
ALU = mybir.AluOpType
AX = mybir.AxisListType

NCORES = 8
_SKIP = set()
TOK = 2048
D = 1024
NCHT = 8
CH = 256
NCH = TOK // CH
W = NCHT * CH
EPS = 1e-5
SQD = 32.0

ARENA = 52224

OF_IDF = 0
OF_IDB = 128
OF_ONEF = 192
OF_ONEB = 320
OF_RW = 448
OF_LAM = 512
OF_G1 = 520
OF_G2 = 528
OF_SEL = 536
OF_ISEL = 544
OF_LAC = 552
OF_RTS = 560
OF_TMPC = 568
OF_AUC = 600
OF_G = 616
OF_HIN = 680
OF_CCS = 688
OF_CCL = 816
OF_EXB = 944
OF_OH = 1456
OF_BIAS = 1968
OF_RB = 4016
OF_SS = 4032
OF_SQ = 4048
OF_MC = 4064
OF_VAR = 4080
OF_SVE = 4096
OF_Q1 = 4112
OF_RS = 4128
OF_C1 = 4144
OF_M2T = 4160
OF_ROWS = 4176
OF_GM = 5456
OF_SCOL = 5472
OF_GMT = 5504
OF_HCOL = 5760
OF_EXPS = 5768
OF_PERS_END = 5784

OF_T = [5792 + i * 4096 for i in range(8)]
OF_SCR = OF_T[7] + 4096
assert OF_SCR == 38560
SCR_END = ARENA


def _alloc(start, end, sizes):
    out = {}
    o = start
    for name, n in sizes:
        out[name] = o
        o += n
    assert o <= end, (o, end)
    return out


P1 = _alloc(OF_SCR, SCR_END, [
    ("B2", W), ("B3", W), ("B4", W), ("B5", W),
    ("r0fm", W // 2), ("xgelbf", W // 2),
])
P1.update(_alloc(OF_T[4], OF_T[6], [
    ("x0a", D), ("x0b", D), ("r0ta", D // 2), ("r0tb", D // 2),
    ("xna", D // 2), ("xnb", D // 2), ("sqscr", D),
    ("xnfa", W // 2), ("xnfb", W // 2),
]))
P1["stage"] = OF_T[6]
P1.update(_alloc(OF_T[6], OF_T[7] + 4096, [
    ("B1", W), ("linbf", W // 2), ("prebf", W // 2),
]))

P3 = _alloc(OF_SCR, SCR_END, [
    ("rga", 3 * W), ("rgb", 3 * W),
])
P3.update(_alloc(OF_T[0], OF_T[4], [
    ("h", W), ("mixa", W // 2), ("mixb", W // 2), ("r1", W), ("xmbf", W // 2),
    ("sig", W), ("x1p", W), ("xga", W // 2), ("xgb", W // 2),
    ("s2t", CH), ("r1b", W),
]))

PC = _alloc(OF_SCR, SCR_END, [
    ("x0a", D), ("x0b", D),
    ("x1A", W), ("x1B", W),
    ("xt", W), ("xtbf", W // 2), ("sqbf", W // 2),
    ("xg0", W // 2), ("xg1", W // 2),
    ("gmt2", 256), ("y", D),
])


class Sched:
    def __init__(self):
        self.cnt = {}
        self.ms = {}

    def inc(self, sem, n, ms=None):
        v = self.cnt[sem] = self.cnt.get(sem, 0) + n
        if ms is not None:
            self.ms[ms] = (sem, v)
        return v


def build(debug=False, upto=4, p01n=None, p01s=5):
    nc = bass.Bass("TRN2", target_bir_lowering=False,
                   detect_race_conditions=False)

    x_in = nc.dram_tensor("x", [TOK, D], F32, kind="ExternalInput").ap()
    wname = ["lin", "conv", "wa", "wx", "lin2", "w1", "w2", "w3"] + \
            [f"e{i}" for i in range(8)]
    wpk = {n: nc.dram_tensor("wpk_" + n, [128, 8192], F32, kind="ExternalInput").ap()
           for n in wname}
    bias1 = nc.dram_tensor("bias1", [1, 4096], F32, kind="ExternalInput").ap()
    bias2 = nc.dram_tensor("bias2", [1, 4096], F32, kind="ExternalInput").ap()
    exb_in = nc.dram_tensor("exb", [8, D], F32, kind="ExternalInput").ap()
    rw_in = nc.dram_tensor("rw", [128, 64], F32, kind="ExternalInput").ap()
    rb_in = nc.dram_tensor("rb", [1, 8], F32, kind="ExternalInput").ap()
    lam_in = nc.dram_tensor("lam", [128, 8], F32, kind="ExternalInput").ap()
    g1_in = nc.dram_tensor("g1c", [128, 8], F32, kind="ExternalInput").ap()
    g2_in = nc.dram_tensor("g2c", [128, 8], F32, kind="ExternalInput").ap()
    sel_in = nc.dram_tensor("sel", [128, 8], F32, kind="ExternalInput").ap()
    isel_in = nc.dram_tensor("isel", [128, 8], F32, kind="ExternalInput").ap()
    oh_in = nc.dram_tensor("oh", [8, D], F32, kind="ExternalInput").ap()
    id_in = nc.dram_tensor("ident", [128, 128], F32, kind="ExternalInput").ap()
    out = nc.dram_tensor("out", [TOK, D], F32, kind="ExternalOutput").ap()

    skind = "ExternalOutput" if debug else "Internal"
    scr_r0 = nc.dram_tensor("scr_r0", [128, NCH * W], BF16, kind=skind).ap()
    scr_a = nc.dram_tensor("scr_a", [128, NCH * W], F32, kind=skind).ap()
    scr_g = nc.dram_tensor("scr_g", [128, NCH * W], F32, kind=skind).ap()
    scr_xg = nc.dram_tensor("scr_xg", [128, NCH * W], BF16, kind=skind).ap()
    scr_x1p = nc.dram_tensor("scr_x1p", [128, NCH * W], F32, kind=skind).ap()
    cc_in = nc.dram_tensor("cc_in", [16, 128], F32).ap()
    cc_out = nc.dram_tensor("cc_out", [64, 128], F32).ap()

    arena_h = nc.alloc_sbuf_tensor("arena", [128, ARENA], F32)
    A = arena_h.ap()
    Ab = A.bitcast(BF16)

    def fv(off, n, p0=0, p1=128):
        return A[p0:p1, off:off + n]

    def bv(off, nbf, p0=0, p1=128):
        return Ab[p0:p1, 2 * off:2 * off + nbf]

    def f3(off):
        return fv(off, W).rearrange("p (a b) -> p a b", a=NCHT)

    def b3(off):
        return bv(off, W).rearrange("p (a b) -> p a b", a=NCHT)

    sch = Sched()

    import contextlib
    SEM_NAMES = (["pe", "ac", "dv", "cc",
                  "dxa", "dxb",
                  "dW", "dw2",
                  "dxg", "dr0", "dx1",
                  "dsa", "dsg",
                  "dr3a", "dr3b",
                  "dpa", "dpb",
                  "dsta", "dstb",
                  "dcc"]
                 + [f"de{i}" for i in range(8)])
    with contextlib.ExitStack() as _es:
        PS = _es.enter_context(nc.psum_tensor("PS", [128, 4096], F32))
        SEM = {n: _es.enter_context(nc.semaphore(n)) for n in SEM_NAMES}
        S_cc = SEM["cc"]
        block = _es.enter_context(nc.Block())
        PSf = PS[:]
        PSb = PSf.bitcast(BF16)

        def bank(b, n=1):
            return PSf[:, b * 512:(b + n) * 512]

        def emit_phase(fn, engines=("sync", "gpsimd", "tensor", "scalar", "vector")):
            snap0 = dict(sch.cnt)
            for e in engines:
                fn(e, None)
            snap1 = dict(sch.cnt)
            emitters = {"sync": block.sync, "gpsimd": block.gpsimd,
                        "tensor": block.tensor, "scalar": block.scalar,
                        "vector": block.vector}
            for e in engines:
                sch.cnt = dict(snap0)
                emitters[e](lambda h, e=e: fn(e, h))
            sch.cnt = dict(snap1)

        def wait(h, ms):
            if h is not None:
                sem, v = sch.ms[ms]
                h.wait_ge(SEM[sem], v)

        def tick(h, inst, sem, n, ms=None):
            if h is not None and inst is not None:
                inst.then_inc(SEM[sem], n)
            return sch.inc(sem, n, ms)

        def fence(h, inst, eng):
            v = sch.inc(eng, 1)
            if h is not None and inst is not None:
                inst.then_inc(SEM[eng], 1)
                h.wait_ge(SEM[eng], v)
            return v

        def full_mm(h, slot_off, rhs_off, b0):
            last = None
            r3 = b3(rhs_off)
            for m in range(8):
                for k in range(8):
                    if h:
                        last = h.matmul(
                            PSf[:, b0 * 512 + m * CH:b0 * 512 + m * CH + CH],
                            bv(slot_off + (k * 8 + m) * 64, 128),
                            r3[:, k, :],
                            start=(k == 0 and m % 2 == 0), stop=False)
            return last

        def bias_mm(h, bslot, b0, last):
            for m in range(8):
                if h:
                    last = h.matmul(
                        PSf[:, b0 * 512 + m * CH:b0 * 512 + m * CH + CH],
                        bv(OF_BIAS + bslot * 512 + m * 64, 128, 0, 1),
                        bv(OF_ONEB, CH, 0, 1),
                        start=False, stop=(m % 2 == 1))
            return last

        def ph_setup(e, h):
            if e == "gpsimd":
                def dm(o, i):
                    return h.dma_start(out=o, in_=i) if h else None
                tick(h, dm(fv(OF_IDF, 128), id_in), "dW", 16)
                tick(h, dm(bv(OF_IDB, 128), id_in), "dW", 16)
                tick(h, dm(fv(OF_RW, 64), rw_in), "dW", 16)
                tick(h, dm(fv(OF_RB, 8, 0, 1), rb_in), "dW", 16)
                tick(h, dm(fv(OF_LAM, 8), lam_in), "dW", 16)
                tick(h, dm(fv(OF_G1, 8), g1_in), "dW", 16)
                tick(h, dm(fv(OF_G2, 8), g2_in), "dW", 16)
                tick(h, dm(fv(OF_SEL, 8), sel_in), "dW", 16)
                tick(h, dm(fv(OF_ISEL, 8), isel_in), "dW", 16)
                tick(h, dm(bv(OF_EXB, 1024, 0, 8), exb_in), "dW", 16)
                tick(h, dm(bv(OF_OH, 1024, 0, 8), oh_in), "dW", 16)
                tick(h, dm(bv(OF_BIAS, 4096, 0, 1), bias1), "dW", 16)
                tick(h, dm(fv(P1["stage"], 4096), wpk["lin"][:, 0:4096]), "dW", 16)
                tick(h, dm(fv(P1["stage"] + 4096, 4096), wpk["lin"][:, 4096:8192]),
                     "dW", 16)
                tick(h, dm(bv(OF_T[1], 8192), wpk["conv"]), "dW", 16)
                tick(h, dm(bv(OF_T[2], 8192), wpk["wa"]), "dW", 16)
                tick(h, dm(bv(OF_T[3], 8192), wpk["wx"]), "dW", 16, "setup_done")
            elif e == "scalar":
                wait(h, "setup_done")
                i = h.activation(fv(OF_TMPC, 8), fv(OF_LAM, 8), AF.Sigmoid) if h else None
                fence(h, i, "ac")
                i = h.activation(fv(OF_LAC, 8), fv(OF_TMPC, 8), AF.Ln) if h else None
                tick(h, i, "ac", 1, "lac_ln")
            elif e == "vector":
                i = None
                if h:
                    h.memset(fv(OF_ONEF, 128), 1.0)
                    h.memset(bv(OF_ONEB, 256), 1.0)
                    i = h.memset(fv(OF_RTS, 8), 0.0)
                tick(h, i, "dv", 1)
                wait(h, "setup_done")
                last = None
                for k in range(8):
                    if h:
                        last = h.tensor_scalar(
                            bv(OF_T[0] + k * 512, 1024),
                            fv(P1["stage"] + k * 1024, 1024),
                            fv(OF_G1 + k, 1), None, ALU.mult)
                tick(h, last, "dv", 1, "fold_lin")
                wait(h, "lac_ln")
                i = h.tensor_scalar(fv(OF_LAC, 8), fv(OF_LAC, 8), 0.125,
                                    None, ALU.mult) if h else None
                tick(h, i, "dv", 1, "lac_done")

        emit_phase(ph_setup, engines=("gpsimd", "scalar", "vector"))

        def x0s(tt):
            return P1["x0a"] if tt % 2 == 0 else P1["x0b"]

        def r0ts(tt):
            return P1["r0ta"] if tt % 2 == 0 else P1["r0tb"]

        def xns(tt):
            return P1["xna"] if tt % 2 == 0 else P1["xnb"]

        def xnfm(c):
            return P1["xnfa"] if c % 2 == 0 else P1["xnfb"]

        def ph01(e, h):
            def fe_act1(cc):
                for tl in range(2):
                    tt = 2 * cc + tl
                    wait(h, f"x0_{tt}")
                    if tt >= 1:
                        wait(h, f"ssq_{tt - 1}")
                    i = h.activation(fv(P1["sqscr"], D), fv(x0s(tt), D),
                                     AF.Square) if h else None
                    tick(h, i, "ac", 1, f"sq_{tt}")

            def fe_act2(cc):
                wait(h, f"var_{cc}")
                i = None
                if h:
                    h.activation(fv(OF_SVE + 2 * cc, 2), fv(OF_VAR + 2 * cc, 2),
                                 AF.Sqrt, bias=np.float32(EPS))
                    i = h.activation(fv(OF_Q1 + 2 * cc, 2), fv(OF_VAR + 2 * cc, 2),
                                     AF.Sqrt, scale=np.float32(D))
                tick(h, i, "ac", 1, f"sveq_{cc}")

            def fe_act3(cc):
                for tl in range(2):
                    tt = 2 * cc + tl
                    wait(h, f"T_{tt}")
                    if cc >= 1 and tl == 0:
                        wait(h, f"spr0_{cc - 1}")
                    rb = 4096 if tt % 2 == 0 else 6144
                    xb = 5120 if tt % 2 == 0 else 7168
                    i = h.activation(
                        bv(P1["r0fm"], W).rearrange("p (a b) -> p a b", a=8)
                        [:, :, tl * 128:tl * 128 + 128],
                        PSb[:, rb:rb + 1024].rearrange("p (a b) -> p a b", a=8),
                        AF.Copy) if h else None
                    tick(h, i, "ac", 1, f"r0e_{tt}")
                    i = h.activation(
                        b3(xnfm(cc))[:, :, tl * 128:tl * 128 + 128],
                        PSb[:, xb:xb + 1024].rearrange("p (a b) -> p a b", a=8),
                        AF.Copy) if h else None
                    if tl == 1:
                        fence(h, i, "ac")
                        sch.ms[f"xne_{tt}"] = ("ac", sch.cnt["ac"])
                    else:
                        tick(h, i, "ac", 1, f"xne_{tt}")
                d = h.dma_start(out=scr_r0[:, cc * W:(cc + 1) * W],
                                in_=bv(P1["r0fm"], W)) if h else None
                tick(h, d, "dr0", 16, f"spr0_{cc}")

            def fe_dve1(cc):
                for tl in range(2):
                    tt = 2 * cc + tl
                    wait(h, f"x0_{tt}")
                    i = h.reduce_sum(fv(OF_SS + tt, 1), fv(x0s(tt), D),
                                     AX.X) if h else None
                    tick(h, i, "dv", 1, f"ssum_{tt}")
                    wait(h, f"sq_{tt}")
                    i = h.reduce_sum(fv(OF_SQ + tt, 1), fv(P1["sqscr"], D),
                                     AX.X) if h else None
                    tick(h, i, "dv", 1, f"ssq_{tt}")
                i = h.tensor_scalar(fv(OF_MC + 2 * cc, 2), fv(OF_SS + 2 * cc, 2),
                                    1.0 / D, None, ALU.mult) if h else None
                fence(h, i, "dv")
                i = h.tensor_tensor(fv(OF_M2T + 2 * cc, 2), fv(OF_MC + 2 * cc, 2),
                                    fv(OF_MC + 2 * cc, 2), ALU.mult) if h else None
                fence(h, i, "dv")
                i = h.scalar_tensor_tensor(
                    fv(OF_VAR + 2 * cc, 2), fv(OF_SQ + 2 * cc, 2), 1.0 / D,
                    fv(OF_M2T + 2 * cc, 2), ALU.mult, ALU.subtract) if h else None
                tick(h, i, "dv", 1, f"var_{cc}")

            def fe_dve2(cc):
                wait(h, f"sveq_{cc}")
                i = h.reciprocal(fv(OF_RS + 2 * cc, 2), fv(OF_SVE + 2 * cc, 2)) if h else None
                fence(h, i, "dv")
                i = h.tensor_tensor(fv(OF_M2T + 2 * cc, 2), fv(OF_Q1 + 2 * cc, 2),
                                    fv(OF_RS + 2 * cc, 2), ALU.mult) if h else None
                fence(h, i, "dv")
                i = h.tensor_scalar(fv(OF_M2T + 2 * cc, 2), fv(OF_M2T + 2 * cc, 2),
                                    1e-12, None, ALU.max) if h else None
                fence(h, i, "dv")
                i = h.reciprocal(fv(OF_M2T + 2 * cc, 2), fv(OF_M2T + 2 * cc, 2)) if h else None
                fence(h, i, "dv")
                i = h.scalar_tensor_tensor(
                    fv(OF_C1 + 2 * cc, 2), fv(OF_M2T + 2 * cc, 2), SQD,
                    fv(OF_RS + 2 * cc, 2), ALU.mult, ALU.mult) if h else None
                fence(h, i, "dv")
                for tl in range(2):
                    tt = 2 * cc + tl
                    if tt >= 2:
                        wait(h, f"T_{tt - 2}")
                    i = None
                    if h:
                        h.tensor_scalar(bv(r0ts(tt), 1024), fv(x0s(tt), D),
                                        fv(OF_MC + tt, 1), fv(OF_RS + tt, 1),
                                        ALU.subtract, ALU.mult)
                        i = h.tensor_scalar(bv(xns(tt), 1024), fv(x0s(tt), D),
                                            fv(OF_MC + tt, 1), fv(OF_C1 + tt, 1),
                                            ALU.subtract, ALU.mult)
                    tick(h, i, "dv", 1, f"apply_{cc}" if tl == 1 else None)

            if e == "sync":
                for tt in range(2 * NCH_RUN):
                    if tt >= 2:
                        wait(h, f"apply_{tt // 2 - 1}")
                        wait(h, f"sq_{tt - 2}")
                    d = h.dma_start(out=fv(x0s(tt), D),
                                    in_=x_in[tt * 128:(tt + 1) * 128, :]) if h else None
                    tick(h, d, "dxa" if tt % 2 == 0 else "dxb", 16, f"x0_{tt}")

            elif e == "vector":
                for s in range(-2, NCH_RUN):
                    if s + 2 < NCH_RUN:
                        fe_dve1(s + 2)
                        fe_dve2(s + 2)
                    if s < 0:
                        continue
                    wait(h, f"prebf_{s}")
                    if s >= 1:
                        wait(h, f"spg_{s - 1}")
                    i = h.tensor_copy(fv(P1["B1"], W), bank(0, 4)) if h else None
                    tick(h, i, "dv", 1, f"pref_{s}")
                    c = s
                    wait(h, f"rt_{c}")
                    last = None
                    for k in range(NCHT):
                        if h:
                            last = h.tensor_scalar(
                                fv(P1["B2"] + k * CH, CH), fv(P1["B2"] + k * CH, CH),
                                fv(OF_LAC + k, 1), None, ALU.mult)
                    fence(h, last, "dv")
                    sch.ms[f"lart_{c}"] = ("dv", sch.cnt["dv"])
                    i = h.tensor_reduce(fv(OF_TMPC, 8), f3(P1["B2"]), AX.X, ALU.add) if h else None
                    fence(h, i, "dv")
                    i = h.tensor_tensor(fv(OF_RTS, 8), fv(OF_RTS, 8),
                                        fv(OF_TMPC, 8), ALU.add) if h else None
                    tick(h, i, "dv", 1, f"lasum_{c}")
                    wait(h, f"it_{c}")
                    wait(h, f"sq1ma_{c}")
                    i = h.tensor_tensor(fv(P1["B1"], W), fv(P1["B3"], W),
                                        fv(P1["B1"], W), ALU.mult) if h else None
                    fence(h, i, "dv")
                    i = h.tensor_tensor(fv(P1["B1"], W), fv(P1["B1"], W),
                                        fv(P1["B5"], W), ALU.mult) if h else None
                    fence(h, i, "dv")
                    sch.ms[f"g_{c}"] = ("dv", sch.cnt["dv"])
                    last = None
                    for k in range(NCHT):
                        init = 0.0 if c == 0 else fv(OF_HCOL + k, 1)
                        if h:
                            last = h.tensor_tensor_scan(
                                fv(P1["B2"] + k * CH, CH), fv(P1["B4"] + k * CH, CH),
                                fv(P1["B1"] + k * CH, CH), init, ALU.mult, ALU.add)
                    fence(h, last, "dv")
                    sch.ms[f"uscan_{c}"] = ("dv", sch.cnt["dv"])
                    i = h.tensor_copy(fv(OF_HCOL, 8), f3(P1["B2"])[:, :, CH - 1]) if h else None
                    tick(h, i, "dv", 1, f"ucols_{c}")

            elif e == "scalar":
                for s in range(-2, NCH_RUN):
                    if s + 2 < NCH_RUN:
                        fe_act1(s + 2)
                    if s >= 0:
                        c = s
                        wait(h, f"wlin_mm_{c}")
                        if c >= 1:
                            wait(h, f"spxg_{c - 1}")
                        i = h.activation(bv(P1["xgelbf"], W), bank(0, 4), AF.Gelu) if h else None
                        fence(h, i, "ac")
                        sch.ms[f"xgel_{c}"] = ("ac", sch.cnt["ac"])
                        i = h.activation(bv(P1["linbf"], W), bank(0, 4), AF.Copy) if h else None
                        tick(h, i, "ac", 1, f"lin_{c}")
                        d = h.dma_start(out=scr_xg[:, c * W:(c + 1) * W],
                                        in_=bv(P1["xgelbf"], W)) if h else None
                        tick(h, d, "dxg", 16, f"spxg_{c}")
                    if s + 2 < NCH_RUN:
                        fe_act2(s + 2)
                    if s >= 0:
                        c = s
                        wait(h, f"wconv_mm_{c}")
                        i = h.activation(bv(P1["prebf"], W), bank(0, 4), AF.Copy) if h else None
                        tick(h, i, "ac", 1, f"prebf_{c}")
                    if s + 2 < NCH_RUN:
                        fe_act3(s + 2)
                    if s < 0:
                        continue
                    c = s
                    wait(h, f"wa_mm_{c}")
                    if c >= 1:
                        wait(h, f"ucols_{c - 1}")
                    i = h.activation(fv(P1["B2"], W), bank(0, 4), AF.Sigmoid) if h else None
                    tick(h, i, "ac", 1, f"rt_{c}")
                    wait(h, f"wx_mm_{c}")
                    if c >= 1:
                        wait(h, f"g_{c - 1}")
                    i = h.activation(fv(P1["B3"], W), bank(0, 4), AF.Sigmoid) if h else None
                    tick(h, i, "ac", 1, f"it_{c}")
                    wait(h, f"lart_{c}")
                    if c >= 1:
                        wait(h, f"spa_{c - 1}")
                        wait(h, f"uscan_{c - 1}")
                    i = h.activation(fv(P1["B4"], W), fv(P1["B2"], W), AF.Exp) if h else None
                    fence(h, i, "ac")
                    sch.ms[f"a_{c}"] = ("ac", sch.cnt["ac"])
                    wait(h, f"lasum_{c}")
                    i = h.activation(fv(P1["B2"], W), fv(P1["B4"], W), AF.Square) if h else None
                    fence(h, i, "ac")
                    sch.ms[f"asq_{c}"] = ("ac", sch.cnt["ac"])
                    i = h.activation(fv(P1["B5"], W), fv(P1["B2"], W), AF.Sqrt,
                                     scale=np.float32(-1.0), bias=np.float32(1.0)) if h else None
                    tick(h, i, "ac", 1, f"sq1ma_{c}")

            elif e == "tensor":
                for s in range(-2, NCH_RUN):
                    if s >= 0:
                        c = s
                        wait(h, f"xne_{2 * c + 1}")
                        if c == 0:
                            wait(h, "fold_lin")
                        else:
                            wait(h, f"it_{c - 1}")
                        last = full_mm(h, OF_T[0], xnfm(c), 0)
                        last = bias_mm(h, 0, 0, last)
                        tick(h, last, "pe", 1, f"wlin_mm_{c}")
                        wait(h, f"lin_{c}")
                        wait(h, f"xgel_{c}")
                        last = full_mm(h, OF_T[1], P1["linbf"], 0)
                        last = bias_mm(h, 1, 0, last)
                        tick(h, last, "pe", 1, f"wconv_mm_{c}")
                        wait(h, f"prebf_{c}")
                        wait(h, f"pref_{c}")
                        last = full_mm(h, OF_T[2], P1["prebf"], 0)
                        last = bias_mm(h, 2, 0, last)
                        tick(h, last, "pe", 1, f"wa_mm_{c}")
                    if s + 2 < NCH_RUN:
                        cc = s + 2
                        for tl in range(2):
                            tt = 2 * cc + tl
                            wait(h, f"apply_{cc}")
                            if tt >= 2:
                                wait(h, f"r0e_{tt - 2}")
                                wait(h, f"xne_{tt - 2}")
                            if cc >= 2 and tl == 0:
                                wait(h, f"wlin_mm_{cc - 2}")
                            rb = 4096 if tt % 2 == 0 else 6144
                            xb = 5120 if tt % 2 == 0 else 7168
                            last = None
                            for k in range(NCHT):
                                if h:
                                    h.transpose(PSb[:, rb + k * 128:rb + k * 128 + 128],
                                                bv(r0ts(tt) + k * 64, 128),
                                                bv(OF_IDB, 128))
                                    last = h.transpose(
                                        PSb[:, xb + k * 128:xb + k * 128 + 128],
                                        bv(xns(tt) + k * 64, 128),
                                        bv(OF_IDB, 128))
                            tick(h, last, "pe", 1, f"T_{tt}")
                    if s >= 0:
                        c = s
                        wait(h, f"rt_{c}")
                        last = full_mm(h, OF_T[3], P1["prebf"], 0)
                        last = bias_mm(h, 3, 0, last)
                        tick(h, last, "pe", 1, f"wx_mm_{c}")

            elif e == "gpsimd":
                for c in range(NCH_RUN):
                    wait(h, f"uscan_{c}")
                    d = h.dma_start(out=scr_a[:, c * W:(c + 1) * W],
                                    in_=fv(P1["B4"], W)) if h else None
                    tick(h, d, "dsa", 16, f"spa_{c}")
                    wait(h, f"g_{c}")
                    d = h.dma_start(out=scr_g[:, c * W:(c + 1) * W],
                                    in_=fv(P1["B1"], W)) if h else None
                    tick(h, d, "dsg", 16, f"spg_{c}")

        NCH_RUN = NCH if p01n is None else p01n
        if upto >= 1:
            emit_phase(ph01)

        def ph2(e, h):
            if e == "scalar":
                wait(h, "lasum_7")
                i = h.activation(fv(OF_AUC, 8), fv(OF_RTS, 8), AF.Exp) if h else None
                tick(h, i, "ac", 1, "Acols")
                wait(h, "Ct")
                i = h.activation(fv(OF_CCS, 128, 0, 16),
                                 PSf[0:16, 3072:3072 + 128], AF.Copy) if h else None
                tick(h, i, "ac", 1, "ccs_done")
                wait(h, "Gt")
                i = h.activation(fv(OF_G, 64), PSf[:, 3072:3072 + 64],
                                 AF.Copy) if h else None
                tick(h, i, "ac", 1, "G_done")
            elif e == "vector":
                wait(h, "ucols_7")
                i = h.tensor_copy(fv(OF_AUC + 8, 8), fv(OF_HCOL, 8)) if h else None
                tick(h, i, "dv", 1, "Ucols")
                wait(h, "G_done")
                wait(h, "setup_done")
                i = h.memset(fv(OF_HIN, 8), 0.0) if h else None
                tick(h, i, "dv", 1)
                for j in range(4):
                    gA = fv(OF_G + j * 16, 8)
                    gU = fv(OF_G + j * 16 + 8, 8)
                    i = h.tensor_scalar(fv(OF_TMPC, 8), gA, fv(OF_SEL + j, 1),
                                        fv(OF_ISEL + j, 1), ALU.mult, ALU.add) if h else None
                    fence(h, i, "dv")
                    i = h.tensor_tensor(fv(OF_HIN, 8), fv(OF_TMPC, 8),
                                        fv(OF_HIN, 8), ALU.mult) if h else None
                    fence(h, i, "dv")
                    i = h.tensor_scalar(fv(OF_TMPC, 8), gU, fv(OF_SEL + j, 1),
                                        None, ALU.mult) if h else None
                    fence(h, i, "dv")
                    i = h.tensor_tensor(fv(OF_HIN, 8), fv(OF_HIN, 8),
                                        fv(OF_TMPC, 8), ALU.add) if h else None
                    fence(h, i, "dv")
                tick(h, None, "dv", 0, "hin_done")
                sch.ms["hin_done"] = ("dv", sch.cnt["dv"])
            elif e == "tensor":
                wait(h, "Acols")
                wait(h, "Ucols")
                wait(h, "xne_15")
                i = h.transpose(PSf[0:16, 3072:3072 + 128],
                                fv(OF_AUC, 16), fv(OF_IDF, 128)) if h else None
                tick(h, i, "pe", 1, "Ct")
                wait(h, "ccl_done")
                wait(h, "ccs_done")
                i = h.transpose(PSf[:, 3072:3072 + 64],
                                fv(OF_CCL, 128, 0, 64),
                                fv(OF_IDF, 64, 0, 64)) if h else None
                tick(h, i, "pe", 1, "Gt")
            elif e == "gpsimd":
                wait(h, "ccs_done")
                d = h.dma_start(out=cc_in[:, :], in_=fv(OF_CCS, 128, 0, 16)) if h else None
                v = tick(h, d, "dcc", 16)
                if h:
                    h.wait_ge(SEM["dcc"], v)
                i = h.collective_compute(
                    "AllGather", ALU.bypass,
                    replica_groups=[[0, 1, 2, 3], [4, 5, 6, 7]],
                    ins=[cc_in.opt()],
                    outs=[cc_out.opt()]) if h else None
                v = tick(h, i, "cc", 1)
                if h:
                    h.wait_ge(S_cc, v)
                d = h.dma_start(out=fv(OF_CCL, 128, 0, 64), in_=cc_out[:, :]) if h else None
                tick(h, d, "dcc", 16, "ccl_done")
                wait(h, "T_15")
                wait(h, "wx_mm_7")
                d = h.dma_start(out=bv(OF_T[4], 8192), in_=wpk["lin2"]) if h else None
                tick(h, d, "dw2", 16)
                d = h.dma_start(out=bv(OF_T[5], 8192), in_=wpk["w1"]) if h else None
                tick(h, d, "dw2", 16)
                d = h.dma_start(out=bv(OF_T[6], 8192), in_=wpk["w2"]) if h else None
                tick(h, d, "dw2", 16)
                d = h.dma_start(out=bv(OF_T[7], 8192), in_=wpk["w3"]) if h else None
                tick(h, d, "dw2", 16)
                d = h.dma_start(out=bv(OF_BIAS, 4096, 0, 1), in_=bias2) if h else None
                tick(h, d, "dw2", 16, "w2b")

        if upto >= 2:
            emit_phase(ph2, engines=("scalar", "vector", "tensor", "gpsimd"))

        def rg(c):
            return P3["rga"] if c % 2 == 0 else P3["rgb"]

        def xg3(c):
            return P3["xga"] if c % 2 == 0 else P3["xgb"]

        def mx3(c):
            return P3["mixa"] if c % 2 == 0 else P3["mixb"]

        def ph3ab(e, h):
            if e == "sync":
                for c in range(NCH):
                    base = rg(c)
                    wait(h, "spa_7")
                    wait(h, "spg_7")
                    wait(h, "spr0_7")
                    wait(h, "spxg_7")
                    if c >= 2:
                        wait(h, f"r1stt_{c - 2}")
                        wait(h, f"mix_{c - 2}")
                    d = h.dma_start(out=fv(base, W),
                                    in_=scr_a[:, c * W:(c + 1) * W]) if h else None
                    tick(h, d, "dr3a" if c % 2 == 0 else "dr3b", 16)
                    d = h.dma_start(out=fv(base + W, W),
                                    in_=scr_g[:, c * W:(c + 1) * W]) if h else None
                    tick(h, d, "dr3a" if c % 2 == 0 else "dr3b", 16)
                    d = h.dma_start(out=bv(base + 2 * W, W),
                                    in_=scr_r0[:, c * W:(c + 1) * W]) if h else None
                    tick(h, d, "dr3a" if c % 2 == 0 else "dr3b", 16)
                    d = h.dma_start(out=bv(xg3(c), W),
                                    in_=scr_xg[:, c * W:(c + 1) * W]) if h else None
                    tick(h, d, "dr3a" if c % 2 == 0 else "dr3b", 16, f"ld3_{c}")

            elif e == "vector":
                def r1b(c):
                    return P3["r1"] if c % 2 == 0 else P3["r1b"]

                def dv_hmix(cc):
                    wait(h, f"ld3_{cc}")
                    if cc == 0:
                        wait(h, "hin_done")
                    last = None
                    for k in range(NCHT):
                        init = fv(OF_HIN + k, 1) if cc == 0 else fv(OF_HCOL + k, 1)
                        if h:
                            last = h.tensor_tensor_scan(
                                fv(P3["h"] + k * CH, CH), fv(rg(cc) + k * CH, CH),
                                fv(rg(cc) + W + k * CH, CH), init, ALU.mult, ALU.add)
                    fence(h, last, "dv")
                    sch.ms[f"hscan_{cc}"] = ("dv", sch.cnt["dv"])
                    i = h.tensor_copy(fv(OF_HCOL, 8), f3(P3["h"])[:, :, CH - 1]) if h else None
                    fence(h, i, "dv")
                    sch.ms[f"hcols_{cc}"] = ("dv", sch.cnt["dv"])
                    if cc >= 2:
                        wait(h, f"stats2_mm_{cc - 2}")
                    i = h.tensor_tensor(bv(mx3(cc), W), bv(xg3(cc), W),
                                        fv(P3["h"], W), ALU.mult) if h else None
                    tick(h, i, "dv", 1, f"mix_{cc}")

                def dv_tail(cc):
                    wait(h, f"gelu_{cc}")
                    wait(h, f"w2_mm_{cc}")
                    i = h.tensor_tensor(bv(P3["xmbf"], W), fv(P3["sig"], W),
                                        bank(0, 4), ALU.mult) if h else None
                    tick(h, i, "dv", 1, f"comb_{cc}")
                    wait(h, f"w3_mm_{cc}")
                    if cc >= 1:
                        wait(h, f"spx1_{cc - 1}")
                    i = h.tensor_tensor(fv(P3["x1p"], W), bank(0, 4),
                                        fv(r1b(cc), W), ALU.add) if h else None
                    tick(h, i, "dv", 1, f"x1p_{cc}")

                dv_hmix(0)
                for c in range(NCH):
                    wait(h, f"wlin2_mm_{c}")
                    i = h.scalar_tensor_tensor(fv(r1b(c), W), PSf[:, 2048:4096],
                                               1.0, bv(rg(c) + 2 * W, W), ALU.mult,
                                               ALU.add) if h else None
                    tick(h, i, "dv", 1, f"r1stt_{c}")
                    if c + 1 < NCH:
                        dv_hmix(c + 1)
                    if c >= 1:
                        dv_tail(c - 1)
                    wait(h, f"sveq2_{c}")
                    i = h.tensor_scalar(fv(P3["s2t"], CH), fv(P3["s2t"], CH),
                                        1e-12 / SQD, None, ALU.max) if h else None
                    fence(h, i, "dv")
                    i = h.reciprocal(fv(P3["s2t"], CH), fv(P3["s2t"], CH)) if h else None
                    fence(h, i, "dv")
                    sch.ms[f"s2row_{c}"] = ("dv", sch.cnt["dv"])
                    last = None
                    for k in range(NCHT):
                        if h:
                            last = h.scalar_tensor_tensor(
                                bv(P3["xmbf"] + k * 128, 256),
                                fv(r1b(c) + k * CH, CH), fv(OF_G2 + k, 1),
                                fv(P3["s2t"], CH), ALU.mult, ALU.mult)
                    tick(h, last, "dv", 1, f"xm_{c}")
                dv_tail(NCH - 1)

            elif e == "tensor":
                def emit_wlin2(c2):
                    wait(h, f"mix_{c2}")
                    if c2 == 0:
                        wait(h, "w2b")
                    if c2 >= 1:
                        wait(h, f"r1stt_{c2 - 1}")
                    last = full_mm(h, OF_T[4], mx3(c2), 4)
                    last = bias_mm(h, 0, 4, last)
                    tick(h, last, "pe", 1, f"wlin2_mm_{c2}")

                emit_wlin2(0)
                for c in range(1, NCH + 1):
                    cc = c - 1
                    wait(h, f"sq2_{cc}")
                    if cc >= 1:
                        wait(h, f"x1p_{cc - 1}")
                        wait(h, f"sveq2_{cc - 1}")
                    last = None
                    for k in range(NCHT):
                        if h:
                            last = h.matmul(
                                PSf[:, 0:256],
                                bv(OF_ONEB, 128),
                                bv(mx3(cc) + k * 128, 256),
                                start=(k == 0), stop=(k == 7))
                    tick(h, last, "pe", 1, f"stats2_mm_{cc}")
                    if cc + 1 < NCH:
                        emit_wlin2(cc + 1)
                    wait(h, f"xm_{cc}")
                    last = full_mm(h, OF_T[5], P3["xmbf"], 0)
                    last = bias_mm(h, 1, 0, last)
                    tick(h, last, "pe", 1, f"w1_mm_{cc}")
                    wait(h, f"sig_{cc}")
                    last = full_mm(h, OF_T[6], P3["xmbf"], 0)
                    last = bias_mm(h, 2, 0, last)
                    tick(h, last, "pe", 1, f"w2_mm_{cc}")
                    wait(h, f"comb_{cc}")
                    last = full_mm(h, OF_T[7], P3["xmbf"], 0)
                    last = bias_mm(h, 3, 0, last)
                    tick(h, last, "pe", 1, f"w3_mm_{cc}")

            elif e == "scalar":
                for c in range(NCH):
                    wait(h, f"r1stt_{c}")
                    if c >= 1:
                        wait(h, f"stats2_mm_{c - 1}")
                    i = h.activation(bv(mx3(c), W),
                                     fv(P3["r1"] if c % 2 == 0 else P3["r1b"], W),
                                     AF.Square) if h else None
                    tick(h, i, "ac", 1, f"sq2_{c}")
                    wait(h, f"stats2_mm_{c}")
                    if c >= 1:
                        wait(h, f"xm_{c - 1}")
                    i = h.activation(fv(P3["s2t"], CH), PSf[:, 0:256],
                                     AF.Sqrt, scale=np.float32(1.0 / D)) if h else None
                    tick(h, i, "ac", 1, f"sveq2_{c}")
                    wait(h, f"w1_mm_{c}")
                    if c >= 1:
                        wait(h, f"comb_{c - 1}")
                    i = h.activation(fv(P3["sig"], W), bank(0, 4), AF.Sigmoid) if h else None
                    fence(h, i, "ac")
                    sch.ms[f"sig_{c}"] = ("ac", sch.cnt["ac"])
                    i = h.activation(fv(P3["sig"], W), fv(P3["sig"], W),
                                     AF.Gelu) if h else None
                    tick(h, i, "ac", 1, f"gelu_{c}")
                    wait(h, f"x1p_{c}")
                    d = h.dma_start(out=scr_x1p[:, c * W:(c + 1) * W],
                                    in_=fv(P3["x1p"], W)) if h else None
                    tick(h, d, "dx1", 16, f"spx1_{c}")

            elif e == "gpsimd":
                wait(h, "x1p_7")
                wait(h, "spx1_7")
                wait(h, "w3_mm_7")
                for ei in range(8):
                    d = h.dma_start(out=bv(OF_T[ei], 8192),
                                    in_=wpk[f"e{ei}"]) if h else None
                    tick(h, d, f"de{ei}", 16, f"ld_e{ei}")

        if upto >= 3:
            emit_phase(ph3ab)

        def x1buf(c):
            return PC["x1A"] if c % 2 == 0 else PC["x1B"]

        def x0cs(tt):
            return PC["x0a"] if tt % 2 == 0 else PC["x0b"]

        def xgs(ei):
            return PC["xg0"] if ei % 2 == 0 else PC["xg1"]

        def gmtb(c):
            return OF_GMT if c % 2 == 0 else PC["gmt2"]

        def ph3c(e, h):
            B6 = 3072
            B7 = 3584

            if e == "sync":
                def ldx1(cc):
                    if cc <= 1:
                        wait(h, "spx1_7")
                    else:
                        wait(h, f"x2T_{cc - 2}_1")
                    d = h.dma_start(out=fv(x1buf(cc), W),
                                    in_=scr_x1p[:, cc * W:(cc + 1) * W]) if h else None
                    tick(h, d, "dpa" if cc % 2 == 0 else "dpb", 16, f"ldx1_{cc}")

                def ldx0(tt):
                    d = h.dma_start(out=fv(x0cs(tt), D),
                                    in_=x_in[tt * 128:(tt + 1) * 128, :]) if h else None
                    tick(h, d, "dxa" if tt % 2 == 0 else "dxb", 16, f"ldx0_{tt}")

                def st(tt):
                    wait(h, f"ev_{tt}")
                    d = h.dma_start(out=out[tt * 128:(tt + 1) * 128, :],
                                    in_=fv(PC["y"], D)) if h else None
                    tick(h, d, "dsta" if tt % 2 == 0 else "dstb", 16, f"st_{tt}")

                ldx1(0)
                ldx0(0)
                ldx0(1)
                wait(h, "x0T_0_0")
                ldx1(1)
                ldx0(2)
                wait(h, "x0T_0_1")
                ldx0(3)
                for c in range(1, NCH):
                    st(2 * c - 2)
                    st(2 * c - 1)
                    if c + 1 < NCH:
                        ldx1(c + 1)
                    if 2 * c + 2 < 2 * NCH:
                        wait(h, f"x0T_{c}_0")
                        ldx0(2 * c + 2)
                        wait(h, f"x0T_{c}_1")
                        ldx0(2 * c + 3)
                st(2 * NCH - 2)
                st(2 * NCH - 1)
                for msn in (f"st_{2 * NCH - 2}", f"st_{2 * NCH - 1}"):
                    sem, v = sch.ms[msn]
                    if h:
                        h.wait_ge(SEM[sem], v)

            elif e == "tensor":
                def pe_x0t(cc, tl):
                    tt = 2 * cc + tl
                    wait(h, f"ldx0_{tt}")
                    if tl == 0:
                        if cc == 0:
                            wait(h, "r1stt_7")
                        elif cc >= 2:
                            wait(h, f"ev_{2 * cc - 3}")
                    else:
                        wait(h, f"x1tta_{cc}")
                    last = None
                    for k in range(NCHT):
                        if h:
                            last = h.transpose(
                                PSf[:, B6 + k * 128:B6 + k * 128 + 128],
                                fv(x0cs(tt) + k * 128, 128),
                                fv(OF_IDF, 128))
                    tick(h, last, "pe", 1, f"x0T_{cc}_{tl}")

                def pe_stats3(cc):
                    wait(h, f"x1tt_{cc}")
                    wait(h, f"sq3_{cc}")
                    last = None
                    for k in range(NCHT):
                        if h:
                            last = h.matmul(PSf[0:1, B6:B6 + 256],
                                            fv(OF_ONEF, 1),
                                            fv(x1buf(cc) + k * CH, CH),
                                            start=(k == 0), stop=(k == 7))
                    for k in range(NCHT):
                        if h:
                            last = h.matmul(PSf[0:1, B6 + 256:B6 + 512],
                                            bv(OF_ONEB, 1),
                                            bv(PC["sqbf"] + k * 128, 256),
                                            start=(k == 0), stop=(k == 7))
                    tick(h, last, "pe", 1, f"stats3_mm_{cc}")

                def pe_bc3(cc):
                    wait(h, f"rows3_{cc}")
                    i = None
                    if h:
                        h.matmul(PSf[:, B7:B7 + 256],
                                 fv(OF_ONEF, 128, 0, 1),
                                 fv(OF_ROWS, 256, 0, 1), start=True, stop=True)
                        i = h.matmul(PSf[:, B7 + 256:B7 + 512],
                                     fv(OF_ONEF, 128, 0, 1),
                                     fv(OF_ROWS + 256, 256, 0, 1),
                                     start=True, stop=True)
                    tick(h, i, "pe", 1, f"bc3_mm_{cc}")

                def pe_rtr(cc):
                    wait(h, f"xt_{cc}")
                    last = None
                    for tl in range(2):
                        for k in range(NCHT):
                            if h:
                                last = h.matmul(
                                    PSf[:, B6 + tl * 8:B6 + tl * 8 + 8],
                                    fv(PC["xt"] + k * CH + tl * 128, 128),
                                    fv(OF_RW + k * 8, 8),
                                    start=(k == 0), stop=False)
                        if h:
                            last = h.matmul(
                                PSf[:, B6 + tl * 8:B6 + tl * 8 + 8],
                                fv(OF_ONEF, 128, 0, 1), fv(OF_RB, 8, 0, 1),
                                start=False, stop=True)
                    tick(h, last, "pe", 1, f"rtr_{cc}")

                def pe_gmt(cc, tl):
                    wait(h, f"gm_{cc}_{tl}")
                    i = h.transpose(
                        PSf[0:8, B6 + 256 + tl * 128:B6 + 256 + tl * 128 + 128],
                        fv(OF_GM + tl * 8, 8), fv(OF_IDF, 128)) if h else None
                    tick(h, i, "pe", 1, f"gmt_{cc}_{tl}")

                def pe_gmbc(cc, ei):
                    boff = 2048 if ei % 2 == 0 else 2560
                    wait(h, f"gmtbf_{cc}")
                    if ei >= 2:
                        wait(h, f"xg_{cc}_{ei - 2}")
                    elif cc >= 1:
                        wait(h, f"xg_{cc - 1}_{ei + 6}")
                    i = h.matmul(PSf[:, boff:boff + CH],
                                 bv(OF_OH + ei * 64, 128, 0, 8),
                                 bv(gmtb(cc), CH, 0, 8),
                                 start=True, stop=True) if h else None
                    tick(h, i, "pe", 1, f"gmbc_{cc}_{ei}")

                def pe_exp(cc, ei):
                    wait(h, f"xg_{cc}_{ei}")
                    wait(h, f"ld_e{ei}")
                    if ei == 0:
                        if cc >= 1:
                            wait(h, f"x2_{cc - 1}")
                        else:
                            wait(h, "x1p_7")
                    last = None
                    for m in range(8):
                        for k in range(8):
                            if h:
                                last = h.matmul(
                                    PSf[:, m * CH:m * CH + CH],
                                    bv(OF_T[ei] + (k * 8 + m) * 64, 128),
                                    b3(xgs(ei))[:, k, :],
                                    start=(ei == 0 and k == 0 and m % 2 == 0),
                                    stop=False)
                    tick(h, last, "pe", 1, f"exp_{cc}_{ei}")

                def pe_expb(cc):
                    last = None
                    for m in range(8):
                        if h:
                            last = h.matmul(PSf[:, m * CH:m * CH + CH],
                                            bv(OF_EXB + m * 64, 128, 0, 8),
                                            bv(gmtb(cc), CH, 0, 8),
                                            start=False, stop=(m % 2 == 1))
                    tick(h, last, "pe", 1, f"expb_{cc}")

                def pe_x2t(cc, tl):
                    if tl == 0:
                        wait(h, f"x2_{cc}")
                        if cc == NCH - 1:
                            wait(h, f"ev_{2 * cc - 1}")
                    else:
                        wait(h, f"ev_{2 * cc}")
                    last = None
                    for k in range(NCHT):
                        if h:
                            last = h.transpose(
                                PSf[:, B6 + k * 128:B6 + k * 128 + 128],
                                fv(x1buf(cc) + k * CH + tl * 128, 128),
                                fv(OF_IDF, 128))
                    tick(h, last, "pe", 1, f"x2T_{cc}_{tl}")

                pe_x0t(0, 0)
                pe_x0t(0, 1)
                pe_stats3(0)
                pe_bc3(0)
                pe_rtr(0)
                pe_gmt(0, 0)
                pe_gmt(0, 1)
                pe_gmbc(0, 0)
                pe_gmbc(0, 1)
                for c in range(NCH):
                    fe = c + 1 if c + 1 < NCH else None
                    pe_exp(c, 0)
                    if c >= 1:
                        pe_x2t(c - 1, 0)
                    pe_gmbc(c, 2)
                    pe_exp(c, 1)
                    if c >= 1:
                        pe_x2t(c - 1, 1)
                    pe_gmbc(c, 3)
                    pe_exp(c, 2)
                    if fe is not None:
                        pe_x0t(fe, 0)
                    pe_gmbc(c, 4)
                    pe_exp(c, 3)
                    if fe is not None:
                        pe_x0t(fe, 1)
                    pe_gmbc(c, 5)
                    pe_exp(c, 4)
                    if fe is not None:
                        pe_stats3(fe)
                    pe_gmbc(c, 6)
                    pe_exp(c, 5)
                    if fe is not None:
                        pe_bc3(fe)
                    pe_gmbc(c, 7)
                    pe_exp(c, 6)
                    if fe is not None:
                        pe_rtr(fe)
                        pe_gmt(fe, 0)
                        pe_gmt(fe, 1)
                        pe_gmbc(fe, 0)
                        pe_gmbc(fe, 1)
                    pe_exp(c, 7)
                    pe_expb(c)
                pe_x2t(NCH - 1, 0)
                pe_x2t(NCH - 1, 1)

            elif e == "vector":
                def dv_x1tt(cc, tl):
                    if tl == 0:
                        wait(h, f"ldx1_{cc}")
                    wait(h, f"x0T_{cc}_{tl}")
                    dst = f3(x1buf(cc))[:, :, tl * 128:tl * 128 + 128]
                    i = h.tensor_tensor(
                        dst, dst,
                        PSf[:, B6:B6 + 1024].rearrange("p (a b) -> p a b", a=8),
                        ALU.add) if h else None
                    tick(h, i, "dv", 1, f"x1tta_{cc}" if tl == 0 else f"x1tt_{cc}")

                def dv_var3(cc):
                    wait(h, f"stats3_mm_{cc}")
                    i = h.tensor_scalar(fv(OF_ROWS + 512, 256, 0, 1),
                                        PSf[0:1, B6:B6 + 256],
                                        1.0 / D, None, ALU.mult) if h else None
                    fence(h, i, "dv")
                    i = h.tensor_tensor(fv(OF_ROWS + 768, 256, 0, 1),
                                        fv(OF_ROWS + 512, 256, 0, 1),
                                        fv(OF_ROWS + 512, 256, 0, 1), ALU.mult) if h else None
                    fence(h, i, "dv")
                    i = h.scalar_tensor_tensor(
                        fv(OF_ROWS + 768, 256, 0, 1),
                        PSf[0:1, B6 + 256:B6 + 512], 1.0 / D,
                        fv(OF_ROWS + 768, 256, 0, 1), ALU.mult, ALU.subtract) if h else None
                    tick(h, i, "dv", 1, f"var3_{cc}")

                def dv_rows3(cc):
                    wait(h, f"sve3_{cc}")
                    i = h.reciprocal(fv(OF_ROWS, 256, 0, 1),
                                     fv(OF_ROWS + 1024, 256, 0, 1)) if h else None
                    fence(h, i, "dv")
                    i = h.tensor_tensor(fv(OF_ROWS + 256, 256, 0, 1),
                                        fv(OF_ROWS + 512, 256, 0, 1),
                                        fv(OF_ROWS, 256, 0, 1), ALU.mult) if h else None
                    tick(h, i, "dv", 1, f"rows3_{cc}")

                def dv_xt(cc):
                    wait(h, f"bc3_mm_{cc}")
                    i = h.tensor_tensor(f3(PC["xt"]), f3(x1buf(cc)),
                                        PSf[:, B7:B7 + 256].unsqueeze(1)
                                        .broadcast_to([128, 8, 256]), ALU.mult) if h else None
                    fence(h, i, "dv")
                    i = h.tensor_tensor(f3(PC["xt"]), f3(PC["xt"]),
                                        PSf[:, B7 + 256:B7 + 512]
                                        .unsqueeze(1).broadcast_to([128, 8, 256]),
                                        ALU.subtract) if h else None
                    fence(h, i, "dv")
                    sch.ms[f"xt_{cc}"] = ("dv", sch.cnt["dv"])

                def dv_softmax(cc):
                    for tl in range(2):
                        wait(h, f"rtr_{cc}")
                        lg = PSf[:, B6 + tl * 8:B6 + tl * 8 + 8]
                        i = h.reduce_max(fv(OF_SCOL + tl * 16, 1), lg, AX.X) if h else None
                        fence(h, i, "dv")
                        i = h.tensor_scalar(fv(OF_SCOL + tl * 16 + 1, 1),
                                            fv(OF_SCOL + tl * 16, 1), -1.0,
                                            None, ALU.mult) if h else None
                        tick(h, i, "dv", 1, f"lmax_{cc}_{tl}")
                        wait(h, f"sme_{cc}_{tl}")
                        i = h.reciprocal(fv(OF_SCOL + tl * 16 + 3, 1),
                                         fv(OF_SCOL + tl * 16 + 2, 1)) if h else None
                        fence(h, i, "dv")
                        i = h.tensor_scalar(fv(OF_GM + tl * 8, 8), lg,
                                            fv(OF_SCOL + tl * 16, 1), None,
                                            ALU.is_ge) if h else None
                        fence(h, i, "dv")
                        i = h.tensor_scalar(fv(OF_GM + tl * 8, 8),
                                            fv(OF_GM + tl * 8, 8),
                                            fv(OF_SCOL + tl * 16 + 3, 1),
                                            None, ALU.mult) if h else None
                        tick(h, i, "dv", 1, f"gm_{cc}_{tl}")

                def dv_xtbf(cc):
                    i = h.tensor_copy(bv(PC["xtbf"], W), fv(PC["xt"], W)) if h else None
                    tick(h, i, "dv", 1, f"xtbf_{cc}")

                def dv_xg(cc, ei):
                    boff = 2048 if ei % 2 == 0 else 2560
                    wait(h, f"gmbc_{cc}_{ei}")
                    if ei >= 2:
                        wait(h, f"exp_{cc}_{ei - 2}")
                    elif cc >= 1:
                        wait(h, f"exp_{cc - 1}_{ei + 6}")
                    i = h.tensor_tensor(
                        b3(xgs(ei)), b3(PC["xtbf"]),
                        PSf[:, boff:boff + CH].unsqueeze(1)
                        .broadcast_to([128, 8, 256]), ALU.mult) if h else None
                    tick(h, i, "dv", 1, f"xg_{cc}_{ei}")

                def dv_x2(cc):
                    wait(h, f"expb_{cc}")
                    i = h.tensor_tensor(fv(x1buf(cc), W), bank(0, 4),
                                        fv(x1buf(cc), W), ALU.add) if h else None
                    tick(h, i, "dv", 1, f"x2_{cc}")

                dv_x1tt(0, 0)
                dv_x1tt(0, 1)
                dv_var3(0)
                dv_rows3(0)
                dv_xt(0)
                dv_softmax(0)
                dv_xtbf(0)
                dv_xg(0, 0)
                dv_xg(0, 1)
                for c in range(NCH):
                    fe = c + 1 if c + 1 < NCH else None
                    dv_xg(c, 2)
                    dv_xg(c, 3)
                    if fe is not None:
                        dv_x1tt(fe, 0)
                    dv_xg(c, 4)
                    if fe is not None:
                        dv_x1tt(fe, 1)
                    dv_xg(c, 5)
                    if fe is not None:
                        dv_var3(fe)
                        dv_rows3(fe)
                    dv_xg(c, 6)
                    if fe is not None:
                        dv_xt(fe)
                        dv_softmax(fe)
                    dv_xg(c, 7)
                    if fe is not None:
                        dv_xtbf(fe)
                        dv_xg(fe, 0)
                    dv_x2(c)
                    if fe is not None:
                        dv_xg(fe, 1)

            elif e == "scalar":
                def ev(tt):
                    wait(h, f"x2T_{tt // 2}_{tt % 2}")
                    if tt >= 1:
                        wait(h, f"st_{tt - 1}")
                    i = h.activation(fv(PC["y"], D), PSf[:, B6:B6 + 1024],
                                     AF.Copy) if h else None
                    tick(h, i, "ac", 1, f"ev_{tt}")

                for c in range(NCH):
                    if c >= 2:
                        ev(2 * (c - 2))
                        ev(2 * (c - 2) + 1)
                    wait(h, f"x1tt_{c}")
                    if c >= 1:
                        wait(h, f"stats3_mm_{c - 1}")
                    i = h.activation(bv(PC["sqbf"], W), fv(x1buf(c), W),
                                     AF.Square) if h else None
                    tick(h, i, "ac", 1, f"sq3_{c}")
                    wait(h, f"var3_{c}")
                    i = h.activation(fv(OF_ROWS + 1024, 256, 0, 1),
                                     fv(OF_ROWS + 768, 256, 0, 1), AF.Sqrt,
                                     bias=np.float32(EPS)) if h else None
                    tick(h, i, "ac", 1, f"sve3_{c}")
                    for tl in range(2):
                        wait(h, f"lmax_{c}_{tl}")
                        i = h.activation(fv(OF_EXPS + tl * 8, 8),
                                         PSf[:, B6 + tl * 8:B6 + tl * 8 + 8],
                                         AF.Exp, bias=fv(OF_SCOL + tl * 16 + 1, 1),
                                         accum_out=fv(OF_SCOL + tl * 16 + 2, 1)) if h else None
                        tick(h, i, "ac", 1, f"sme_{c}_{tl}")
                    wait(h, f"gmt_{c}_1")
                    if c >= 2:
                        wait(h, f"expb_{c - 2}")
                    i = h.activation(bv(gmtb(c), 256, 0, 8),
                                     PSf[0:8, B6 + 256:B6 + 512],
                                     AF.Copy) if h else None
                    tick(h, i, "ac", 1, f"gmtbf_{c}")
                for tt in range(4 * NCH - 4, 4 * NCH):
                    pass
                for tt in (2 * NCH - 4, 2 * NCH - 3, 2 * NCH - 2, 2 * NCH - 1):
                    ev(tt)

            elif e == "gpsimd":
                pass

        if upto >= 4:
            emit_phase(ph3c)
        if upto < 4:
            def ph_fin(e, h):
                if e != "sync":
                    return
                fins = {0: ["setup_done"],
                        1: ([f"r0e_{2*NCH_RUN-1}", f"xne_{2*NCH_RUN-1}",
                             f"T_{2*NCH_RUN-1}", f"apply_{NCH_RUN-1}",
                             f"spr0_{NCH_RUN-1}"]
                            + ([f"wlin_mm_{NCH_RUN-1}", f"spxg_{NCH_RUN-1}",
                                f"spr0_{NCH_RUN-1}"] if p01s >= 1 else [])
                            + (([f"wconv_mm_{NCH_RUN-1}"]
                                + ([f"pref_{NCH_RUN-1}"] if "pref" not in _SKIP else [])
                                + ([f"prebf_{NCH_RUN-1}"] if "prebf" not in _SKIP else []))
                               if p01s >= 2 else [])
                            + ([f"wa_mm_{NCH_RUN-1}", f"rt_{NCH_RUN-1}"] if p01s >= 3 else [])
                            + ([f"wx_mm_{NCH_RUN-1}", f"it_{NCH_RUN-1}"] if p01s >= 4 else [])
                            + ([f"spa_{NCH_RUN-1}", f"spg_{NCH_RUN-1}",
                                f"ucols_{NCH_RUN-1}"] if p01s >= 5 else [])),
                        2: ["hin_done", "G_done", "w2b"],
                        3: ["spx1_7", "w3_mm_7", "ld_e7"]}
                for u in range(upto + 1):
                    for msn in fins.get(u, []):
                        wait(h, msn)
                dump_off = getattr(sys.modules[__name__], "_DUMP", None)
                src_ap = fv(OF_IDF, 128) if dump_off is None else fv(dump_off, 128)
                d = h.dma_start(out=out[0:128, 0:128], in_=src_ap) if h else None
                v = tick(h, d, "dsta", 16)
                if h:
                    h.wait_ge(SEM["dsta"], v)
            emit_phase(ph_fin, engines=("sync",))

    return nc


def _pack_T(Wm):
    return np.ascontiguousarray(
        Wm.reshape(8, 128, 8, 128).transpose(3, 2, 0, 1).reshape(128, 8192))


def _pack_N(Wm):
    return np.ascontiguousarray(
        Wm.reshape(8, 128, 8, 128).transpose(1, 0, 2, 3).reshape(128, 8192))


def _prep_inputs(inputs):
    def f(k):
        return np.asarray(inputs[k], dtype=np.float32)
    x = np.ascontiguousarray(f("L0_attn_x").reshape(-1, D))
    com = {}
    com["wpk_lin"] = _pack_T(f("L0_attn_W_lin"))
    com["wpk_conv"] = _pack_T(f("L0_attn_W_conv"))
    com["wpk_wa"] = _pack_T(f("L0_attn_Wa"))
    com["wpk_wx"] = _pack_T(f("L0_attn_Wx"))
    com["wpk_lin2"] = _pack_T(f("L0_attn_W_lin2"))
    com["wpk_w1"] = _pack_T(f("L0_attn_W1"))
    com["wpk_w2"] = _pack_T(f("L0_attn_W2"))
    com["wpk_w3"] = _pack_T(f("L0_attn_W3"))
    ew = f("L0_ffn_expert_w")
    for e in range(8):
        com[f"wpk_e{e}"] = _pack_N(ew[e])
    com["bias1"] = np.concatenate(
        [f("L0_attn_b_lin"), f("L0_attn_b_conv"),
         f("L0_attn_ba"), f("L0_attn_bx")]).reshape(1, 4096)
    com["bias2"] = np.concatenate(
        [f("L0_attn_b_lin2"), f("L0_attn_b1"),
         f("L0_attn_b2"), f("L0_attn_b3")]).reshape(1, 4096)
    com["exb"] = np.ascontiguousarray(f("L0_ffn_expert_b"))
    com["rw"] = np.ascontiguousarray(
        f("L0_ffn_router_w").reshape(8, 128, 8).transpose(1, 0, 2).reshape(128, 64))
    com["rb"] = f("L0_ffn_router_b").reshape(1, 8)
    com["lam"] = np.ascontiguousarray(f("L0_attn_Lam").reshape(8, 128).T)
    com["g1c"] = np.ascontiguousarray(f("L0_attn_g1").reshape(8, 128).T)
    com["g2c"] = np.ascontiguousarray(f("L0_attn_g2").reshape(8, 128).T)
    oh = np.zeros((8, D), dtype=np.float32)
    for e in range(8):
        oh[e, e * 128:(e + 1) * 128] = 1.0
    com["oh"] = oh
    com["ident"] = np.eye(128, dtype=np.float32)
    in_maps = []
    for k in range(NCORES):
        m = dict(com)
        m["x"] = np.ascontiguousarray(x[k * TOK:(k + 1) * TOK, :])
        r = k % 4
        sel = np.zeros((128, 8), dtype=np.float32)
        sel[:, :r] = 1.0
        m["sel"] = sel
        m["isel"] = np.ascontiguousarray(1.0 - sel)
        in_maps.append(m)
    return in_maps


def kernel(**inputs):
    in_maps = _prep_inputs(inputs)
    nc = build(debug=False)
    res = run_bass_kernel_spmd(nc, in_maps, core_ids=list(range(NCORES)))
    outs = [res.results[k]["out"] for k in range(NCORES)]
    full = np.concatenate(outs, axis=0).reshape(2, 8192, 1024)
    return np.ascontiguousarray(full.astype(np.float32))


# version 10
# speedup vs baseline: 1.3517x; 1.0114x over previous
import sys
import numpy as np
import concourse.bass as bass
import concourse.mybir as mybir
from concourse.bass_utils import run_bass_kernel_spmd

F32 = mybir.dt.float32
BF16 = mybir.dt.bfloat16
AF = mybir.ActivationFunctionType
ALU = mybir.AluOpType
AX = mybir.AxisListType

NCORES = 8
_SKIP = set()
TOK = 2048
D = 1024
NCHT = 8
CH = 256
NCH = TOK // CH
W = NCHT * CH
EPS = 1e-5
SQD = 32.0

ARENA = 52224

OF_IDF = 0
OF_IDB = 128
OF_ONEF = 192
OF_ONEB = 320
OF_RW = 448
OF_LAM = 512
OF_G1 = 520
OF_G2 = 528
OF_SEL = 536
OF_ISEL = 544
OF_LAC = 552
OF_RTS = 560
OF_TMPC = 568
OF_AUC = 600
OF_G = 616
OF_HIN = 680
OF_CCS = 688
OF_CCL = 816
OF_EXB = 944
OF_OH = 1456
OF_BIAS = 1968
OF_RB = 4016
OF_SS = 4032
OF_SQ = 4048
OF_MC = 4064
OF_VAR = 4080
OF_SVE = 4096
OF_Q1 = 4112
OF_RS = 4128
OF_C1 = 4144
OF_M2T = 4160
OF_ROWS = 4176
OF_GM = 5456
OF_SCOL = 5472
OF_GMT = 5504
OF_HCOL = 5760
OF_EXPS = 5768
OF_PERS_END = 5784

OF_T = [5792 + i * 4096 for i in range(8)]
OF_SCR = OF_T[7] + 4096
assert OF_SCR == 38560
SCR_END = ARENA


def _alloc(start, end, sizes):
    out = {}
    o = start
    for name, n in sizes:
        out[name] = o
        o += n
    assert o <= end, (o, end)
    return out


P1 = _alloc(OF_SCR, SCR_END, [
    ("B2", W), ("B3", W), ("B4", W), ("B5", W),
    ("r0fm", W // 2), ("xgelbf", W // 2),
])
P1.update(_alloc(OF_T[4], OF_T[6], [
    ("x0a", D), ("x0b", D), ("r0ta", D // 2), ("r0tb", D // 2),
    ("xna", D // 2), ("xnb", D // 2), ("sqscr", D),
    ("xnfa", W // 2), ("xnfb", W // 2),
]))
P1["stage"] = OF_T[6]
P1.update(_alloc(OF_T[6], OF_T[7] + 4096, [
    ("B1", W), ("linbf", W // 2), ("prebf", W // 2),
]))

P3 = _alloc(OF_SCR, SCR_END, [
    ("rga", 3 * W), ("rgb", 3 * W),
])
P3.update(_alloc(OF_T[0], OF_T[4], [
    ("h", W), ("mixa", W // 2), ("mixb", W // 2), ("r1", W), ("xmbf", W // 2),
    ("sig", W), ("x1p", W), ("xga", W // 2), ("xgb", W // 2),
    ("s2t", CH), ("r1b", W),
]))

PC = _alloc(OF_SCR, SCR_END, [
    ("x0a", D), ("x0b", D),
    ("x1A", W), ("x1B", W),
    ("xt", W), ("xtbf", W // 2), ("sqbf", W // 2),
    ("xg0", W // 2), ("xg1", W // 2),
    ("gmt2", 256), ("y", D),
])


class Sched:
    def __init__(self):
        self.cnt = {}
        self.ms = {}

    def inc(self, sem, n, ms=None):
        v = self.cnt[sem] = self.cnt.get(sem, 0) + n
        if ms is not None:
            self.ms[ms] = (sem, v)
        return v


def build(debug=False, upto=4, p01n=None, p01s=5):
    nc = bass.Bass("TRN2", target_bir_lowering=False,
                   detect_race_conditions=False)

    x_in = nc.dram_tensor("x", [TOK, D], F32, kind="ExternalInput").ap()
    wname = ["lin", "conv", "wa", "wx", "lin2", "w1", "w2", "w3"] + \
            [f"e{i}" for i in range(8)]
    wpk = {n: nc.dram_tensor("wpk_" + n, [128, 8192], F32, kind="ExternalInput").ap()
           for n in wname}
    bias1 = nc.dram_tensor("bias1", [1, 4096], F32, kind="ExternalInput").ap()
    bias2 = nc.dram_tensor("bias2", [1, 4096], F32, kind="ExternalInput").ap()
    exb_in = nc.dram_tensor("exb", [8, D], F32, kind="ExternalInput").ap()
    rw_in = nc.dram_tensor("rw", [128, 64], F32, kind="ExternalInput").ap()
    rb_in = nc.dram_tensor("rb", [1, 8], F32, kind="ExternalInput").ap()
    lam_in = nc.dram_tensor("lam", [128, 8], F32, kind="ExternalInput").ap()
    g1_in = nc.dram_tensor("g1c", [128, 8], F32, kind="ExternalInput").ap()
    g2_in = nc.dram_tensor("g2c", [128, 8], F32, kind="ExternalInput").ap()
    sel_in = nc.dram_tensor("sel", [128, 8], F32, kind="ExternalInput").ap()
    isel_in = nc.dram_tensor("isel", [128, 8], F32, kind="ExternalInput").ap()
    oh_in = nc.dram_tensor("oh", [8, D], F32, kind="ExternalInput").ap()
    id_in = nc.dram_tensor("ident", [128, 128], F32, kind="ExternalInput").ap()
    out = nc.dram_tensor("out", [TOK, D], F32, kind="ExternalOutput").ap()

    skind = "ExternalOutput" if debug else "Internal"
    scr_r0 = nc.dram_tensor("scr_r0", [128, NCH * W], BF16, kind=skind).ap()
    scr_a = nc.dram_tensor("scr_a", [128, NCH * W], F32, kind=skind).ap()
    scr_g = nc.dram_tensor("scr_g", [128, NCH * W], F32, kind=skind).ap()
    scr_xg = nc.dram_tensor("scr_xg", [128, NCH * W], BF16, kind=skind).ap()
    scr_x1p = nc.dram_tensor("scr_x1p", [128, NCH * W], F32, kind=skind).ap()
    cc_in = nc.dram_tensor("cc_in", [16, 128], F32).ap()
    cc_out = nc.dram_tensor("cc_out", [64, 128], F32).ap()

    arena_h = nc.alloc_sbuf_tensor("arena", [128, ARENA], F32)
    A = arena_h.ap()
    Ab = A.bitcast(BF16)

    def fv(off, n, p0=0, p1=128):
        return A[p0:p1, off:off + n]

    def bv(off, nbf, p0=0, p1=128):
        return Ab[p0:p1, 2 * off:2 * off + nbf]

    def f3(off):
        return fv(off, W).rearrange("p (a b) -> p a b", a=NCHT)

    def b3(off):
        return bv(off, W).rearrange("p (a b) -> p a b", a=NCHT)

    sch = Sched()

    import contextlib
    SEM_NAMES = (["pe", "ac", "dv", "cc",
                  "dxa", "dxb",
                  "dW", "dw2",
                  "dxg", "dr0", "dx1",
                  "dsa", "dsg",
                  "dr3a", "dr3b",
                  "dpa", "dpb",
                  "dsta", "dstb",
                  "dcc"]
                 + [f"de{i}" for i in range(8)])
    with contextlib.ExitStack() as _es:
        PS = _es.enter_context(nc.psum_tensor("PS", [128, 4096], F32))
        SEM = {n: _es.enter_context(nc.semaphore(n)) for n in SEM_NAMES}
        S_cc = SEM["cc"]
        block = _es.enter_context(nc.Block())
        PSf = PS[:]
        PSb = PSf.bitcast(BF16)

        def bank(b, n=1):
            return PSf[:, b * 512:(b + n) * 512]

        def emit_phase(fn, engines=("sync", "gpsimd", "tensor", "scalar", "vector")):
            snap0 = dict(sch.cnt)
            for e in engines:
                fn(e, None)
            snap1 = dict(sch.cnt)
            emitters = {"sync": block.sync, "gpsimd": block.gpsimd,
                        "tensor": block.tensor, "scalar": block.scalar,
                        "vector": block.vector}
            for e in engines:
                sch.cnt = dict(snap0)
                emitters[e](lambda h, e=e: fn(e, h))
            sch.cnt = dict(snap1)

        def wait(h, ms):
            if h is not None:
                sem, v = sch.ms[ms]
                h.wait_ge(SEM[sem], v)

        def tick(h, inst, sem, n, ms=None):
            if h is not None and inst is not None:
                inst.then_inc(SEM[sem], n)
            return sch.inc(sem, n, ms)

        def fence(h, inst, eng):
            v = sch.inc(eng, 1)
            if h is not None and inst is not None:
                inst.then_inc(SEM[eng], 1)
                h.wait_ge(SEM[eng], v)
            return v

        def full_mm(h, slot_off, rhs_off, b0):
            last = None
            r3 = b3(rhs_off)
            for m in range(8):
                for k in range(8):
                    if h:
                        last = h.matmul(
                            PSf[:, b0 * 512 + m * CH:b0 * 512 + m * CH + CH],
                            bv(slot_off + (k * 8 + m) * 64, 128),
                            r3[:, k, :],
                            start=(k == 0 and m % 2 == 0), stop=False)
            return last

        def bias_mm(h, bslot, b0, last):
            for m in range(8):
                if h:
                    last = h.matmul(
                        PSf[:, b0 * 512 + m * CH:b0 * 512 + m * CH + CH],
                        bv(OF_BIAS + bslot * 512 + m * 64, 128, 0, 1),
                        bv(OF_ONEB, CH, 0, 1),
                        start=False, stop=(m % 2 == 1))
            return last

        def ph_setup(e, h):
            if e == "gpsimd":
                def dm(o, i):
                    return h.dma_start(out=o, in_=i) if h else None
                tick(h, dm(fv(OF_IDF, 128), id_in), "dW", 16)
                tick(h, dm(bv(OF_IDB, 128), id_in), "dW", 16)
                tick(h, dm(fv(OF_RW, 64), rw_in), "dW", 16)
                tick(h, dm(fv(OF_RB, 8, 0, 1), rb_in), "dW", 16)
                tick(h, dm(fv(OF_LAM, 8), lam_in), "dW", 16)
                tick(h, dm(fv(OF_G1, 8), g1_in), "dW", 16)
                tick(h, dm(fv(OF_G2, 8), g2_in), "dW", 16)
                tick(h, dm(fv(OF_SEL, 8), sel_in), "dW", 16)
                tick(h, dm(fv(OF_ISEL, 8), isel_in), "dW", 16)
                tick(h, dm(bv(OF_EXB, 1024, 0, 8), exb_in), "dW", 16)
                tick(h, dm(bv(OF_OH, 1024, 0, 8), oh_in), "dW", 16)
                tick(h, dm(bv(OF_BIAS, 4096, 0, 1), bias1), "dW", 16)
                tick(h, dm(fv(P1["stage"], 4096), wpk["lin"][:, 0:4096]), "dW", 16)
                tick(h, dm(fv(P1["stage"] + 4096, 4096), wpk["lin"][:, 4096:8192]),
                     "dW", 16)
                tick(h, dm(bv(OF_T[1], 8192), wpk["conv"]), "dW", 16)
                tick(h, dm(bv(OF_T[2], 8192), wpk["wa"]), "dW", 16)
                tick(h, dm(bv(OF_T[3], 8192), wpk["wx"]), "dW", 16, "setup_done")
            elif e == "scalar":
                wait(h, "setup_done")
                i = h.activation(fv(OF_TMPC, 8), fv(OF_LAM, 8), AF.Sigmoid) if h else None
                fence(h, i, "ac")
                i = h.activation(fv(OF_LAC, 8), fv(OF_TMPC, 8), AF.Ln) if h else None
                tick(h, i, "ac", 1, "lac_ln")
            elif e == "vector":
                i = None
                if h:
                    h.memset(fv(OF_ONEF, 128), 1.0)
                    h.memset(bv(OF_ONEB, 256), 1.0)
                    i = h.memset(fv(OF_RTS, 8), 0.0)
                tick(h, i, "dv", 1)
                wait(h, "setup_done")
                last = None
                for k in range(8):
                    if h:
                        last = h.tensor_scalar(
                            bv(OF_T[0] + k * 512, 1024),
                            fv(P1["stage"] + k * 1024, 1024),
                            fv(OF_G1 + k, 1), None, ALU.mult)
                tick(h, last, "dv", 1, "fold_lin")
                wait(h, "lac_ln")
                i = h.tensor_scalar(fv(OF_LAC, 8), fv(OF_LAC, 8), 0.125,
                                    None, ALU.mult) if h else None
                tick(h, i, "dv", 1, "lac_done")

        emit_phase(ph_setup, engines=("gpsimd", "scalar", "vector"))

        def x0s(tt):
            return P1["x0a"] if tt % 2 == 0 else P1["x0b"]

        def r0ts(tt):
            return P1["r0ta"] if tt % 2 == 0 else P1["r0tb"]

        def xns(tt):
            return P1["xna"] if tt % 2 == 0 else P1["xnb"]

        def xnfm(c):
            return P1["xnfa"] if c % 2 == 0 else P1["xnfb"]

        def ph01(e, h):
            def fe_act1(cc):
                for tl in range(2):
                    tt = 2 * cc + tl
                    wait(h, f"x0_{tt}")
                    if tt >= 1:
                        wait(h, f"ssq_{tt - 1}")
                    i = h.activation(fv(P1["sqscr"], D), fv(x0s(tt), D),
                                     AF.Square) if h else None
                    tick(h, i, "ac", 1, f"sq_{tt}")

            def fe_act2(cc):
                wait(h, f"var_{cc}")
                i = None
                if h:
                    h.activation(fv(OF_SVE + 2 * cc, 2), fv(OF_VAR + 2 * cc, 2),
                                 AF.Sqrt, bias=np.float32(EPS))
                    i = h.activation(fv(OF_Q1 + 2 * cc, 2), fv(OF_VAR + 2 * cc, 2),
                                     AF.Sqrt, scale=np.float32(D))
                tick(h, i, "ac", 1, f"sveq_{cc}")

            def fe_act3(cc):
                for tl in range(2):
                    tt = 2 * cc + tl
                    wait(h, f"T_{tt}")
                    if cc >= 1 and tl == 0:
                        wait(h, f"spr0_{cc - 1}")
                    rb = 4096 if tt % 2 == 0 else 6144
                    xb = 5120 if tt % 2 == 0 else 7168
                    i = h.activation(
                        bv(P1["r0fm"], W).rearrange("p (a b) -> p a b", a=8)
                        [:, :, tl * 128:tl * 128 + 128],
                        PSb[:, rb:rb + 1024].rearrange("p (a b) -> p a b", a=8),
                        AF.Copy) if h else None
                    tick(h, i, "ac", 1, f"r0e_{tt}")
                    i = h.activation(
                        b3(xnfm(cc))[:, :, tl * 128:tl * 128 + 128],
                        PSb[:, xb:xb + 1024].rearrange("p (a b) -> p a b", a=8),
                        AF.Copy) if h else None
                    if tl == 1:
                        fence(h, i, "ac")
                        sch.ms[f"xne_{tt}"] = ("ac", sch.cnt["ac"])
                    else:
                        tick(h, i, "ac", 1, f"xne_{tt}")
                d = h.dma_start(out=scr_r0[:, cc * W:(cc + 1) * W],
                                in_=bv(P1["r0fm"], W)) if h else None
                tick(h, d, "dr0", 16, f"spr0_{cc}")

            def fe_dve1(cc):
                for tl in range(2):
                    tt = 2 * cc + tl
                    wait(h, f"x0_{tt}")
                    i = h.reduce_sum(fv(OF_SS + tt, 1), fv(x0s(tt), D),
                                     AX.X) if h else None
                    tick(h, i, "dv", 1, f"ssum_{tt}")
                    wait(h, f"sq_{tt}")
                    i = h.reduce_sum(fv(OF_SQ + tt, 1), fv(P1["sqscr"], D),
                                     AX.X) if h else None
                    tick(h, i, "dv", 1, f"ssq_{tt}")
                i = h.tensor_scalar(fv(OF_MC + 2 * cc, 2), fv(OF_SS + 2 * cc, 2),
                                    1.0 / D, None, ALU.mult) if h else None
                fence(h, i, "dv")
                i = h.tensor_tensor(fv(OF_M2T + 2 * cc, 2), fv(OF_MC + 2 * cc, 2),
                                    fv(OF_MC + 2 * cc, 2), ALU.mult) if h else None
                fence(h, i, "dv")
                i = h.scalar_tensor_tensor(
                    fv(OF_VAR + 2 * cc, 2), fv(OF_SQ + 2 * cc, 2), 1.0 / D,
                    fv(OF_M2T + 2 * cc, 2), ALU.mult, ALU.subtract) if h else None
                tick(h, i, "dv", 1, f"var_{cc}")

            def fe_dve2(cc):
                wait(h, f"sveq_{cc}")
                i = h.reciprocal(fv(OF_RS + 2 * cc, 2), fv(OF_SVE + 2 * cc, 2)) if h else None
                fence(h, i, "dv")
                i = h.tensor_tensor(fv(OF_M2T + 2 * cc, 2), fv(OF_Q1 + 2 * cc, 2),
                                    fv(OF_RS + 2 * cc, 2), ALU.mult) if h else None
                fence(h, i, "dv")
                i = h.tensor_scalar(fv(OF_M2T + 2 * cc, 2), fv(OF_M2T + 2 * cc, 2),
                                    1e-12, None, ALU.max) if h else None
                fence(h, i, "dv")
                i = h.reciprocal(fv(OF_M2T + 2 * cc, 2), fv(OF_M2T + 2 * cc, 2)) if h else None
                fence(h, i, "dv")
                i = h.scalar_tensor_tensor(
                    fv(OF_C1 + 2 * cc, 2), fv(OF_M2T + 2 * cc, 2), SQD,
                    fv(OF_RS + 2 * cc, 2), ALU.mult, ALU.mult) if h else None
                fence(h, i, "dv")
                for tl in range(2):
                    tt = 2 * cc + tl
                    if tt >= 2:
                        wait(h, f"T_{tt - 2}")
                    i = None
                    if h:
                        h.tensor_scalar(bv(r0ts(tt), 1024), fv(x0s(tt), D),
                                        fv(OF_MC + tt, 1), fv(OF_RS + tt, 1),
                                        ALU.subtract, ALU.mult)
                        i = h.tensor_scalar(bv(xns(tt), 1024), fv(x0s(tt), D),
                                            fv(OF_MC + tt, 1), fv(OF_C1 + tt, 1),
                                            ALU.subtract, ALU.mult)
                    tick(h, i, "dv", 1, f"apply_{cc}" if tl == 1 else None)

            if e == "sync":
                for tt in range(2 * NCH_RUN):
                    if tt >= 2:
                        wait(h, f"apply_{tt // 2 - 1}")
                        wait(h, f"sq_{tt - 2}")
                    d = h.dma_start(out=fv(x0s(tt), D),
                                    in_=x_in[tt * 128:(tt + 1) * 128, :]) if h else None
                    tick(h, d, "dxa" if tt % 2 == 0 else "dxb", 16, f"x0_{tt}")

            elif e == "vector":
                for s in range(-2, NCH_RUN):
                    if s + 2 < NCH_RUN:
                        fe_dve1(s + 2)
                    if s >= 0:
                        wait(h, f"prebf_{s}")
                        if s >= 1:
                            wait(h, f"spg_{s - 1}")
                        i = h.tensor_copy(fv(P1["B1"], W), bank(0, 4)) if h else None
                        tick(h, i, "dv", 1, f"pref_{s}")
                    if s + 2 < NCH_RUN:
                        fe_dve2(s + 2)
                    if s < 0:
                        continue
                    c = s
                    wait(h, f"rt_{c}")
                    last = None
                    for k in range(NCHT):
                        if h:
                            last = h.tensor_scalar(
                                fv(P1["B2"] + k * CH, CH), fv(P1["B2"] + k * CH, CH),
                                fv(OF_LAC + k, 1), None, ALU.mult)
                    fence(h, last, "dv")
                    sch.ms[f"lart_{c}"] = ("dv", sch.cnt["dv"])
                    i = h.tensor_reduce(fv(OF_TMPC, 8), f3(P1["B2"]), AX.X, ALU.add) if h else None
                    fence(h, i, "dv")
                    i = h.tensor_tensor(fv(OF_RTS, 8), fv(OF_RTS, 8),
                                        fv(OF_TMPC, 8), ALU.add) if h else None
                    tick(h, i, "dv", 1, f"lasum_{c}")
                    wait(h, f"it_{c}")
                    wait(h, f"sq1ma_{c}")
                    i = h.tensor_tensor(fv(P1["B1"], W), fv(P1["B3"], W),
                                        fv(P1["B1"], W), ALU.mult) if h else None
                    fence(h, i, "dv")
                    i = h.tensor_tensor(fv(P1["B1"], W), fv(P1["B1"], W),
                                        fv(P1["B5"], W), ALU.mult) if h else None
                    fence(h, i, "dv")
                    sch.ms[f"g_{c}"] = ("dv", sch.cnt["dv"])
                    last = None
                    for k in range(NCHT):
                        init = 0.0 if c == 0 else fv(OF_HCOL + k, 1)
                        if h:
                            last = h.tensor_tensor_scan(
                                fv(P1["B2"] + k * CH, CH), fv(P1["B4"] + k * CH, CH),
                                fv(P1["B1"] + k * CH, CH), init, ALU.mult, ALU.add)
                    fence(h, last, "dv")
                    sch.ms[f"uscan_{c}"] = ("dv", sch.cnt["dv"])
                    i = h.tensor_copy(fv(OF_HCOL, 8), f3(P1["B2"])[:, :, CH - 1]) if h else None
                    tick(h, i, "dv", 1, f"ucols_{c}")

            elif e == "scalar":
                for s in range(-2, NCH_RUN):
                    if s + 2 < NCH_RUN:
                        fe_act1(s + 2)
                    if s >= 0:
                        c = s
                        wait(h, f"wlin_mm_{c}")
                        i = h.activation(bv(P1["linbf"], W), bank(0, 4), AF.Copy) if h else None
                        tick(h, i, "ac", 1, f"lin_{c}")
                        if c >= 1:
                            wait(h, f"spxg_{c - 1}")
                        i = h.activation(bv(P1["xgelbf"], W), bv(P1["linbf"], W),
                                         AF.Gelu) if h else None
                        fence(h, i, "ac")
                        sch.ms[f"xgel_{c}"] = ("ac", sch.cnt["ac"])
                        d = h.dma_start(out=scr_xg[:, c * W:(c + 1) * W],
                                        in_=bv(P1["xgelbf"], W)) if h else None
                        tick(h, d, "dxg", 16, f"spxg_{c}")
                        wait(h, f"wconv_mm_{c}")
                        i = h.activation(bv(P1["prebf"], W), bank(0, 4), AF.Copy) if h else None
                        tick(h, i, "ac", 1, f"prebf_{c}")
                    if s + 2 < NCH_RUN:
                        fe_act2(s + 2)
                        fe_act3(s + 2)
                    if s < 0:
                        continue
                    c = s
                    wait(h, f"wa_mm_{c}")
                    if c >= 1:
                        wait(h, f"ucols_{c - 1}")
                    i = h.activation(fv(P1["B2"], W), bank(0, 4), AF.Sigmoid) if h else None
                    tick(h, i, "ac", 1, f"rt_{c}")
                    wait(h, f"wx_mm_{c}")
                    if c >= 1:
                        wait(h, f"g_{c - 1}")
                    i = h.activation(fv(P1["B3"], W), bank(0, 4), AF.Sigmoid) if h else None
                    tick(h, i, "ac", 1, f"it_{c}")
                    wait(h, f"lart_{c}")
                    if c >= 1:
                        wait(h, f"spa_{c - 1}")
                        wait(h, f"uscan_{c - 1}")
                    i = h.activation(fv(P1["B4"], W), fv(P1["B2"], W), AF.Exp) if h else None
                    fence(h, i, "ac")
                    sch.ms[f"a_{c}"] = ("ac", sch.cnt["ac"])
                    wait(h, f"lasum_{c}")
                    i = h.activation(fv(P1["B2"], W), fv(P1["B4"], W), AF.Square) if h else None
                    fence(h, i, "ac")
                    sch.ms[f"asq_{c}"] = ("ac", sch.cnt["ac"])
                    i = h.activation(fv(P1["B5"], W), fv(P1["B2"], W), AF.Sqrt,
                                     scale=np.float32(-1.0), bias=np.float32(1.0)) if h else None
                    tick(h, i, "ac", 1, f"sq1ma_{c}")

            elif e == "tensor":
                for s in range(-2, NCH_RUN):
                    if s >= 0:
                        c = s
                        wait(h, f"xne_{2 * c + 1}")
                        if c == 0:
                            wait(h, "fold_lin")
                        else:
                            wait(h, f"it_{c - 1}")
                        last = full_mm(h, OF_T[0], xnfm(c), 0)
                        last = bias_mm(h, 0, 0, last)
                        tick(h, last, "pe", 1, f"wlin_mm_{c}")
                        wait(h, f"lin_{c}")
                        last = full_mm(h, OF_T[1], P1["linbf"], 0)
                        last = bias_mm(h, 1, 0, last)
                        tick(h, last, "pe", 1, f"wconv_mm_{c}")
                        wait(h, f"prebf_{c}")
                        wait(h, f"pref_{c}")
                        last = full_mm(h, OF_T[2], P1["prebf"], 0)
                        last = bias_mm(h, 2, 0, last)
                        tick(h, last, "pe", 1, f"wa_mm_{c}")
                    if s + 2 < NCH_RUN:
                        cc = s + 2
                        for tl in range(2):
                            tt = 2 * cc + tl
                            wait(h, f"apply_{cc}")
                            if tt >= 2:
                                wait(h, f"r0e_{tt - 2}")
                                wait(h, f"xne_{tt - 2}")
                            if cc >= 2 and tl == 0:
                                wait(h, f"wlin_mm_{cc - 2}")
                            rb = 4096 if tt % 2 == 0 else 6144
                            xb = 5120 if tt % 2 == 0 else 7168
                            last = None
                            for k in range(NCHT):
                                if h:
                                    h.transpose(PSb[:, rb + k * 128:rb + k * 128 + 128],
                                                bv(r0ts(tt) + k * 64, 128),
                                                bv(OF_IDB, 128))
                                    last = h.transpose(
                                        PSb[:, xb + k * 128:xb + k * 128 + 128],
                                        bv(xns(tt) + k * 64, 128),
                                        bv(OF_IDB, 128))
                            tick(h, last, "pe", 1, f"T_{tt}")
                    if s >= 0:
                        c = s
                        wait(h, f"rt_{c}")
                        last = full_mm(h, OF_T[3], P1["prebf"], 0)
                        last = bias_mm(h, 3, 0, last)
                        tick(h, last, "pe", 1, f"wx_mm_{c}")

            elif e == "gpsimd":
                for c in range(NCH_RUN):
                    wait(h, f"uscan_{c}")
                    d = h.dma_start(out=scr_a[:, c * W:(c + 1) * W],
                                    in_=fv(P1["B4"], W)) if h else None
                    tick(h, d, "dsa", 16, f"spa_{c}")
                    wait(h, f"g_{c}")
                    d = h.dma_start(out=scr_g[:, c * W:(c + 1) * W],
                                    in_=fv(P1["B1"], W)) if h else None
                    tick(h, d, "dsg", 16, f"spg_{c}")

        NCH_RUN = NCH if p01n is None else p01n
        if upto >= 1:
            emit_phase(ph01)

        def ph2(e, h):
            if e == "scalar":
                wait(h, "lasum_7")
                i = h.activation(fv(OF_AUC, 8), fv(OF_RTS, 8), AF.Exp) if h else None
                tick(h, i, "ac", 1, "Acols")
                wait(h, "Ct")
                i = h.activation(fv(OF_CCS, 128, 0, 16),
                                 PSf[0:16, 3072:3072 + 128], AF.Copy) if h else None
                tick(h, i, "ac", 1, "ccs_done")
                wait(h, "Gt")
                i = h.activation(fv(OF_G, 64), PSf[:, 3072:3072 + 64],
                                 AF.Copy) if h else None
                tick(h, i, "ac", 1, "G_done")
            elif e == "vector":
                wait(h, "ucols_7")
                i = h.tensor_copy(fv(OF_AUC + 8, 8), fv(OF_HCOL, 8)) if h else None
                tick(h, i, "dv", 1, "Ucols")
                wait(h, "G_done")
                wait(h, "setup_done")
                i = h.memset(fv(OF_HIN, 8), 0.0) if h else None
                tick(h, i, "dv", 1)
                for j in range(4):
                    gA = fv(OF_G + j * 16, 8)
                    gU = fv(OF_G + j * 16 + 8, 8)
                    i = h.tensor_scalar(fv(OF_TMPC, 8), gA, fv(OF_SEL + j, 1),
                                        fv(OF_ISEL + j, 1), ALU.mult, ALU.add) if h else None
                    fence(h, i, "dv")
                    i = h.tensor_tensor(fv(OF_HIN, 8), fv(OF_TMPC, 8),
                                        fv(OF_HIN, 8), ALU.mult) if h else None
                    fence(h, i, "dv")
                    i = h.tensor_scalar(fv(OF_TMPC, 8), gU, fv(OF_SEL + j, 1),
                                        None, ALU.mult) if h else None
                    fence(h, i, "dv")
                    i = h.tensor_tensor(fv(OF_HIN, 8), fv(OF_HIN, 8),
                                        fv(OF_TMPC, 8), ALU.add) if h else None
                    fence(h, i, "dv")
                tick(h, None, "dv", 0, "hin_done")
                sch.ms["hin_done"] = ("dv", sch.cnt["dv"])
            elif e == "tensor":
                wait(h, "Acols")
                wait(h, "Ucols")
                wait(h, "xne_15")
                i = h.transpose(PSf[0:16, 3072:3072 + 128],
                                fv(OF_AUC, 16), fv(OF_IDF, 128)) if h else None
                tick(h, i, "pe", 1, "Ct")
                wait(h, "ccl_done")
                wait(h, "ccs_done")
                i = h.transpose(PSf[:, 3072:3072 + 64],
                                fv(OF_CCL, 128, 0, 64),
                                fv(OF_IDF, 64, 0, 64)) if h else None
                tick(h, i, "pe", 1, "Gt")
            elif e == "gpsimd":
                wait(h, "ccs_done")
                d = h.dma_start(out=cc_in[:, :], in_=fv(OF_CCS, 128, 0, 16)) if h else None
                v = tick(h, d, "dcc", 16)
                if h:
                    h.wait_ge(SEM["dcc"], v)
                i = h.collective_compute(
                    "AllGather", ALU.bypass,
                    replica_groups=[[0, 1, 2, 3], [4, 5, 6, 7]],
                    ins=[cc_in.opt()],
                    outs=[cc_out.opt()]) if h else None
                v = tick(h, i, "cc", 1)
                if h:
                    h.wait_ge(S_cc, v)
                d = h.dma_start(out=fv(OF_CCL, 128, 0, 64), in_=cc_out[:, :]) if h else None
                tick(h, d, "dcc", 16, "ccl_done")
                wait(h, "T_15")
                wait(h, "wx_mm_7")
                d = h.dma_start(out=bv(OF_T[4], 8192), in_=wpk["lin2"]) if h else None
                tick(h, d, "dw2", 16)
                d = h.dma_start(out=bv(OF_T[5], 8192), in_=wpk["w1"]) if h else None
                tick(h, d, "dw2", 16)
                d = h.dma_start(out=bv(OF_T[6], 8192), in_=wpk["w2"]) if h else None
                tick(h, d, "dw2", 16)
                d = h.dma_start(out=bv(OF_T[7], 8192), in_=wpk["w3"]) if h else None
                tick(h, d, "dw2", 16)
                d = h.dma_start(out=bv(OF_BIAS, 4096, 0, 1), in_=bias2) if h else None
                tick(h, d, "dw2", 16, "w2b")

        if upto >= 2:
            emit_phase(ph2, engines=("scalar", "vector", "tensor", "gpsimd"))

        def rg(c):
            return P3["rga"] if c % 2 == 0 else P3["rgb"]

        def xg3(c):
            return P3["xga"] if c % 2 == 0 else P3["xgb"]

        def mx3(c):
            return P3["mixa"] if c % 2 == 0 else P3["mixb"]

        def ph3ab(e, h):
            if e == "sync":
                for c in range(NCH):
                    base = rg(c)
                    wait(h, "spa_7")
                    wait(h, "spg_7")
                    wait(h, "spr0_7")
                    wait(h, "spxg_7")
                    if c >= 2:
                        wait(h, f"r1stt_{c - 2}")
                        wait(h, f"mix_{c - 2}")
                    d = h.dma_start(out=fv(base, W),
                                    in_=scr_a[:, c * W:(c + 1) * W]) if h else None
                    tick(h, d, "dr3a" if c % 2 == 0 else "dr3b", 16)
                    d = h.dma_start(out=fv(base + W, W),
                                    in_=scr_g[:, c * W:(c + 1) * W]) if h else None
                    tick(h, d, "dr3a" if c % 2 == 0 else "dr3b", 16)
                    d = h.dma_start(out=bv(base + 2 * W, W),
                                    in_=scr_r0[:, c * W:(c + 1) * W]) if h else None
                    tick(h, d, "dr3a" if c % 2 == 0 else "dr3b", 16)
                    d = h.dma_start(out=bv(xg3(c), W),
                                    in_=scr_xg[:, c * W:(c + 1) * W]) if h else None
                    tick(h, d, "dr3a" if c % 2 == 0 else "dr3b", 16, f"ld3_{c}")

            elif e == "vector":
                def r1b(c):
                    return P3["r1"] if c % 2 == 0 else P3["r1b"]

                def dv_hmix(cc):
                    wait(h, f"ld3_{cc}")
                    if cc == 0:
                        wait(h, "hin_done")
                    last = None
                    for k in range(NCHT):
                        init = fv(OF_HIN + k, 1) if cc == 0 else fv(OF_HCOL + k, 1)
                        if h:
                            last = h.tensor_tensor_scan(
                                fv(P3["h"] + k * CH, CH), fv(rg(cc) + k * CH, CH),
                                fv(rg(cc) + W + k * CH, CH), init, ALU.mult, ALU.add)
                    fence(h, last, "dv")
                    sch.ms[f"hscan_{cc}"] = ("dv", sch.cnt["dv"])
                    i = h.tensor_copy(fv(OF_HCOL, 8), f3(P3["h"])[:, :, CH - 1]) if h else None
                    fence(h, i, "dv")
                    sch.ms[f"hcols_{cc}"] = ("dv", sch.cnt["dv"])
                    if cc >= 2:
                        wait(h, f"stats2_mm_{cc - 2}")
                    i = h.tensor_tensor(bv(mx3(cc), W), bv(xg3(cc), W),
                                        fv(P3["h"], W), ALU.mult) if h else None
                    tick(h, i, "dv", 1, f"mix_{cc}")

                def dv_tail(cc):
                    wait(h, f"gelu_{cc}")
                    wait(h, f"w2_mm_{cc}")
                    i = h.tensor_tensor(bv(P3["xmbf"], W), fv(P3["sig"], W),
                                        bank(0, 4), ALU.mult) if h else None
                    tick(h, i, "dv", 1, f"comb_{cc}")
                    wait(h, f"w3_mm_{cc}")
                    if cc >= 1:
                        wait(h, f"spx1_{cc - 1}")
                    i = h.tensor_tensor(fv(P3["x1p"], W), bank(0, 4),
                                        fv(r1b(cc), W), ALU.add) if h else None
                    tick(h, i, "dv", 1, f"x1p_{cc}")

                dv_hmix(0)
                for c in range(NCH):
                    wait(h, f"wlin2_mm_{c}")
                    i = h.scalar_tensor_tensor(fv(r1b(c), W), PSf[:, 2048:4096],
                                               1.0, bv(rg(c) + 2 * W, W), ALU.mult,
                                               ALU.add) if h else None
                    tick(h, i, "dv", 1, f"r1stt_{c}")
                    if c + 1 < NCH:
                        dv_hmix(c + 1)
                    if c >= 1:
                        dv_tail(c - 1)
                    wait(h, f"sveq2_{c}")
                    i = h.tensor_scalar(fv(P3["s2t"], CH), fv(P3["s2t"], CH),
                                        1e-12 / SQD, None, ALU.max) if h else None
                    fence(h, i, "dv")
                    i = h.reciprocal(fv(P3["s2t"], CH), fv(P3["s2t"], CH)) if h else None
                    fence(h, i, "dv")
                    sch.ms[f"s2row_{c}"] = ("dv", sch.cnt["dv"])
                    last = None
                    for k in range(NCHT):
                        if h:
                            last = h.scalar_tensor_tensor(
                                bv(P3["xmbf"] + k * 128, 256),
                                fv(r1b(c) + k * CH, CH), fv(OF_G2 + k, 1),
                                fv(P3["s2t"], CH), ALU.mult, ALU.mult)
                    tick(h, last, "dv", 1, f"xm_{c}")
                dv_tail(NCH - 1)

            elif e == "tensor":
                def emit_wlin2(c2):
                    wait(h, f"mix_{c2}")
                    if c2 == 0:
                        wait(h, "w2b")
                    if c2 >= 1:
                        wait(h, f"r1stt_{c2 - 1}")
                    last = full_mm(h, OF_T[4], mx3(c2), 4)
                    last = bias_mm(h, 0, 4, last)
                    tick(h, last, "pe", 1, f"wlin2_mm_{c2}")

                emit_wlin2(0)
                for c in range(1, NCH + 1):
                    cc = c - 1
                    wait(h, f"sq2_{cc}")
                    if cc >= 1:
                        wait(h, f"x1p_{cc - 1}")
                        wait(h, f"sveq2_{cc - 1}")
                    last = None
                    for k in range(NCHT):
                        if h:
                            last = h.matmul(
                                PSf[:, 0:256],
                                bv(OF_ONEB, 128),
                                bv(mx3(cc) + k * 128, 256),
                                start=(k == 0), stop=(k == 7))
                    tick(h, last, "pe", 1, f"stats2_mm_{cc}")
                    if cc + 1 < NCH:
                        emit_wlin2(cc + 1)
                    wait(h, f"xm_{cc}")
                    last = full_mm(h, OF_T[5], P3["xmbf"], 0)
                    last = bias_mm(h, 1, 0, last)
                    tick(h, last, "pe", 1, f"w1_mm_{cc}")
                    wait(h, f"sig_{cc}")
                    last = full_mm(h, OF_T[6], P3["xmbf"], 0)
                    last = bias_mm(h, 2, 0, last)
                    tick(h, last, "pe", 1, f"w2_mm_{cc}")
                    wait(h, f"comb_{cc}")
                    last = full_mm(h, OF_T[7], P3["xmbf"], 0)
                    last = bias_mm(h, 3, 0, last)
                    tick(h, last, "pe", 1, f"w3_mm_{cc}")

            elif e == "scalar":
                for c in range(NCH):
                    wait(h, f"r1stt_{c}")
                    if c >= 1:
                        wait(h, f"stats2_mm_{c - 1}")
                    i = h.activation(bv(mx3(c), W),
                                     fv(P3["r1"] if c % 2 == 0 else P3["r1b"], W),
                                     AF.Square) if h else None
                    tick(h, i, "ac", 1, f"sq2_{c}")
                    wait(h, f"stats2_mm_{c}")
                    if c >= 1:
                        wait(h, f"xm_{c - 1}")
                    i = h.activation(fv(P3["s2t"], CH), PSf[:, 0:256],
                                     AF.Sqrt, scale=np.float32(1.0 / D)) if h else None
                    tick(h, i, "ac", 1, f"sveq2_{c}")
                    wait(h, f"w1_mm_{c}")
                    if c >= 1:
                        wait(h, f"comb_{c - 1}")
                    i = h.activation(fv(P3["sig"], W), bank(0, 4), AF.Sigmoid) if h else None
                    fence(h, i, "ac")
                    sch.ms[f"sig_{c}"] = ("ac", sch.cnt["ac"])
                    i = h.activation(fv(P3["sig"], W), fv(P3["sig"], W),
                                     AF.Gelu) if h else None
                    tick(h, i, "ac", 1, f"gelu_{c}")
                    wait(h, f"x1p_{c}")
                    d = h.dma_start(out=scr_x1p[:, c * W:(c + 1) * W],
                                    in_=fv(P3["x1p"], W)) if h else None
                    tick(h, d, "dx1", 16, f"spx1_{c}")

            elif e == "gpsimd":
                wait(h, "x1p_7")
                wait(h, "spx1_7")
                wait(h, "w3_mm_7")
                for ei in range(8):
                    d = h.dma_start(out=bv(OF_T[ei], 8192),
                                    in_=wpk[f"e{ei}"]) if h else None
                    tick(h, d, f"de{ei}", 16, f"ld_e{ei}")

        if upto >= 3:
            emit_phase(ph3ab)

        def x1buf(c):
            return PC["x1A"] if c % 2 == 0 else PC["x1B"]

        def x0cs(tt):
            return PC["x0a"] if tt % 2 == 0 else PC["x0b"]

        def xgs(ei):
            return PC["xg0"] if ei % 2 == 0 else PC["xg1"]

        def gmtb(c):
            return OF_GMT if c % 2 == 0 else PC["gmt2"]

        def ph3c(e, h):
            B6 = 3072
            B7 = 3584

            if e == "sync":
                def ldx1(cc):
                    if cc <= 1:
                        wait(h, "spx1_7")
                    else:
                        wait(h, f"x2T_{cc - 2}_1")
                    d = h.dma_start(out=fv(x1buf(cc), W),
                                    in_=scr_x1p[:, cc * W:(cc + 1) * W]) if h else None
                    tick(h, d, "dpa" if cc % 2 == 0 else "dpb", 16, f"ldx1_{cc}")

                def ldx0(tt):
                    d = h.dma_start(out=fv(x0cs(tt), D),
                                    in_=x_in[tt * 128:(tt + 1) * 128, :]) if h else None
                    tick(h, d, "dxa" if tt % 2 == 0 else "dxb", 16, f"ldx0_{tt}")

                def st(tt):
                    wait(h, f"ev_{tt}")
                    d = h.dma_start(out=out[tt * 128:(tt + 1) * 128, :],
                                    in_=fv(PC["y"], D)) if h else None
                    tick(h, d, "dsta" if tt % 2 == 0 else "dstb", 16, f"st_{tt}")

                ldx1(0)
                ldx0(0)
                ldx0(1)
                wait(h, "x0T_0_0")
                ldx1(1)
                ldx0(2)
                wait(h, "x0T_0_1")
                ldx0(3)
                for c in range(1, NCH):
                    st(2 * c - 2)
                    st(2 * c - 1)
                    if c + 1 < NCH:
                        ldx1(c + 1)
                    if 2 * c + 2 < 2 * NCH:
                        wait(h, f"x0T_{c}_0")
                        ldx0(2 * c + 2)
                        wait(h, f"x0T_{c}_1")
                        ldx0(2 * c + 3)
                st(2 * NCH - 2)
                st(2 * NCH - 1)
                for msn in (f"st_{2 * NCH - 2}", f"st_{2 * NCH - 1}"):
                    sem, v = sch.ms[msn]
                    if h:
                        h.wait_ge(SEM[sem], v)

            elif e == "tensor":
                def pe_x0t(cc, tl):
                    tt = 2 * cc + tl
                    wait(h, f"ldx0_{tt}")
                    if tl == 0:
                        if cc == 0:
                            wait(h, "r1stt_7")
                        elif cc >= 2:
                            wait(h, f"ev_{2 * cc - 3}")
                    else:
                        wait(h, f"x1tta_{cc}")
                    last = None
                    for k in range(NCHT):
                        if h:
                            last = h.transpose(
                                PSf[:, B6 + k * 128:B6 + k * 128 + 128],
                                fv(x0cs(tt) + k * 128, 128),
                                fv(OF_IDF, 128))
                    tick(h, last, "pe", 1, f"x0T_{cc}_{tl}")

                def pe_stats3(cc):
                    wait(h, f"x1tt_{cc}")
                    wait(h, f"sq3_{cc}")
                    last = None
                    for k in range(NCHT):
                        if h:
                            last = h.matmul(PSf[0:1, B6:B6 + 256],
                                            fv(OF_ONEF, 1),
                                            fv(x1buf(cc) + k * CH, CH),
                                            start=(k == 0), stop=(k == 7))
                    for k in range(NCHT):
                        if h:
                            last = h.matmul(PSf[0:1, B6 + 256:B6 + 512],
                                            bv(OF_ONEB, 1),
                                            bv(PC["sqbf"] + k * 128, 256),
                                            start=(k == 0), stop=(k == 7))
                    tick(h, last, "pe", 1, f"stats3_mm_{cc}")

                def pe_bc3(cc):
                    wait(h, f"rows3_{cc}")
                    i = None
                    if h:
                        h.matmul(PSf[:, B7:B7 + 256],
                                 fv(OF_ONEF, 128, 0, 1),
                                 fv(OF_ROWS, 256, 0, 1), start=True, stop=True)
                        i = h.matmul(PSf[:, B7 + 256:B7 + 512],
                                     fv(OF_ONEF, 128, 0, 1),
                                     fv(OF_ROWS + 256, 256, 0, 1),
                                     start=True, stop=True)
                    tick(h, i, "pe", 1, f"bc3_mm_{cc}")

                def pe_rtr(cc):
                    wait(h, f"xt_{cc}")
                    last = None
                    for tl in range(2):
                        for k in range(NCHT):
                            if h:
                                last = h.matmul(
                                    PSf[:, B6 + tl * 8:B6 + tl * 8 + 8],
                                    fv(PC["xt"] + k * CH + tl * 128, 128),
                                    fv(OF_RW + k * 8, 8),
                                    start=(k == 0), stop=False)
                        if h:
                            last = h.matmul(
                                PSf[:, B6 + tl * 8:B6 + tl * 8 + 8],
                                fv(OF_ONEF, 128, 0, 1), fv(OF_RB, 8, 0, 1),
                                start=False, stop=True)
                    tick(h, last, "pe", 1, f"rtr_{cc}")

                def pe_gmt(cc, tl):
                    wait(h, f"gm_{cc}_{tl}")
                    i = h.transpose(
                        PSf[0:8, B6 + 256 + tl * 128:B6 + 256 + tl * 128 + 128],
                        fv(OF_GM + tl * 8, 8), fv(OF_IDF, 128)) if h else None
                    tick(h, i, "pe", 1, f"gmt_{cc}_{tl}")

                def pe_gmbc(cc, ei):
                    boff = 2048 if ei % 2 == 0 else 2560
                    wait(h, f"gmtbf_{cc}")
                    if ei >= 2:
                        wait(h, f"xg_{cc}_{ei - 2}")
                    elif cc >= 1:
                        wait(h, f"xg_{cc - 1}_{ei + 6}")
                    i = h.matmul(PSf[:, boff:boff + CH],
                                 bv(OF_OH + ei * 64, 128, 0, 8),
                                 bv(gmtb(cc), CH, 0, 8),
                                 start=True, stop=True) if h else None
                    tick(h, i, "pe", 1, f"gmbc_{cc}_{ei}")

                def pe_exp(cc, ei):
                    wait(h, f"xg_{cc}_{ei}")
                    wait(h, f"ld_e{ei}")
                    if ei == 0:
                        if cc >= 1:
                            wait(h, f"x2_{cc - 1}")
                        else:
                            wait(h, "x1p_7")
                    last = None
                    for m in range(8):
                        for k in range(8):
                            if h:
                                last = h.matmul(
                                    PSf[:, m * CH:m * CH + CH],
                                    bv(OF_T[ei] + (k * 8 + m) * 64, 128),
                                    b3(xgs(ei))[:, k, :],
                                    start=(ei == 0 and k == 0 and m % 2 == 0),
                                    stop=False)
                    tick(h, last, "pe", 1, f"exp_{cc}_{ei}")

                def pe_expb(cc):
                    last = None
                    for m in range(8):
                        if h:
                            last = h.matmul(PSf[:, m * CH:m * CH + CH],
                                            bv(OF_EXB + m * 64, 128, 0, 8),
                                            bv(gmtb(cc), CH, 0, 8),
                                            start=False, stop=(m % 2 == 1))
                    tick(h, last, "pe", 1, f"expb_{cc}")

                def pe_x2t(cc, tl):
                    if tl == 0:
                        wait(h, f"x2_{cc}")
                        if cc == NCH - 1:
                            wait(h, f"ev_{2 * cc - 1}")
                    else:
                        wait(h, f"ev_{2 * cc}")
                    last = None
                    for k in range(NCHT):
                        if h:
                            last = h.transpose(
                                PSf[:, B6 + k * 128:B6 + k * 128 + 128],
                                fv(x1buf(cc) + k * CH + tl * 128, 128),
                                fv(OF_IDF, 128))
                    tick(h, last, "pe", 1, f"x2T_{cc}_{tl}")

                pe_x0t(0, 0)
                pe_x0t(0, 1)
                pe_stats3(0)
                pe_bc3(0)
                pe_rtr(0)
                pe_gmt(0, 0)
                pe_gmt(0, 1)
                pe_gmbc(0, 0)
                pe_gmbc(0, 1)
                for c in range(NCH):
                    fe = c + 1 if c + 1 < NCH else None
                    pe_exp(c, 0)
                    if c >= 1:
                        pe_x2t(c - 1, 0)
                    pe_gmbc(c, 2)
                    pe_exp(c, 1)
                    if c >= 1:
                        pe_x2t(c - 1, 1)
                    pe_gmbc(c, 3)
                    pe_exp(c, 2)
                    if fe is not None:
                        pe_x0t(fe, 0)
                    pe_gmbc(c, 4)
                    pe_exp(c, 3)
                    if fe is not None:
                        pe_x0t(fe, 1)
                    pe_gmbc(c, 5)
                    pe_exp(c, 4)
                    if fe is not None:
                        pe_stats3(fe)
                    pe_gmbc(c, 6)
                    pe_exp(c, 5)
                    if fe is not None:
                        pe_bc3(fe)
                    pe_gmbc(c, 7)
                    pe_exp(c, 6)
                    if fe is not None:
                        pe_rtr(fe)
                        pe_gmt(fe, 0)
                        pe_gmt(fe, 1)
                        pe_gmbc(fe, 0)
                        pe_gmbc(fe, 1)
                    pe_exp(c, 7)
                    pe_expb(c)
                pe_x2t(NCH - 1, 0)
                pe_x2t(NCH - 1, 1)

            elif e == "vector":
                def dv_x1tt(cc, tl):
                    if tl == 0:
                        wait(h, f"ldx1_{cc}")
                    wait(h, f"x0T_{cc}_{tl}")
                    dst = f3(x1buf(cc))[:, :, tl * 128:tl * 128 + 128]
                    i = h.tensor_tensor(
                        dst, dst,
                        PSf[:, B6:B6 + 1024].rearrange("p (a b) -> p a b", a=8),
                        ALU.add) if h else None
                    tick(h, i, "dv", 1, f"x1tta_{cc}" if tl == 0 else f"x1tt_{cc}")

                def dv_var3(cc):
                    wait(h, f"stats3_mm_{cc}")
                    i = h.tensor_scalar(fv(OF_ROWS + 512, 256, 0, 1),
                                        PSf[0:1, B6:B6 + 256],
                                        1.0 / D, None, ALU.mult) if h else None
                    fence(h, i, "dv")
                    i = h.tensor_tensor(fv(OF_ROWS + 768, 256, 0, 1),
                                        fv(OF_ROWS + 512, 256, 0, 1),
                                        fv(OF_ROWS + 512, 256, 0, 1), ALU.mult) if h else None
                    fence(h, i, "dv")
                    i = h.scalar_tensor_tensor(
                        fv(OF_ROWS + 768, 256, 0, 1),
                        PSf[0:1, B6 + 256:B6 + 512], 1.0 / D,
                        fv(OF_ROWS + 768, 256, 0, 1), ALU.mult, ALU.subtract) if h else None
                    tick(h, i, "dv", 1, f"var3_{cc}")

                def dv_rows3(cc):
                    wait(h, f"sve3_{cc}")
                    i = h.reciprocal(fv(OF_ROWS, 256, 0, 1),
                                     fv(OF_ROWS + 1024, 256, 0, 1)) if h else None
                    fence(h, i, "dv")
                    i = h.tensor_tensor(fv(OF_ROWS + 256, 256, 0, 1),
                                        fv(OF_ROWS + 512, 256, 0, 1),
                                        fv(OF_ROWS, 256, 0, 1), ALU.mult) if h else None
                    tick(h, i, "dv", 1, f"rows3_{cc}")

                def dv_xt(cc):
                    wait(h, f"bc3_mm_{cc}")
                    i = h.tensor_tensor(f3(PC["xt"]), f3(x1buf(cc)),
                                        PSf[:, B7:B7 + 256].unsqueeze(1)
                                        .broadcast_to([128, 8, 256]), ALU.mult) if h else None
                    fence(h, i, "dv")
                    i = h.tensor_tensor(f3(PC["xt"]), f3(PC["xt"]),
                                        PSf[:, B7 + 256:B7 + 512]
                                        .unsqueeze(1).broadcast_to([128, 8, 256]),
                                        ALU.subtract) if h else None
                    fence(h, i, "dv")
                    sch.ms[f"xt_{cc}"] = ("dv", sch.cnt["dv"])

                def dv_softmax(cc):
                    for tl in range(2):
                        wait(h, f"rtr_{cc}")
                        lg = PSf[:, B6 + tl * 8:B6 + tl * 8 + 8]
                        i = h.reduce_max(fv(OF_SCOL + tl * 16, 1), lg, AX.X) if h else None
                        fence(h, i, "dv")
                        i = h.tensor_scalar(fv(OF_SCOL + tl * 16 + 1, 1),
                                            fv(OF_SCOL + tl * 16, 1), -1.0,
                                            None, ALU.mult) if h else None
                        tick(h, i, "dv", 1, f"lmax_{cc}_{tl}")
                        wait(h, f"sme_{cc}_{tl}")
                        i = h.reciprocal(fv(OF_SCOL + tl * 16 + 3, 1),
                                         fv(OF_SCOL + tl * 16 + 2, 1)) if h else None
                        fence(h, i, "dv")
                        i = h.tensor_scalar(fv(OF_GM + tl * 8, 8), lg,
                                            fv(OF_SCOL + tl * 16, 1), None,
                                            ALU.is_ge) if h else None
                        fence(h, i, "dv")
                        i = h.tensor_scalar(fv(OF_GM + tl * 8, 8),
                                            fv(OF_GM + tl * 8, 8),
                                            fv(OF_SCOL + tl * 16 + 3, 1),
                                            None, ALU.mult) if h else None
                        tick(h, i, "dv", 1, f"gm_{cc}_{tl}")

                def dv_xtbf(cc):
                    i = h.tensor_copy(bv(PC["xtbf"], W), fv(PC["xt"], W)) if h else None
                    tick(h, i, "dv", 1, f"xtbf_{cc}")

                def dv_xg(cc, ei):
                    boff = 2048 if ei % 2 == 0 else 2560
                    wait(h, f"gmbc_{cc}_{ei}")
                    if ei >= 2:
                        wait(h, f"exp_{cc}_{ei - 2}")
                    elif cc >= 1:
                        wait(h, f"exp_{cc - 1}_{ei + 6}")
                    i = h.tensor_tensor(
                        b3(xgs(ei)), b3(PC["xtbf"]),
                        PSf[:, boff:boff + CH].unsqueeze(1)
                        .broadcast_to([128, 8, 256]), ALU.mult) if h else None
                    tick(h, i, "dv", 1, f"xg_{cc}_{ei}")

                def dv_x2(cc):
                    wait(h, f"expb_{cc}")
                    i = h.tensor_tensor(fv(x1buf(cc), W), bank(0, 4),
                                        fv(x1buf(cc), W), ALU.add) if h else None
                    tick(h, i, "dv", 1, f"x2_{cc}")

                dv_x1tt(0, 0)
                dv_x1tt(0, 1)
                dv_var3(0)
                dv_rows3(0)
                dv_xt(0)
                dv_softmax(0)
                dv_xtbf(0)
                dv_xg(0, 0)
                dv_xg(0, 1)
                for c in range(NCH):
                    fe = c + 1 if c + 1 < NCH else None
                    dv_xg(c, 2)
                    dv_xg(c, 3)
                    if fe is not None:
                        dv_x1tt(fe, 0)
                    dv_xg(c, 4)
                    if fe is not None:
                        dv_x1tt(fe, 1)
                    dv_xg(c, 5)
                    if fe is not None:
                        dv_var3(fe)
                        dv_rows3(fe)
                    dv_xg(c, 6)
                    if fe is not None:
                        dv_xt(fe)
                        dv_softmax(fe)
                    dv_xg(c, 7)
                    if fe is not None:
                        dv_xtbf(fe)
                        dv_xg(fe, 0)
                    dv_x2(c)
                    if fe is not None:
                        dv_xg(fe, 1)

            elif e == "scalar":
                def ev(tt):
                    wait(h, f"x2T_{tt // 2}_{tt % 2}")
                    if tt >= 1:
                        wait(h, f"st_{tt - 1}")
                    i = h.activation(fv(PC["y"], D), PSf[:, B6:B6 + 1024],
                                     AF.Copy) if h else None
                    tick(h, i, "ac", 1, f"ev_{tt}")

                for c in range(NCH):
                    if c >= 2:
                        ev(2 * (c - 2))
                        ev(2 * (c - 2) + 1)
                    wait(h, f"x1tt_{c}")
                    if c >= 1:
                        wait(h, f"stats3_mm_{c - 1}")
                    i = h.activation(bv(PC["sqbf"], W), fv(x1buf(c), W),
                                     AF.Square) if h else None
                    tick(h, i, "ac", 1, f"sq3_{c}")
                    wait(h, f"var3_{c}")
                    i = h.activation(fv(OF_ROWS + 1024, 256, 0, 1),
                                     fv(OF_ROWS + 768, 256, 0, 1), AF.Sqrt,
                                     bias=np.float32(EPS)) if h else None
                    tick(h, i, "ac", 1, f"sve3_{c}")
                    for tl in range(2):
                        wait(h, f"lmax_{c}_{tl}")
                        i = h.activation(fv(OF_EXPS + tl * 8, 8),
                                         PSf[:, B6 + tl * 8:B6 + tl * 8 + 8],
                                         AF.Exp, bias=fv(OF_SCOL + tl * 16 + 1, 1),
                                         accum_out=fv(OF_SCOL + tl * 16 + 2, 1)) if h else None
                        tick(h, i, "ac", 1, f"sme_{c}_{tl}")
                    wait(h, f"gmt_{c}_1")
                    if c >= 2:
                        wait(h, f"expb_{c - 2}")
                    i = h.activation(bv(gmtb(c), 256, 0, 8),
                                     PSf[0:8, B6 + 256:B6 + 512],
                                     AF.Copy) if h else None
                    tick(h, i, "ac", 1, f"gmtbf_{c}")
                for tt in range(4 * NCH - 4, 4 * NCH):
                    pass
                for tt in (2 * NCH - 4, 2 * NCH - 3, 2 * NCH - 2, 2 * NCH - 1):
                    ev(tt)

            elif e == "gpsimd":
                pass

        if upto >= 4:
            emit_phase(ph3c)
        if upto < 4:
            def ph_fin(e, h):
                if e != "sync":
                    return
                fins = {0: ["setup_done"],
                        1: ([f"r0e_{2*NCH_RUN-1}", f"xne_{2*NCH_RUN-1}",
                             f"T_{2*NCH_RUN-1}", f"apply_{NCH_RUN-1}",
                             f"spr0_{NCH_RUN-1}"]
                            + ([f"wlin_mm_{NCH_RUN-1}", f"spxg_{NCH_RUN-1}",
                                f"spr0_{NCH_RUN-1}"] if p01s >= 1 else [])
                            + (([f"wconv_mm_{NCH_RUN-1}"]
                                + ([f"pref_{NCH_RUN-1}"] if "pref" not in _SKIP else [])
                                + ([f"prebf_{NCH_RUN-1}"] if "prebf" not in _SKIP else []))
                               if p01s >= 2 else [])
                            + ([f"wa_mm_{NCH_RUN-1}", f"rt_{NCH_RUN-1}"] if p01s >= 3 else [])
                            + ([f"wx_mm_{NCH_RUN-1}", f"it_{NCH_RUN-1}"] if p01s >= 4 else [])
                            + ([f"spa_{NCH_RUN-1}", f"spg_{NCH_RUN-1}",
                                f"ucols_{NCH_RUN-1}"] if p01s >= 5 else [])),
                        2: ["hin_done", "G_done", "w2b"],
                        3: ["spx1_7", "w3_mm_7", "ld_e7"]}
                for u in range(upto + 1):
                    for msn in fins.get(u, []):
                        wait(h, msn)
                dump_off = getattr(sys.modules[__name__], "_DUMP", None)
                src_ap = fv(OF_IDF, 128) if dump_off is None else fv(dump_off, 128)
                d = h.dma_start(out=out[0:128, 0:128], in_=src_ap) if h else None
                v = tick(h, d, "dsta", 16)
                if h:
                    h.wait_ge(SEM["dsta"], v)
            emit_phase(ph_fin, engines=("sync",))

    return nc


def _pack_T(Wm):
    return np.ascontiguousarray(
        Wm.reshape(8, 128, 8, 128).transpose(3, 2, 0, 1).reshape(128, 8192))


def _pack_N(Wm):
    return np.ascontiguousarray(
        Wm.reshape(8, 128, 8, 128).transpose(1, 0, 2, 3).reshape(128, 8192))


def _prep_inputs(inputs):
    def f(k):
        return np.asarray(inputs[k], dtype=np.float32)
    x = np.ascontiguousarray(f("L0_attn_x").reshape(-1, D))
    com = {}
    com["wpk_lin"] = _pack_T(f("L0_attn_W_lin"))
    com["wpk_conv"] = _pack_T(f("L0_attn_W_conv"))
    com["wpk_wa"] = _pack_T(f("L0_attn_Wa"))
    com["wpk_wx"] = _pack_T(f("L0_attn_Wx"))
    com["wpk_lin2"] = _pack_T(f("L0_attn_W_lin2"))
    com["wpk_w1"] = _pack_T(f("L0_attn_W1"))
    com["wpk_w2"] = _pack_T(f("L0_attn_W2"))
    com["wpk_w3"] = _pack_T(f("L0_attn_W3"))
    ew = f("L0_ffn_expert_w")
    for e in range(8):
        com[f"wpk_e{e}"] = _pack_N(ew[e])
    com["bias1"] = np.concatenate(
        [f("L0_attn_b_lin"), f("L0_attn_b_conv"),
         f("L0_attn_ba"), f("L0_attn_bx")]).reshape(1, 4096)
    com["bias2"] = np.concatenate(
        [f("L0_attn_b_lin2"), f("L0_attn_b1"),
         f("L0_attn_b2"), f("L0_attn_b3")]).reshape(1, 4096)
    com["exb"] = np.ascontiguousarray(f("L0_ffn_expert_b"))
    com["rw"] = np.ascontiguousarray(
        f("L0_ffn_router_w").reshape(8, 128, 8).transpose(1, 0, 2).reshape(128, 64))
    com["rb"] = f("L0_ffn_router_b").reshape(1, 8)
    com["lam"] = np.ascontiguousarray(f("L0_attn_Lam").reshape(8, 128).T)
    com["g1c"] = np.ascontiguousarray(f("L0_attn_g1").reshape(8, 128).T)
    com["g2c"] = np.ascontiguousarray(f("L0_attn_g2").reshape(8, 128).T)
    oh = np.zeros((8, D), dtype=np.float32)
    for e in range(8):
        oh[e, e * 128:(e + 1) * 128] = 1.0
    com["oh"] = oh
    com["ident"] = np.eye(128, dtype=np.float32)
    in_maps = []
    for k in range(NCORES):
        m = dict(com)
        m["x"] = np.ascontiguousarray(x[k * TOK:(k + 1) * TOK, :])
        r = k % 4
        sel = np.zeros((128, 8), dtype=np.float32)
        sel[:, :r] = 1.0
        m["sel"] = sel
        m["isel"] = np.ascontiguousarray(1.0 - sel)
        in_maps.append(m)
    return in_maps


def kernel(**inputs):
    in_maps = _prep_inputs(inputs)
    nc = build(debug=False)
    res = run_bass_kernel_spmd(nc, in_maps, core_ids=list(range(NCORES)))
    outs = [res.results[k]["out"] for k in range(NCORES)]
    full = np.concatenate(outs, axis=0).reshape(2, 8192, 1024)
    return np.ascontiguousarray(full.astype(np.float32))


# version 11
# speedup vs baseline: 1.3530x; 1.0009x over previous
import sys
import numpy as np
import concourse.bass as bass
import concourse.mybir as mybir
from concourse.bass_utils import run_bass_kernel_spmd

F32 = mybir.dt.float32
BF16 = mybir.dt.bfloat16
AF = mybir.ActivationFunctionType
ALU = mybir.AluOpType
AX = mybir.AxisListType

NCORES = 8
_SKIP = set()
TOK = 2048
D = 1024
NCHT = 8
CH = 256
NCH = TOK // CH
W = NCHT * CH
EPS = 1e-5
SQD = 32.0

ARENA = 52224

OF_IDF = 0
OF_IDB = 128
OF_ONEF = 192
OF_ONEB = 320
OF_RW = 448
OF_LAM = 512
OF_G1 = 520
OF_G2 = 528
OF_SEL = 536
OF_ISEL = 544
OF_LAC = 552
OF_RTS = 560
OF_TMPC = 568
OF_AUC = 600
OF_G = 616
OF_HIN = 680
OF_CCS = 688
OF_CCL = 816
OF_EXB = 944
OF_OH = 1456
OF_BIAS = 1968
OF_RB = 4016
OF_SS = 4032
OF_SQ = 4048
OF_MC = 4064
OF_VAR = 4080
OF_SVE = 4096
OF_Q1 = 4112
OF_RS = 4128
OF_C1 = 4144
OF_M2T = 4160
OF_ROWS = 4176
OF_GM = 5456
OF_SCOL = 5472
OF_GMT = 5504
OF_HCOL = 5760
OF_EXPS = 5768
OF_PERS_END = 5784

OF_T = [5792 + i * 4096 for i in range(8)]
OF_SCR = OF_T[7] + 4096
assert OF_SCR == 38560
SCR_END = ARENA


def _alloc(start, end, sizes):
    out = {}
    o = start
    for name, n in sizes:
        out[name] = o
        o += n
    assert o <= end, (o, end)
    return out


P1 = _alloc(OF_SCR, SCR_END, [
    ("B2", W), ("B3", W), ("B4", W), ("B5", W),
    ("r0fm", W // 2), ("xgelbf", W // 2),
])
P1.update(_alloc(OF_T[4], OF_T[6], [
    ("x0a", D), ("x0b", D), ("r0ta", D // 2), ("r0tb", D // 2),
    ("xna", D // 2), ("xnb", D // 2), ("sqscr", D),
    ("xnfa", W // 2), ("xnfb", W // 2),
]))
P1["stage"] = OF_T[6]
P1.update(_alloc(OF_T[6], OF_T[7] + 4096, [
    ("B1", W), ("linbf", W // 2), ("prebf", W // 2),
]))

P3 = _alloc(OF_SCR, SCR_END, [
    ("rga", 3 * W), ("rgb", 3 * W),
])
P3.update(_alloc(OF_T[0], OF_T[4], [
    ("h", W), ("mixa", W // 2), ("mixb", W // 2), ("r1", W), ("xmbf", W // 2),
    ("sig", W), ("x1p", W), ("xga", W // 2), ("xgb", W // 2),
    ("s2t", CH), ("r1b", W),
]))

PC = _alloc(OF_SCR, SCR_END, [
    ("x0a", D), ("x0b", D),
    ("x1A", W), ("x1B", W),
    ("xt", W), ("xtbf", W // 2), ("sqbf", W // 2),
    ("xg0", W // 2), ("xg1", W // 2),
    ("gmt2", 256), ("y", D),
])


EPERM = [4, 5, 6, 7, 0, 1, 2, 3]


class Sched:
    def __init__(self):
        self.cnt = {}
        self.ms = {}

    def inc(self, sem, n, ms=None):
        v = self.cnt[sem] = self.cnt.get(sem, 0) + n
        if ms is not None:
            self.ms[ms] = (sem, v)
        return v


def build(debug=False, upto=4, p01n=None, p01s=5):
    nc = bass.Bass("TRN2", target_bir_lowering=False,
                   detect_race_conditions=False)

    x_in = nc.dram_tensor("x", [TOK, D], F32, kind="ExternalInput").ap()
    wname = ["lin", "conv", "wa", "wx", "lin2", "w1", "w2", "w3"] + \
            [f"e{i}" for i in range(8)]
    wpk = {n: nc.dram_tensor("wpk_" + n, [128, 8192], F32, kind="ExternalInput").ap()
           for n in wname}
    bias1 = nc.dram_tensor("bias1", [1, 4096], F32, kind="ExternalInput").ap()
    bias2 = nc.dram_tensor("bias2", [1, 4096], F32, kind="ExternalInput").ap()
    exb_in = nc.dram_tensor("exb", [8, D], F32, kind="ExternalInput").ap()
    rw_in = nc.dram_tensor("rw", [128, 64], F32, kind="ExternalInput").ap()
    rb_in = nc.dram_tensor("rb", [1, 8], F32, kind="ExternalInput").ap()
    lam_in = nc.dram_tensor("lam", [128, 8], F32, kind="ExternalInput").ap()
    g1_in = nc.dram_tensor("g1c", [128, 8], F32, kind="ExternalInput").ap()
    g2_in = nc.dram_tensor("g2c", [128, 8], F32, kind="ExternalInput").ap()
    sel_in = nc.dram_tensor("sel", [128, 8], F32, kind="ExternalInput").ap()
    isel_in = nc.dram_tensor("isel", [128, 8], F32, kind="ExternalInput").ap()
    oh_in = nc.dram_tensor("oh", [8, D], F32, kind="ExternalInput").ap()
    id_in = nc.dram_tensor("ident", [128, 128], F32, kind="ExternalInput").ap()
    out = nc.dram_tensor("out", [TOK, D], F32, kind="ExternalOutput").ap()

    skind = "ExternalOutput" if debug else "Internal"
    scr_r0 = nc.dram_tensor("scr_r0", [128, NCH * W], BF16, kind=skind).ap()
    scr_a = nc.dram_tensor("scr_a", [128, NCH * W], F32, kind=skind).ap()
    scr_g = nc.dram_tensor("scr_g", [128, NCH * W], F32, kind=skind).ap()
    scr_xg = nc.dram_tensor("scr_xg", [128, NCH * W], BF16, kind=skind).ap()
    scr_x1p = nc.dram_tensor("scr_x1p", [128, NCH * W], F32, kind=skind).ap()
    cc_in = nc.dram_tensor("cc_in", [16, 128], F32).ap()
    cc_out = nc.dram_tensor("cc_out", [64, 128], F32).ap()

    arena_h = nc.alloc_sbuf_tensor("arena", [128, ARENA], F32)
    A = arena_h.ap()
    Ab = A.bitcast(BF16)

    def fv(off, n, p0=0, p1=128):
        return A[p0:p1, off:off + n]

    def bv(off, nbf, p0=0, p1=128):
        return Ab[p0:p1, 2 * off:2 * off + nbf]

    def f3(off):
        return fv(off, W).rearrange("p (a b) -> p a b", a=NCHT)

    def b3(off):
        return bv(off, W).rearrange("p (a b) -> p a b", a=NCHT)

    sch = Sched()

    import contextlib
    SEM_NAMES = (["pe", "ac", "dv", "cc",
                  "dxa", "dxb",
                  "dW", "dw2",
                  "dxg", "dr0", "dx1",
                  "dsa", "dsg",
                  "dr3a", "dr3b",
                  "dpa", "dpb",
                  "dsta", "dstb",
                  "dcc"]
                 + [f"de{i}" for i in range(8)])
    with contextlib.ExitStack() as _es:
        PS = _es.enter_context(nc.psum_tensor("PS", [128, 4096], F32))
        SEM = {n: _es.enter_context(nc.semaphore(n)) for n in SEM_NAMES}
        S_cc = SEM["cc"]
        block = _es.enter_context(nc.Block())
        PSf = PS[:]
        PSb = PSf.bitcast(BF16)

        def bank(b, n=1):
            return PSf[:, b * 512:(b + n) * 512]

        def emit_phase(fn, engines=("sync", "gpsimd", "tensor", "scalar", "vector")):
            snap0 = dict(sch.cnt)
            for e in engines:
                fn(e, None)
            snap1 = dict(sch.cnt)
            emitters = {"sync": block.sync, "gpsimd": block.gpsimd,
                        "tensor": block.tensor, "scalar": block.scalar,
                        "vector": block.vector}
            for e in engines:
                sch.cnt = dict(snap0)
                emitters[e](lambda h, e=e: fn(e, h))
            sch.cnt = dict(snap1)

        def wait(h, ms):
            if h is not None:
                sem, v = sch.ms[ms]
                h.wait_ge(SEM[sem], v)

        def tick(h, inst, sem, n, ms=None):
            if h is not None and inst is not None:
                inst.then_inc(SEM[sem], n)
            return sch.inc(sem, n, ms)

        def fence(h, inst, eng):
            v = sch.inc(eng, 1)
            if h is not None and inst is not None:
                inst.then_inc(SEM[eng], 1)
                h.wait_ge(SEM[eng], v)
            return v

        def full_mm(h, slot_off, rhs_off, b0):
            last = None
            r3 = b3(rhs_off)
            for m in range(8):
                for k in range(8):
                    if h:
                        last = h.matmul(
                            PSf[:, b0 * 512 + m * CH:b0 * 512 + m * CH + CH],
                            bv(slot_off + (k * 8 + m) * 64, 128),
                            r3[:, k, :],
                            start=(k == 0 and m % 2 == 0), stop=False)
            return last

        def bias_mm(h, bslot, b0, last):
            for m in range(8):
                if h:
                    last = h.matmul(
                        PSf[:, b0 * 512 + m * CH:b0 * 512 + m * CH + CH],
                        bv(OF_BIAS + bslot * 512 + m * 64, 128, 0, 1),
                        bv(OF_ONEB, CH, 0, 1),
                        start=False, stop=(m % 2 == 1))
            return last

        def ph_setup(e, h):
            if e == "gpsimd":
                def dm(o, i):
                    return h.dma_start(out=o, in_=i) if h else None
                tick(h, dm(fv(OF_IDF, 128), id_in), "dW", 16)
                tick(h, dm(bv(OF_IDB, 128), id_in), "dW", 16)
                tick(h, dm(fv(OF_RW, 64), rw_in), "dW", 16)
                tick(h, dm(fv(OF_RB, 8, 0, 1), rb_in), "dW", 16)
                tick(h, dm(fv(OF_LAM, 8), lam_in), "dW", 16)
                tick(h, dm(fv(OF_G1, 8), g1_in), "dW", 16)
                tick(h, dm(fv(OF_G2, 8), g2_in), "dW", 16)
                tick(h, dm(fv(OF_SEL, 8), sel_in), "dW", 16)
                tick(h, dm(fv(OF_ISEL, 8), isel_in), "dW", 16)
                tick(h, dm(bv(OF_EXB, 1024, 0, 8), exb_in), "dW", 16)
                tick(h, dm(bv(OF_OH, 1024, 0, 8), oh_in), "dW", 16)
                tick(h, dm(bv(OF_BIAS, 4096, 0, 1), bias1), "dW", 16)
                tick(h, dm(fv(P1["stage"], 4096), wpk["lin"][:, 0:4096]), "dW", 16)
                tick(h, dm(fv(P1["stage"] + 4096, 4096), wpk["lin"][:, 4096:8192]),
                     "dW", 16)
                tick(h, dm(bv(OF_T[1], 8192), wpk["conv"]), "dW", 16)
                tick(h, dm(bv(OF_T[2], 8192), wpk["wa"]), "dW", 16)
                tick(h, dm(bv(OF_T[3], 8192), wpk["wx"]), "dW", 16, "setup_done")
            elif e == "scalar":
                wait(h, "setup_done")
                i = h.activation(fv(OF_TMPC, 8), fv(OF_LAM, 8), AF.Sigmoid) if h else None
                fence(h, i, "ac")
                i = h.activation(fv(OF_LAC, 8), fv(OF_TMPC, 8), AF.Ln) if h else None
                tick(h, i, "ac", 1, "lac_ln")
            elif e == "vector":
                i = None
                if h:
                    h.memset(fv(OF_ONEF, 128), 1.0)
                    h.memset(bv(OF_ONEB, 256), 1.0)
                    i = h.memset(fv(OF_RTS, 8), 0.0)
                tick(h, i, "dv", 1)
                wait(h, "setup_done")
                last = None
                for k in range(8):
                    if h:
                        last = h.tensor_scalar(
                            bv(OF_T[0] + k * 512, 1024),
                            fv(P1["stage"] + k * 1024, 1024),
                            fv(OF_G1 + k, 1), None, ALU.mult)
                tick(h, last, "dv", 1, "fold_lin")
                wait(h, "lac_ln")
                i = h.tensor_scalar(fv(OF_LAC, 8), fv(OF_LAC, 8), 0.125,
                                    None, ALU.mult) if h else None
                tick(h, i, "dv", 1, "lac_done")

        emit_phase(ph_setup, engines=("gpsimd", "scalar", "vector"))

        def x0s(tt):
            return P1["x0a"] if tt % 2 == 0 else P1["x0b"]

        def r0ts(tt):
            return P1["r0ta"] if tt % 2 == 0 else P1["r0tb"]

        def xns(tt):
            return P1["xna"] if tt % 2 == 0 else P1["xnb"]

        def xnfm(c):
            return P1["xnfa"] if c % 2 == 0 else P1["xnfb"]

        def ph01(e, h):
            def fe_act1(cc):
                for tl in range(2):
                    tt = 2 * cc + tl
                    wait(h, f"x0_{tt}")
                    if tt >= 1:
                        wait(h, f"ssq_{tt - 1}")
                    i = h.activation(fv(P1["sqscr"], D), fv(x0s(tt), D),
                                     AF.Square) if h else None
                    tick(h, i, "ac", 1, f"sq_{tt}")

            def fe_act2(cc):
                wait(h, f"var_{cc}")
                i = None
                if h:
                    h.activation(fv(OF_SVE + 2 * cc, 2), fv(OF_VAR + 2 * cc, 2),
                                 AF.Sqrt, bias=np.float32(EPS))
                    i = h.activation(fv(OF_Q1 + 2 * cc, 2), fv(OF_VAR + 2 * cc, 2),
                                     AF.Sqrt, scale=np.float32(D))
                tick(h, i, "ac", 1, f"sveq_{cc}")

            def fe_act3(cc):
                for tl in range(2):
                    tt = 2 * cc + tl
                    wait(h, f"T_{tt}")
                    if cc >= 1 and tl == 0:
                        wait(h, f"spr0_{cc - 1}")
                    rb = 4096 if tt % 2 == 0 else 6144
                    xb = 5120 if tt % 2 == 0 else 7168
                    i = h.activation(
                        bv(P1["r0fm"], W).rearrange("p (a b) -> p a b", a=8)
                        [:, :, tl * 128:tl * 128 + 128],
                        PSb[:, rb:rb + 1024].rearrange("p (a b) -> p a b", a=8),
                        AF.Copy) if h else None
                    tick(h, i, "ac", 1, f"r0e_{tt}")
                    i = h.activation(
                        b3(xnfm(cc))[:, :, tl * 128:tl * 128 + 128],
                        PSb[:, xb:xb + 1024].rearrange("p (a b) -> p a b", a=8),
                        AF.Copy) if h else None
                    if tl == 1:
                        fence(h, i, "ac")
                        sch.ms[f"xne_{tt}"] = ("ac", sch.cnt["ac"])
                    else:
                        tick(h, i, "ac", 1, f"xne_{tt}")
                d = h.dma_start(out=scr_r0[:, cc * W:(cc + 1) * W],
                                in_=bv(P1["r0fm"], W)) if h else None
                tick(h, d, "dr0", 16, f"spr0_{cc}")

            def fe_dve1(cc):
                for tl in range(2):
                    tt = 2 * cc + tl
                    wait(h, f"x0_{tt}")
                    i = h.reduce_sum(fv(OF_SS + tt, 1), fv(x0s(tt), D),
                                     AX.X) if h else None
                    tick(h, i, "dv", 1, f"ssum_{tt}")
                    wait(h, f"sq_{tt}")
                    i = h.reduce_sum(fv(OF_SQ + tt, 1), fv(P1["sqscr"], D),
                                     AX.X) if h else None
                    tick(h, i, "dv", 1, f"ssq_{tt}")
                i = h.tensor_scalar(fv(OF_MC + 2 * cc, 2), fv(OF_SS + 2 * cc, 2),
                                    1.0 / D, None, ALU.mult) if h else None
                fence(h, i, "dv")
                i = h.tensor_tensor(fv(OF_M2T + 2 * cc, 2), fv(OF_MC + 2 * cc, 2),
                                    fv(OF_MC + 2 * cc, 2), ALU.mult) if h else None
                fence(h, i, "dv")
                i = h.scalar_tensor_tensor(
                    fv(OF_VAR + 2 * cc, 2), fv(OF_SQ + 2 * cc, 2), 1.0 / D,
                    fv(OF_M2T + 2 * cc, 2), ALU.mult, ALU.subtract) if h else None
                tick(h, i, "dv", 1, f"var_{cc}")

            def fe_dve2(cc):
                wait(h, f"sveq_{cc}")
                i = h.reciprocal(fv(OF_RS + 2 * cc, 2), fv(OF_SVE + 2 * cc, 2)) if h else None
                fence(h, i, "dv")
                i = h.tensor_tensor(fv(OF_M2T + 2 * cc, 2), fv(OF_Q1 + 2 * cc, 2),
                                    fv(OF_RS + 2 * cc, 2), ALU.mult) if h else None
                fence(h, i, "dv")
                i = h.tensor_scalar(fv(OF_M2T + 2 * cc, 2), fv(OF_M2T + 2 * cc, 2),
                                    1e-12, None, ALU.max) if h else None
                fence(h, i, "dv")
                i = h.reciprocal(fv(OF_M2T + 2 * cc, 2), fv(OF_M2T + 2 * cc, 2)) if h else None
                fence(h, i, "dv")
                i = h.scalar_tensor_tensor(
                    fv(OF_C1 + 2 * cc, 2), fv(OF_M2T + 2 * cc, 2), SQD,
                    fv(OF_RS + 2 * cc, 2), ALU.mult, ALU.mult) if h else None
                fence(h, i, "dv")
                for tl in range(2):
                    tt = 2 * cc + tl
                    if tt >= 2:
                        wait(h, f"T_{tt - 2}")
                    i = None
                    if h:
                        h.tensor_scalar(bv(r0ts(tt), 1024), fv(x0s(tt), D),
                                        fv(OF_MC + tt, 1), fv(OF_RS + tt, 1),
                                        ALU.subtract, ALU.mult)
                        i = h.tensor_scalar(bv(xns(tt), 1024), fv(x0s(tt), D),
                                            fv(OF_MC + tt, 1), fv(OF_C1 + tt, 1),
                                            ALU.subtract, ALU.mult)
                    tick(h, i, "dv", 1, f"apply_{cc}" if tl == 1 else None)

            if e == "sync":
                for tt in range(2 * NCH_RUN):
                    if tt >= 2:
                        wait(h, f"apply_{tt // 2 - 1}")
                        wait(h, f"sq_{tt - 2}")
                    d = h.dma_start(out=fv(x0s(tt), D),
                                    in_=x_in[tt * 128:(tt + 1) * 128, :]) if h else None
                    tick(h, d, "dxa" if tt % 2 == 0 else "dxb", 16, f"x0_{tt}")

            elif e == "vector":
                for s in range(-2, NCH_RUN):
                    if s + 2 < NCH_RUN:
                        fe_dve1(s + 2)
                    if s >= 0:
                        wait(h, f"prebf_{s}")
                        if s >= 1:
                            wait(h, f"spg_{s - 1}")
                        i = h.tensor_copy(fv(P1["B1"], W), bank(0, 4)) if h else None
                        tick(h, i, "dv", 1, f"pref_{s}")
                    if s + 2 < NCH_RUN:
                        fe_dve2(s + 2)
                    if s < 0:
                        continue
                    c = s
                    wait(h, f"rt_{c}")
                    last = None
                    for k in range(NCHT):
                        if h:
                            last = h.tensor_scalar(
                                fv(P1["B2"] + k * CH, CH), fv(P1["B2"] + k * CH, CH),
                                fv(OF_LAC + k, 1), None, ALU.mult)
                    fence(h, last, "dv")
                    sch.ms[f"lart_{c}"] = ("dv", sch.cnt["dv"])
                    i = h.tensor_reduce(fv(OF_TMPC, 8), f3(P1["B2"]), AX.X, ALU.add) if h else None
                    fence(h, i, "dv")
                    i = h.tensor_tensor(fv(OF_RTS, 8), fv(OF_RTS, 8),
                                        fv(OF_TMPC, 8), ALU.add) if h else None
                    tick(h, i, "dv", 1, f"lasum_{c}")
                    wait(h, f"it_{c}")
                    wait(h, f"sq1ma_{c}")
                    i = h.tensor_tensor(fv(P1["B1"], W), fv(P1["B3"], W),
                                        fv(P1["B1"], W), ALU.mult) if h else None
                    fence(h, i, "dv")
                    i = h.tensor_tensor(fv(P1["B1"], W), fv(P1["B1"], W),
                                        fv(P1["B5"], W), ALU.mult) if h else None
                    fence(h, i, "dv")
                    sch.ms[f"g_{c}"] = ("dv", sch.cnt["dv"])
                    last = None
                    for k in range(NCHT):
                        init = 0.0 if c == 0 else fv(OF_HCOL + k, 1)
                        if h:
                            last = h.tensor_tensor_scan(
                                fv(P1["B2"] + k * CH, CH), fv(P1["B4"] + k * CH, CH),
                                fv(P1["B1"] + k * CH, CH), init, ALU.mult, ALU.add)
                    fence(h, last, "dv")
                    sch.ms[f"uscan_{c}"] = ("dv", sch.cnt["dv"])
                    i = h.tensor_copy(fv(OF_HCOL, 8), f3(P1["B2"])[:, :, CH - 1]) if h else None
                    tick(h, i, "dv", 1, f"ucols_{c}")

            elif e == "scalar":
                for s in range(-2, NCH_RUN):
                    if s + 2 < NCH_RUN:
                        fe_act1(s + 2)
                    if s >= 0:
                        c = s
                        wait(h, f"wlin_mm_{c}")
                        i = h.activation(bv(P1["linbf"], W), bank(0, 4), AF.Copy) if h else None
                        tick(h, i, "ac", 1, f"lin_{c}")
                        if c >= 1:
                            wait(h, f"spxg_{c - 1}")
                        i = h.activation(bv(P1["xgelbf"], W), bv(P1["linbf"], W),
                                         AF.Gelu) if h else None
                        fence(h, i, "ac")
                        sch.ms[f"xgel_{c}"] = ("ac", sch.cnt["ac"])
                        d = h.dma_start(out=scr_xg[:, c * W:(c + 1) * W],
                                        in_=bv(P1["xgelbf"], W)) if h else None
                        tick(h, d, "dxg", 16, f"spxg_{c}")
                        wait(h, f"wconv_mm_{c}")
                        i = h.activation(bv(P1["prebf"], W), bank(0, 4), AF.Copy) if h else None
                        tick(h, i, "ac", 1, f"prebf_{c}")
                    if s + 2 < NCH_RUN:
                        fe_act2(s + 2)
                        fe_act3(s + 2)
                    if s < 0:
                        continue
                    c = s
                    wait(h, f"wa_mm_{c}")
                    if c >= 1:
                        wait(h, f"ucols_{c - 1}")
                    i = h.activation(fv(P1["B2"], W), bank(0, 4), AF.Sigmoid) if h else None
                    tick(h, i, "ac", 1, f"rt_{c}")
                    wait(h, f"wx_mm_{c}")
                    if c >= 1:
                        wait(h, f"g_{c - 1}")
                    i = h.activation(fv(P1["B3"], W), bank(0, 4), AF.Sigmoid) if h else None
                    tick(h, i, "ac", 1, f"it_{c}")
                    wait(h, f"lart_{c}")
                    if c >= 1:
                        wait(h, f"spa_{c - 1}")
                        wait(h, f"uscan_{c - 1}")
                    i = h.activation(fv(P1["B4"], W), fv(P1["B2"], W), AF.Exp) if h else None
                    fence(h, i, "ac")
                    sch.ms[f"a_{c}"] = ("ac", sch.cnt["ac"])
                    wait(h, f"lasum_{c}")
                    i = h.activation(fv(P1["B2"], W), fv(P1["B4"], W), AF.Square) if h else None
                    fence(h, i, "ac")
                    sch.ms[f"asq_{c}"] = ("ac", sch.cnt["ac"])
                    i = h.activation(fv(P1["B5"], W), fv(P1["B2"], W), AF.Sqrt,
                                     scale=np.float32(-1.0), bias=np.float32(1.0)) if h else None
                    tick(h, i, "ac", 1, f"sq1ma_{c}")

            elif e == "tensor":
                for s in range(-2, NCH_RUN):
                    if s >= 0:
                        c = s
                        wait(h, f"xne_{2 * c + 1}")
                        if c == 0:
                            wait(h, "fold_lin")
                        else:
                            wait(h, f"it_{c - 1}")
                        last = full_mm(h, OF_T[0], xnfm(c), 0)
                        last = bias_mm(h, 0, 0, last)
                        tick(h, last, "pe", 1, f"wlin_mm_{c}")
                        wait(h, f"lin_{c}")
                        last = full_mm(h, OF_T[1], P1["linbf"], 0)
                        last = bias_mm(h, 1, 0, last)
                        tick(h, last, "pe", 1, f"wconv_mm_{c}")
                        wait(h, f"prebf_{c}")
                        wait(h, f"pref_{c}")
                        last = full_mm(h, OF_T[2], P1["prebf"], 0)
                        last = bias_mm(h, 2, 0, last)
                        tick(h, last, "pe", 1, f"wa_mm_{c}")
                    if s + 2 < NCH_RUN:
                        cc = s + 2
                        for tl in range(2):
                            tt = 2 * cc + tl
                            wait(h, f"apply_{cc}")
                            if tt >= 2:
                                wait(h, f"r0e_{tt - 2}")
                                wait(h, f"xne_{tt - 2}")
                            if cc >= 2 and tl == 0:
                                wait(h, f"wlin_mm_{cc - 2}")
                            rb = 4096 if tt % 2 == 0 else 6144
                            xb = 5120 if tt % 2 == 0 else 7168
                            last = None
                            for k in range(NCHT):
                                if h:
                                    h.transpose(PSb[:, rb + k * 128:rb + k * 128 + 128],
                                                bv(r0ts(tt) + k * 64, 128),
                                                bv(OF_IDB, 128))
                                    last = h.transpose(
                                        PSb[:, xb + k * 128:xb + k * 128 + 128],
                                        bv(xns(tt) + k * 64, 128),
                                        bv(OF_IDB, 128))
                            tick(h, last, "pe", 1, f"T_{tt}")
                    if s >= 0:
                        c = s
                        wait(h, f"rt_{c}")
                        last = full_mm(h, OF_T[3], P1["prebf"], 0)
                        last = bias_mm(h, 3, 0, last)
                        tick(h, last, "pe", 1, f"wx_mm_{c}")

            elif e == "gpsimd":
                for c in range(NCH_RUN):
                    wait(h, f"uscan_{c}")
                    d = h.dma_start(out=scr_a[:, c * W:(c + 1) * W],
                                    in_=fv(P1["B4"], W)) if h else None
                    tick(h, d, "dsa", 16, f"spa_{c}")
                    wait(h, f"g_{c}")
                    d = h.dma_start(out=scr_g[:, c * W:(c + 1) * W],
                                    in_=fv(P1["B1"], W)) if h else None
                    tick(h, d, "dsg", 16, f"spg_{c}")

        NCH_RUN = NCH if p01n is None else p01n
        if upto >= 1:
            emit_phase(ph01)

        def ph2(e, h):
            if e == "scalar":
                wait(h, "lasum_7")
                i = h.activation(fv(OF_AUC, 8), fv(OF_RTS, 8), AF.Exp) if h else None
                tick(h, i, "ac", 1, "Acols")
                wait(h, "Ct")
                i = h.activation(fv(OF_CCS, 128, 0, 16),
                                 PSf[0:16, 3072:3072 + 128], AF.Copy) if h else None
                tick(h, i, "ac", 1, "ccs_done")
                wait(h, "Gt")
                i = h.activation(fv(OF_G, 64), PSf[:, 3072:3072 + 64],
                                 AF.Copy) if h else None
                tick(h, i, "ac", 1, "G_done")
            elif e == "vector":
                wait(h, "ucols_7")
                i = h.tensor_copy(fv(OF_AUC + 8, 8), fv(OF_HCOL, 8)) if h else None
                tick(h, i, "dv", 1, "Ucols")
                wait(h, "G_done")
                wait(h, "setup_done")
                i = h.memset(fv(OF_HIN, 8), 0.0) if h else None
                tick(h, i, "dv", 1)
                for j in range(4):
                    gA = fv(OF_G + j * 16, 8)
                    gU = fv(OF_G + j * 16 + 8, 8)
                    i = h.tensor_scalar(fv(OF_TMPC, 8), gA, fv(OF_SEL + j, 1),
                                        fv(OF_ISEL + j, 1), ALU.mult, ALU.add) if h else None
                    fence(h, i, "dv")
                    i = h.tensor_tensor(fv(OF_HIN, 8), fv(OF_TMPC, 8),
                                        fv(OF_HIN, 8), ALU.mult) if h else None
                    fence(h, i, "dv")
                    i = h.tensor_scalar(fv(OF_TMPC, 8), gU, fv(OF_SEL + j, 1),
                                        None, ALU.mult) if h else None
                    fence(h, i, "dv")
                    i = h.tensor_tensor(fv(OF_HIN, 8), fv(OF_HIN, 8),
                                        fv(OF_TMPC, 8), ALU.add) if h else None
                    fence(h, i, "dv")
                tick(h, None, "dv", 0, "hin_done")
                sch.ms["hin_done"] = ("dv", sch.cnt["dv"])
            elif e == "tensor":
                wait(h, "Acols")
                wait(h, "Ucols")
                wait(h, "xne_15")
                i = h.transpose(PSf[0:16, 3072:3072 + 128],
                                fv(OF_AUC, 16), fv(OF_IDF, 128)) if h else None
                tick(h, i, "pe", 1, "Ct")
                wait(h, "ccl_done")
                wait(h, "ccs_done")
                i = h.transpose(PSf[:, 3072:3072 + 64],
                                fv(OF_CCL, 128, 0, 64),
                                fv(OF_IDF, 64, 0, 64)) if h else None
                tick(h, i, "pe", 1, "Gt")
            elif e == "gpsimd":
                wait(h, "ccs_done")
                d = h.dma_start(out=cc_in[:, :], in_=fv(OF_CCS, 128, 0, 16)) if h else None
                v = tick(h, d, "dcc", 16)
                if h:
                    h.wait_ge(SEM["dcc"], v)
                i = h.collective_compute(
                    "AllGather", ALU.bypass,
                    replica_groups=[[0, 1, 2, 3], [4, 5, 6, 7]],
                    ins=[cc_in.opt()],
                    outs=[cc_out.opt()]) if h else None
                v = tick(h, i, "cc", 1)
                if h:
                    h.wait_ge(S_cc, v)
                d = h.dma_start(out=fv(OF_CCL, 128, 0, 64), in_=cc_out[:, :]) if h else None
                tick(h, d, "dcc", 16, "ccl_done")
                wait(h, "T_15")
                wait(h, "wx_mm_7")
                d = h.dma_start(out=bv(OF_T[4], 8192), in_=wpk["lin2"]) if h else None
                tick(h, d, "dw2", 16)
                d = h.dma_start(out=bv(OF_T[5], 8192), in_=wpk["w1"]) if h else None
                tick(h, d, "dw2", 16)
                d = h.dma_start(out=bv(OF_T[6], 8192), in_=wpk["w2"]) if h else None
                tick(h, d, "dw2", 16)
                d = h.dma_start(out=bv(OF_T[7], 8192), in_=wpk["w3"]) if h else None
                tick(h, d, "dw2", 16)
                d = h.dma_start(out=bv(OF_BIAS, 4096, 0, 1), in_=bias2) if h else None
                tick(h, d, "dw2", 16, "w2b")

        if upto >= 2:
            emit_phase(ph2, engines=("scalar", "vector", "tensor", "gpsimd"))

        def rg(c):
            return P3["rga"] if c % 2 == 0 else P3["rgb"]

        def xg3(c):
            return P3["xga"] if c % 2 == 0 else P3["xgb"]

        def mx3(c):
            return P3["mixa"] if c % 2 == 0 else P3["mixb"]

        def ph3ab(e, h):
            if e == "sync":
                for c in range(NCH):
                    base = rg(c)
                    wait(h, "spa_7")
                    wait(h, "spg_7")
                    wait(h, "spr0_7")
                    wait(h, "spxg_7")
                    if c >= 2:
                        wait(h, f"r1stt_{c - 2}")
                        wait(h, f"mix_{c - 2}")
                    d = h.dma_start(out=fv(base, W),
                                    in_=scr_a[:, c * W:(c + 1) * W]) if h else None
                    tick(h, d, "dr3a" if c % 2 == 0 else "dr3b", 16)
                    d = h.dma_start(out=fv(base + W, W),
                                    in_=scr_g[:, c * W:(c + 1) * W]) if h else None
                    tick(h, d, "dr3a" if c % 2 == 0 else "dr3b", 16)
                    d = h.dma_start(out=bv(base + 2 * W, W),
                                    in_=scr_r0[:, c * W:(c + 1) * W]) if h else None
                    tick(h, d, "dr3a" if c % 2 == 0 else "dr3b", 16)
                    d = h.dma_start(out=bv(xg3(c), W),
                                    in_=scr_xg[:, c * W:(c + 1) * W]) if h else None
                    tick(h, d, "dr3a" if c % 2 == 0 else "dr3b", 16, f"ld3_{c}")

            elif e == "vector":
                def r1b(c):
                    return P3["r1"] if c % 2 == 0 else P3["r1b"]

                def dv_hmix(cc):
                    wait(h, f"ld3_{cc}")
                    if cc == 0:
                        wait(h, "hin_done")
                    last = None
                    for k in range(NCHT):
                        init = fv(OF_HIN + k, 1) if cc == 0 else fv(OF_HCOL + k, 1)
                        if h:
                            last = h.tensor_tensor_scan(
                                fv(P3["h"] + k * CH, CH), fv(rg(cc) + k * CH, CH),
                                fv(rg(cc) + W + k * CH, CH), init, ALU.mult, ALU.add)
                    fence(h, last, "dv")
                    sch.ms[f"hscan_{cc}"] = ("dv", sch.cnt["dv"])
                    i = h.tensor_copy(fv(OF_HCOL, 8), f3(P3["h"])[:, :, CH - 1]) if h else None
                    fence(h, i, "dv")
                    sch.ms[f"hcols_{cc}"] = ("dv", sch.cnt["dv"])
                    if cc >= 2:
                        wait(h, f"stats2_mm_{cc - 2}")
                    i = h.tensor_tensor(bv(mx3(cc), W), bv(xg3(cc), W),
                                        fv(P3["h"], W), ALU.mult) if h else None
                    tick(h, i, "dv", 1, f"mix_{cc}")

                def dv_tail(cc):
                    wait(h, f"gelu_{cc}")
                    wait(h, f"w2_mm_{cc}")
                    i = h.tensor_tensor(bv(P3["xmbf"], W), fv(P3["sig"], W),
                                        bank(0, 4), ALU.mult) if h else None
                    tick(h, i, "dv", 1, f"comb_{cc}")
                    wait(h, f"w3_mm_{cc}")
                    if cc >= 1:
                        wait(h, f"spx1_{cc - 1}")
                    i = h.tensor_tensor(fv(P3["x1p"], W), bank(0, 4),
                                        fv(r1b(cc), W), ALU.add) if h else None
                    tick(h, i, "dv", 1, f"x1p_{cc}")

                dv_hmix(0)
                for c in range(NCH):
                    wait(h, f"wlin2_mm_{c}")
                    i = h.scalar_tensor_tensor(fv(r1b(c), W), PSf[:, 2048:4096],
                                               1.0, bv(rg(c) + 2 * W, W), ALU.mult,
                                               ALU.add) if h else None
                    tick(h, i, "dv", 1, f"r1stt_{c}")
                    if c + 1 < NCH:
                        dv_hmix(c + 1)
                    if c >= 1:
                        dv_tail(c - 1)
                    wait(h, f"sveq2_{c}")
                    i = h.tensor_scalar(fv(P3["s2t"], CH), fv(P3["s2t"], CH),
                                        1e-12 / SQD, None, ALU.max) if h else None
                    fence(h, i, "dv")
                    i = h.reciprocal(fv(P3["s2t"], CH), fv(P3["s2t"], CH)) if h else None
                    fence(h, i, "dv")
                    sch.ms[f"s2row_{c}"] = ("dv", sch.cnt["dv"])
                    last = None
                    for k in range(NCHT):
                        if h:
                            last = h.scalar_tensor_tensor(
                                bv(P3["xmbf"] + k * 128, 256),
                                fv(r1b(c) + k * CH, CH), fv(OF_G2 + k, 1),
                                fv(P3["s2t"], CH), ALU.mult, ALU.mult)
                    tick(h, last, "dv", 1, f"xm_{c}")
                dv_tail(NCH - 1)

            elif e == "tensor":
                def emit_wlin2(c2):
                    wait(h, f"mix_{c2}")
                    if c2 == 0:
                        wait(h, "w2b")
                    if c2 >= 1:
                        wait(h, f"r1stt_{c2 - 1}")
                    last = full_mm(h, OF_T[4], mx3(c2), 4)
                    last = bias_mm(h, 0, 4, last)
                    tick(h, last, "pe", 1, f"wlin2_mm_{c2}")

                emit_wlin2(0)
                for c in range(1, NCH + 1):
                    cc = c - 1
                    wait(h, f"sq2_{cc}")
                    if cc >= 1:
                        wait(h, f"x1p_{cc - 1}")
                        wait(h, f"sveq2_{cc - 1}")
                    last = None
                    for k in range(NCHT):
                        if h:
                            last = h.matmul(
                                PSf[:, 0:256],
                                bv(OF_ONEB, 128),
                                bv(mx3(cc) + k * 128, 256),
                                start=(k == 0), stop=(k == 7))
                    tick(h, last, "pe", 1, f"stats2_mm_{cc}")
                    if cc + 1 < NCH:
                        emit_wlin2(cc + 1)
                    wait(h, f"xm_{cc}")
                    last = full_mm(h, OF_T[5], P3["xmbf"], 0)
                    last = bias_mm(h, 1, 0, last)
                    tick(h, last, "pe", 1, f"w1_mm_{cc}")
                    wait(h, f"sig_{cc}")
                    last = full_mm(h, OF_T[6], P3["xmbf"], 0)
                    last = bias_mm(h, 2, 0, last)
                    tick(h, last, "pe", 1, f"w2_mm_{cc}")
                    wait(h, f"comb_{cc}")
                    last = full_mm(h, OF_T[7], P3["xmbf"], 0)
                    last = bias_mm(h, 3, 0, last)
                    tick(h, last, "pe", 1, f"w3_mm_{cc}")

            elif e == "scalar":
                for c in range(NCH):
                    wait(h, f"r1stt_{c}")
                    if c >= 1:
                        wait(h, f"stats2_mm_{c - 1}")
                    i = h.activation(bv(mx3(c), W),
                                     fv(P3["r1"] if c % 2 == 0 else P3["r1b"], W),
                                     AF.Square) if h else None
                    tick(h, i, "ac", 1, f"sq2_{c}")
                    wait(h, f"stats2_mm_{c}")
                    if c >= 1:
                        wait(h, f"xm_{c - 1}")
                    i = h.activation(fv(P3["s2t"], CH), PSf[:, 0:256],
                                     AF.Sqrt, scale=np.float32(1.0 / D)) if h else None
                    tick(h, i, "ac", 1, f"sveq2_{c}")
                    wait(h, f"w1_mm_{c}")
                    if c >= 1:
                        wait(h, f"comb_{c - 1}")
                    i = h.activation(fv(P3["sig"], W), bank(0, 4), AF.Sigmoid) if h else None
                    fence(h, i, "ac")
                    sch.ms[f"sig_{c}"] = ("ac", sch.cnt["ac"])
                    i = h.activation(fv(P3["sig"], W), fv(P3["sig"], W),
                                     AF.Gelu) if h else None
                    tick(h, i, "ac", 1, f"gelu_{c}")
                    wait(h, f"x1p_{c}")
                    d = h.dma_start(out=scr_x1p[:, c * W:(c + 1) * W],
                                    in_=fv(P3["x1p"], W)) if h else None
                    tick(h, d, "dx1", 16, f"spx1_{c}")

            elif e == "gpsimd":
                slot_free = {4: ["wlin2_mm_7"], 5: ["w1_mm_7"], 6: ["w2_mm_7"],
                             7: ["w3_mm_7"], 0: ["stats2_mm_7"],
                             1: ["comb_7", "x1p_6"],
                             2: ["spx1_7", "mix_7"],
                             3: ["xm_7", "x1p_7"]}
                for ei in range(8):
                    for msn in slot_free[EPERM[ei]]:
                        wait(h, msn)
                    d = h.dma_start(out=bv(OF_T[EPERM[ei]], 8192),
                                    in_=wpk[f"e{ei}"]) if h else None
                    tick(h, d, f"de{ei}", 16, f"ld_e{ei}")

        if upto >= 3:
            emit_phase(ph3ab)

        def x1buf(c):
            return PC["x1A"] if c % 2 == 0 else PC["x1B"]

        def x0cs(tt):
            return PC["x0a"] if tt % 2 == 0 else PC["x0b"]

        def xgs(ei):
            return PC["xg0"] if ei % 2 == 0 else PC["xg1"]

        def gmtb(c):
            return OF_GMT if c % 2 == 0 else PC["gmt2"]

        def ph3c(e, h):
            B6 = 3072
            B7 = 3584

            if e == "sync":
                def ldx1(cc):
                    if cc <= 1:
                        wait(h, "spx1_7")
                    else:
                        wait(h, f"x2T_{cc - 2}_1")
                    d = h.dma_start(out=fv(x1buf(cc), W),
                                    in_=scr_x1p[:, cc * W:(cc + 1) * W]) if h else None
                    tick(h, d, "dpa" if cc % 2 == 0 else "dpb", 16, f"ldx1_{cc}")

                def ldx0(tt):
                    d = h.dma_start(out=fv(x0cs(tt), D),
                                    in_=x_in[tt * 128:(tt + 1) * 128, :]) if h else None
                    tick(h, d, "dxa" if tt % 2 == 0 else "dxb", 16, f"ldx0_{tt}")

                def st(tt):
                    wait(h, f"ev_{tt}")
                    d = h.dma_start(out=out[tt * 128:(tt + 1) * 128, :],
                                    in_=fv(PC["y"], D)) if h else None
                    tick(h, d, "dsta" if tt % 2 == 0 else "dstb", 16, f"st_{tt}")

                ldx1(0)
                ldx0(0)
                ldx0(1)
                wait(h, "x0T_0_0")
                ldx1(1)
                ldx0(2)
                wait(h, "x0T_0_1")
                ldx0(3)
                for c in range(1, NCH):
                    st(2 * c - 2)
                    st(2 * c - 1)
                    if c + 1 < NCH:
                        ldx1(c + 1)
                    if 2 * c + 2 < 2 * NCH:
                        wait(h, f"x0T_{c}_0")
                        ldx0(2 * c + 2)
                        wait(h, f"x0T_{c}_1")
                        ldx0(2 * c + 3)
                st(2 * NCH - 2)
                st(2 * NCH - 1)
                for msn in (f"st_{2 * NCH - 2}", f"st_{2 * NCH - 1}"):
                    sem, v = sch.ms[msn]
                    if h:
                        h.wait_ge(SEM[sem], v)

            elif e == "tensor":
                def pe_x0t(cc, tl):
                    tt = 2 * cc + tl
                    wait(h, f"ldx0_{tt}")
                    if tl == 0:
                        if cc == 0:
                            wait(h, "r1stt_7")
                        elif cc >= 2:
                            wait(h, f"ev_{2 * cc - 3}")
                    else:
                        wait(h, f"x1tta_{cc}")
                    last = None
                    for k in range(NCHT):
                        if h:
                            last = h.transpose(
                                PSf[:, B6 + k * 128:B6 + k * 128 + 128],
                                fv(x0cs(tt) + k * 128, 128),
                                fv(OF_IDF, 128))
                    tick(h, last, "pe", 1, f"x0T_{cc}_{tl}")

                def pe_stats3(cc):
                    wait(h, f"x1tt_{cc}")
                    wait(h, f"sq3_{cc}")
                    last = None
                    for k in range(NCHT):
                        if h:
                            last = h.matmul(PSf[0:1, B6:B6 + 256],
                                            fv(OF_ONEF, 1),
                                            fv(x1buf(cc) + k * CH, CH),
                                            start=(k == 0), stop=(k == 7))
                    for k in range(NCHT):
                        if h:
                            last = h.matmul(PSf[0:1, B6 + 256:B6 + 512],
                                            bv(OF_ONEB, 1),
                                            bv(PC["sqbf"] + k * 128, 256),
                                            start=(k == 0), stop=(k == 7))
                    tick(h, last, "pe", 1, f"stats3_mm_{cc}")

                def pe_bc3(cc):
                    wait(h, f"rows3_{cc}")
                    i = None
                    if h:
                        h.matmul(PSf[:, B7:B7 + 256],
                                 fv(OF_ONEF, 128, 0, 1),
                                 fv(OF_ROWS, 256, 0, 1), start=True, stop=True)
                        i = h.matmul(PSf[:, B7 + 256:B7 + 512],
                                     fv(OF_ONEF, 128, 0, 1),
                                     fv(OF_ROWS + 256, 256, 0, 1),
                                     start=True, stop=True)
                    tick(h, i, "pe", 1, f"bc3_mm_{cc}")

                def pe_rtr(cc):
                    wait(h, f"xt_{cc}")
                    last = None
                    for tl in range(2):
                        for k in range(NCHT):
                            if h:
                                last = h.matmul(
                                    PSf[:, B6 + tl * 8:B6 + tl * 8 + 8],
                                    fv(PC["xt"] + k * CH + tl * 128, 128),
                                    fv(OF_RW + k * 8, 8),
                                    start=(k == 0), stop=False)
                        if h:
                            last = h.matmul(
                                PSf[:, B6 + tl * 8:B6 + tl * 8 + 8],
                                fv(OF_ONEF, 128, 0, 1), fv(OF_RB, 8, 0, 1),
                                start=False, stop=True)
                    tick(h, last, "pe", 1, f"rtr_{cc}")

                def pe_gmt(cc, tl):
                    wait(h, f"gm_{cc}_{tl}")
                    i = h.transpose(
                        PSf[0:8, B6 + 256 + tl * 128:B6 + 256 + tl * 128 + 128],
                        fv(OF_GM + tl * 8, 8), fv(OF_IDF, 128)) if h else None
                    tick(h, i, "pe", 1, f"gmt_{cc}_{tl}")

                def pe_gmbc(cc, ei):
                    boff = 2048 if ei % 2 == 0 else 2560
                    wait(h, f"gmtbf_{cc}")
                    if ei >= 2:
                        wait(h, f"xg_{cc}_{ei - 2}")
                    elif cc >= 1:
                        wait(h, f"xg_{cc - 1}_{ei + 6}")
                    i = h.matmul(PSf[:, boff:boff + CH],
                                 bv(OF_OH + ei * 64, 128, 0, 8),
                                 bv(gmtb(cc), CH, 0, 8),
                                 start=True, stop=True) if h else None
                    tick(h, i, "pe", 1, f"gmbc_{cc}_{ei}")

                def pe_exp(cc, ei):
                    wait(h, f"xg_{cc}_{ei}")
                    wait(h, f"ld_e{ei}")
                    if ei == 0:
                        if cc >= 1:
                            wait(h, f"x2_{cc - 1}")
                        else:
                            wait(h, "x1p_7")
                    last = None
                    for m in range(8):
                        for k in range(8):
                            if h:
                                last = h.matmul(
                                    PSf[:, m * CH:m * CH + CH],
                                    bv(OF_T[EPERM[ei]] + (k * 8 + m) * 64, 128),
                                    b3(xgs(ei))[:, k, :],
                                    start=(ei == 0 and k == 0 and m % 2 == 0),
                                    stop=False)
                    tick(h, last, "pe", 1, f"exp_{cc}_{ei}")

                def pe_expb(cc):
                    last = None
                    for m in range(8):
                        if h:
                            last = h.matmul(PSf[:, m * CH:m * CH + CH],
                                            bv(OF_EXB + m * 64, 128, 0, 8),
                                            bv(gmtb(cc), CH, 0, 8),
                                            start=False, stop=(m % 2 == 1))
                    tick(h, last, "pe", 1, f"expb_{cc}")

                def pe_x2t(cc, tl):
                    if tl == 0:
                        wait(h, f"x2_{cc}")
                        if cc == NCH - 1:
                            wait(h, f"ev_{2 * cc - 1}")
                    else:
                        wait(h, f"ev_{2 * cc}")
                    last = None
                    for k in range(NCHT):
                        if h:
                            last = h.transpose(
                                PSf[:, B6 + k * 128:B6 + k * 128 + 128],
                                fv(x1buf(cc) + k * CH + tl * 128, 128),
                                fv(OF_IDF, 128))
                    tick(h, last, "pe", 1, f"x2T_{cc}_{tl}")

                pe_x0t(0, 0)
                pe_x0t(0, 1)
                pe_stats3(0)
                pe_bc3(0)
                pe_rtr(0)
                pe_gmt(0, 0)
                pe_gmt(0, 1)
                pe_gmbc(0, 0)
                pe_gmbc(0, 1)
                for c in range(NCH):
                    fe = c + 1 if c + 1 < NCH else None
                    pe_exp(c, 0)
                    if c >= 1:
                        pe_x2t(c - 1, 0)
                    pe_gmbc(c, 2)
                    pe_exp(c, 1)
                    if c >= 1:
                        pe_x2t(c - 1, 1)
                    pe_gmbc(c, 3)
                    pe_exp(c, 2)
                    if fe is not None:
                        pe_x0t(fe, 0)
                    pe_gmbc(c, 4)
                    pe_exp(c, 3)
                    if fe is not None:
                        pe_x0t(fe, 1)
                    pe_gmbc(c, 5)
                    pe_exp(c, 4)
                    if fe is not None:
                        pe_stats3(fe)
                    pe_gmbc(c, 6)
                    pe_exp(c, 5)
                    if fe is not None:
                        pe_bc3(fe)
                    pe_gmbc(c, 7)
                    pe_exp(c, 6)
                    if fe is not None:
                        pe_rtr(fe)
                        pe_gmt(fe, 0)
                        pe_gmt(fe, 1)
                        pe_gmbc(fe, 0)
                        pe_gmbc(fe, 1)
                    pe_exp(c, 7)
                    pe_expb(c)
                pe_x2t(NCH - 1, 0)
                pe_x2t(NCH - 1, 1)

            elif e == "vector":
                def dv_x1tt(cc, tl):
                    if tl == 0:
                        wait(h, f"ldx1_{cc}")
                    wait(h, f"x0T_{cc}_{tl}")
                    dst = f3(x1buf(cc))[:, :, tl * 128:tl * 128 + 128]
                    i = h.tensor_tensor(
                        dst, dst,
                        PSf[:, B6:B6 + 1024].rearrange("p (a b) -> p a b", a=8),
                        ALU.add) if h else None
                    tick(h, i, "dv", 1, f"x1tta_{cc}" if tl == 0 else f"x1tt_{cc}")

                def dv_var3(cc):
                    wait(h, f"stats3_mm_{cc}")
                    i = h.tensor_scalar(fv(OF_ROWS + 512, 256, 0, 1),
                                        PSf[0:1, B6:B6 + 256],
                                        1.0 / D, None, ALU.mult) if h else None
                    fence(h, i, "dv")
                    i = h.tensor_tensor(fv(OF_ROWS + 768, 256, 0, 1),
                                        fv(OF_ROWS + 512, 256, 0, 1),
                                        fv(OF_ROWS + 512, 256, 0, 1), ALU.mult) if h else None
                    fence(h, i, "dv")
                    i = h.scalar_tensor_tensor(
                        fv(OF_ROWS + 768, 256, 0, 1),
                        PSf[0:1, B6 + 256:B6 + 512], 1.0 / D,
                        fv(OF_ROWS + 768, 256, 0, 1), ALU.mult, ALU.subtract) if h else None
                    tick(h, i, "dv", 1, f"var3_{cc}")

                def dv_rows3(cc):
                    wait(h, f"sve3_{cc}")
                    i = h.reciprocal(fv(OF_ROWS, 256, 0, 1),
                                     fv(OF_ROWS + 1024, 256, 0, 1)) if h else None
                    fence(h, i, "dv")
                    i = h.tensor_tensor(fv(OF_ROWS + 256, 256, 0, 1),
                                        fv(OF_ROWS + 512, 256, 0, 1),
                                        fv(OF_ROWS, 256, 0, 1), ALU.mult) if h else None
                    tick(h, i, "dv", 1, f"rows3_{cc}")

                def dv_xt(cc):
                    wait(h, f"bc3_mm_{cc}")
                    i = h.tensor_tensor(f3(PC["xt"]), f3(x1buf(cc)),
                                        PSf[:, B7:B7 + 256].unsqueeze(1)
                                        .broadcast_to([128, 8, 256]), ALU.mult) if h else None
                    fence(h, i, "dv")
                    i = h.tensor_tensor(f3(PC["xt"]), f3(PC["xt"]),
                                        PSf[:, B7 + 256:B7 + 512]
                                        .unsqueeze(1).broadcast_to([128, 8, 256]),
                                        ALU.subtract) if h else None
                    fence(h, i, "dv")
                    sch.ms[f"xt_{cc}"] = ("dv", sch.cnt["dv"])

                def dv_softmax(cc):
                    for tl in range(2):
                        wait(h, f"rtr_{cc}")
                        lg = PSf[:, B6 + tl * 8:B6 + tl * 8 + 8]
                        i = h.reduce_max(fv(OF_SCOL + tl * 16, 1), lg, AX.X) if h else None
                        fence(h, i, "dv")
                        i = h.tensor_scalar(fv(OF_SCOL + tl * 16 + 1, 1),
                                            fv(OF_SCOL + tl * 16, 1), -1.0,
                                            None, ALU.mult) if h else None
                        tick(h, i, "dv", 1, f"lmax_{cc}_{tl}")
                        wait(h, f"sme_{cc}_{tl}")
                        i = h.reciprocal(fv(OF_SCOL + tl * 16 + 3, 1),
                                         fv(OF_SCOL + tl * 16 + 2, 1)) if h else None
                        fence(h, i, "dv")
                        i = h.tensor_scalar(fv(OF_GM + tl * 8, 8), lg,
                                            fv(OF_SCOL + tl * 16, 1), None,
                                            ALU.is_ge) if h else None
                        fence(h, i, "dv")
                        i = h.tensor_scalar(fv(OF_GM + tl * 8, 8),
                                            fv(OF_GM + tl * 8, 8),
                                            fv(OF_SCOL + tl * 16 + 3, 1),
                                            None, ALU.mult) if h else None
                        tick(h, i, "dv", 1, f"gm_{cc}_{tl}")

                def dv_xtbf(cc):
                    i = h.tensor_copy(bv(PC["xtbf"], W), fv(PC["xt"], W)) if h else None
                    tick(h, i, "dv", 1, f"xtbf_{cc}")

                def dv_xg(cc, ei):
                    boff = 2048 if ei % 2 == 0 else 2560
                    wait(h, f"gmbc_{cc}_{ei}")
                    if ei >= 2:
                        wait(h, f"exp_{cc}_{ei - 2}")
                    elif cc >= 1:
                        wait(h, f"exp_{cc - 1}_{ei + 6}")
                    i = h.tensor_tensor(
                        b3(xgs(ei)), b3(PC["xtbf"]),
                        PSf[:, boff:boff + CH].unsqueeze(1)
                        .broadcast_to([128, 8, 256]), ALU.mult) if h else None
                    tick(h, i, "dv", 1, f"xg_{cc}_{ei}")

                def dv_x2(cc):
                    wait(h, f"expb_{cc}")
                    i = h.tensor_tensor(fv(x1buf(cc), W), bank(0, 4),
                                        fv(x1buf(cc), W), ALU.add) if h else None
                    tick(h, i, "dv", 1, f"x2_{cc}")

                dv_x1tt(0, 0)
                dv_x1tt(0, 1)
                dv_var3(0)
                dv_rows3(0)
                dv_xt(0)
                dv_softmax(0)
                dv_xtbf(0)
                dv_xg(0, 0)
                dv_xg(0, 1)
                for c in range(NCH):
                    fe = c + 1 if c + 1 < NCH else None
                    dv_xg(c, 2)
                    dv_xg(c, 3)
                    if fe is not None:
                        dv_x1tt(fe, 0)
                    dv_xg(c, 4)
                    if fe is not None:
                        dv_x1tt(fe, 1)
                    dv_xg(c, 5)
                    if fe is not None:
                        dv_var3(fe)
                        dv_rows3(fe)
                    dv_xg(c, 6)
                    if fe is not None:
                        dv_xt(fe)
                        dv_softmax(fe)
                    dv_xg(c, 7)
                    if fe is not None:
                        dv_xtbf(fe)
                        dv_xg(fe, 0)
                    dv_x2(c)
                    if fe is not None:
                        dv_xg(fe, 1)

            elif e == "scalar":
                def ev(tt):
                    wait(h, f"x2T_{tt // 2}_{tt % 2}")
                    if tt >= 1:
                        wait(h, f"st_{tt - 1}")
                    i = h.activation(fv(PC["y"], D), PSf[:, B6:B6 + 1024],
                                     AF.Copy) if h else None
                    tick(h, i, "ac", 1, f"ev_{tt}")

                for c in range(NCH):
                    if c >= 2:
                        ev(2 * (c - 2))
                        ev(2 * (c - 2) + 1)
                    wait(h, f"x1tt_{c}")
                    if c >= 1:
                        wait(h, f"stats3_mm_{c - 1}")
                    i = h.activation(bv(PC["sqbf"], W), fv(x1buf(c), W),
                                     AF.Square) if h else None
                    tick(h, i, "ac", 1, f"sq3_{c}")
                    wait(h, f"var3_{c}")
                    i = h.activation(fv(OF_ROWS + 1024, 256, 0, 1),
                                     fv(OF_ROWS + 768, 256, 0, 1), AF.Sqrt,
                                     bias=np.float32(EPS)) if h else None
                    tick(h, i, "ac", 1, f"sve3_{c}")
                    for tl in range(2):
                        wait(h, f"lmax_{c}_{tl}")
                        i = h.activation(fv(OF_EXPS + tl * 8, 8),
                                         PSf[:, B6 + tl * 8:B6 + tl * 8 + 8],
                                         AF.Exp, bias=fv(OF_SCOL + tl * 16 + 1, 1),
                                         accum_out=fv(OF_SCOL + tl * 16 + 2, 1)) if h else None
                        tick(h, i, "ac", 1, f"sme_{c}_{tl}")
                    wait(h, f"gmt_{c}_1")
                    if c >= 2:
                        wait(h, f"expb_{c - 2}")
                    i = h.activation(bv(gmtb(c), 256, 0, 8),
                                     PSf[0:8, B6 + 256:B6 + 512],
                                     AF.Copy) if h else None
                    tick(h, i, "ac", 1, f"gmtbf_{c}")
                for tt in range(4 * NCH - 4, 4 * NCH):
                    pass
                for tt in (2 * NCH - 4, 2 * NCH - 3, 2 * NCH - 2, 2 * NCH - 1):
                    ev(tt)

            elif e == "gpsimd":
                pass

        if upto >= 4:
            emit_phase(ph3c)
        if upto < 4:
            def ph_fin(e, h):
                if e != "sync":
                    return
                fins = {0: ["setup_done"],
                        1: ([f"r0e_{2*NCH_RUN-1}", f"xne_{2*NCH_RUN-1}",
                             f"T_{2*NCH_RUN-1}", f"apply_{NCH_RUN-1}",
                             f"spr0_{NCH_RUN-1}"]
                            + ([f"wlin_mm_{NCH_RUN-1}", f"spxg_{NCH_RUN-1}",
                                f"spr0_{NCH_RUN-1}"] if p01s >= 1 else [])
                            + (([f"wconv_mm_{NCH_RUN-1}"]
                                + ([f"pref_{NCH_RUN-1}"] if "pref" not in _SKIP else [])
                                + ([f"prebf_{NCH_RUN-1}"] if "prebf" not in _SKIP else []))
                               if p01s >= 2 else [])
                            + ([f"wa_mm_{NCH_RUN-1}", f"rt_{NCH_RUN-1}"] if p01s >= 3 else [])
                            + ([f"wx_mm_{NCH_RUN-1}", f"it_{NCH_RUN-1}"] if p01s >= 4 else [])
                            + ([f"spa_{NCH_RUN-1}", f"spg_{NCH_RUN-1}",
                                f"ucols_{NCH_RUN-1}"] if p01s >= 5 else [])),
                        2: ["hin_done", "G_done", "w2b"],
                        3: ["spx1_7", "w3_mm_7", "ld_e7"]}
                for u in range(upto + 1):
                    for msn in fins.get(u, []):
                        wait(h, msn)
                dump_off = getattr(sys.modules[__name__], "_DUMP", None)
                src_ap = fv(OF_IDF, 128) if dump_off is None else fv(dump_off, 128)
                d = h.dma_start(out=out[0:128, 0:128], in_=src_ap) if h else None
                v = tick(h, d, "dsta", 16)
                if h:
                    h.wait_ge(SEM["dsta"], v)
            emit_phase(ph_fin, engines=("sync",))

    return nc


def _pack_T(Wm):
    return np.ascontiguousarray(
        Wm.reshape(8, 128, 8, 128).transpose(3, 2, 0, 1).reshape(128, 8192))


def _pack_N(Wm):
    return np.ascontiguousarray(
        Wm.reshape(8, 128, 8, 128).transpose(1, 0, 2, 3).reshape(128, 8192))


def _prep_inputs(inputs):
    def f(k):
        return np.asarray(inputs[k], dtype=np.float32)
    x = np.ascontiguousarray(f("L0_attn_x").reshape(-1, D))
    com = {}
    com["wpk_lin"] = _pack_T(f("L0_attn_W_lin"))
    com["wpk_conv"] = _pack_T(f("L0_attn_W_conv"))
    com["wpk_wa"] = _pack_T(f("L0_attn_Wa"))
    com["wpk_wx"] = _pack_T(f("L0_attn_Wx"))
    com["wpk_lin2"] = _pack_T(f("L0_attn_W_lin2"))
    com["wpk_w1"] = _pack_T(f("L0_attn_W1"))
    com["wpk_w2"] = _pack_T(f("L0_attn_W2"))
    com["wpk_w3"] = _pack_T(f("L0_attn_W3"))
    ew = f("L0_ffn_expert_w")
    for e in range(8):
        com[f"wpk_e{e}"] = _pack_N(ew[e])
    com["bias1"] = np.concatenate(
        [f("L0_attn_b_lin"), f("L0_attn_b_conv"),
         f("L0_attn_ba"), f("L0_attn_bx")]).reshape(1, 4096)
    com["bias2"] = np.concatenate(
        [f("L0_attn_b_lin2"), f("L0_attn_b1"),
         f("L0_attn_b2"), f("L0_attn_b3")]).reshape(1, 4096)
    com["exb"] = np.ascontiguousarray(f("L0_ffn_expert_b"))
    com["rw"] = np.ascontiguousarray(
        f("L0_ffn_router_w").reshape(8, 128, 8).transpose(1, 0, 2).reshape(128, 64))
    com["rb"] = f("L0_ffn_router_b").reshape(1, 8)
    com["lam"] = np.ascontiguousarray(f("L0_attn_Lam").reshape(8, 128).T)
    com["g1c"] = np.ascontiguousarray(f("L0_attn_g1").reshape(8, 128).T)
    com["g2c"] = np.ascontiguousarray(f("L0_attn_g2").reshape(8, 128).T)
    oh = np.zeros((8, D), dtype=np.float32)
    for e in range(8):
        oh[e, e * 128:(e + 1) * 128] = 1.0
    com["oh"] = oh
    com["ident"] = np.eye(128, dtype=np.float32)
    in_maps = []
    for k in range(NCORES):
        m = dict(com)
        m["x"] = np.ascontiguousarray(x[k * TOK:(k + 1) * TOK, :])
        r = k % 4
        sel = np.zeros((128, 8), dtype=np.float32)
        sel[:, :r] = 1.0
        m["sel"] = sel
        m["isel"] = np.ascontiguousarray(1.0 - sel)
        in_maps.append(m)
    return in_maps


def kernel(**inputs):
    in_maps = _prep_inputs(inputs)
    nc = build(debug=False)
    res = run_bass_kernel_spmd(nc, in_maps, core_ids=list(range(NCORES)))
    outs = [res.results[k]["out"] for k in range(NCORES)]
    full = np.concatenate(outs, axis=0).reshape(2, 8192, 1024)
    return np.ascontiguousarray(full.astype(np.float32))


# version 12
# speedup vs baseline: 1.3683x; 1.0113x over previous
import sys
import numpy as np
import concourse.bass as bass
import concourse.mybir as mybir
from concourse.bass_utils import run_bass_kernel_spmd

F32 = mybir.dt.float32
BF16 = mybir.dt.bfloat16
AF = mybir.ActivationFunctionType
ALU = mybir.AluOpType
AX = mybir.AxisListType

NCORES = 8
_SKIP = set()
TOK = 2048
D = 1024
NCHT = 8
CH = 256
NCH = TOK // CH
W = NCHT * CH
EPS = 1e-5
SQD = 32.0

ARENA = 52224

OF_IDF = 0
OF_IDB = 128
OF_ONEF = 192
OF_ONEB = 320
OF_RW = 448
OF_LAM = 512
OF_G1 = 520
OF_G2 = 528
OF_SEL = 536
OF_ISEL = 544
OF_LAC = 552
OF_RTS = 560
OF_TMPC = 568
OF_AUC = 600
OF_G = 616
OF_HIN = 680
OF_CCS = 688
OF_CCL = 816
OF_EXB = 944
OF_OH = 1456
OF_BIAS = 1968
OF_RB = 4016
OF_SS = 4032
OF_SQ = 4048
OF_MC = 4064
OF_VAR = 4080
OF_SVE = 4096
OF_Q1 = 4112
OF_RS = 4128
OF_C1 = 4144
OF_M2T = 4160
OF_ROWS = 4176
OF_GM = 5456
OF_SCOL = 5472
OF_GMT = 5504
OF_HCOL = 5760
OF_EXPS = 5768
OF_PERS_END = 5784

OF_T = [5792 + i * 4096 for i in range(8)]
OF_SCR = OF_T[7] + 4096
assert OF_SCR == 38560
SCR_END = ARENA


def _alloc(start, end, sizes):
    out = {}
    o = start
    for name, n in sizes:
        out[name] = o
        o += n
    assert o <= end, (o, end)
    return out


P1 = _alloc(OF_SCR, SCR_END, [
    ("B2", W), ("B3", W), ("B4", W), ("B5", W),
    ("r0fm", W // 2), ("xgelbf", W // 2),
])
P1.update(_alloc(OF_T[4], OF_T[6], [
    ("x0a", D), ("x0b", D), ("r0ta", D // 2), ("r0tb", D // 2),
    ("xna", D // 2), ("xnb", D // 2), ("sqscr", D),
    ("xnfa", W // 2), ("xnfb", W // 2),
]))
P1["stage"] = OF_T[6]
P1.update(_alloc(OF_T[6], OF_T[7] + 4096, [
    ("B1", W), ("linbf", W // 2), ("prebf", W // 2),
]))

P3 = _alloc(OF_SCR, SCR_END, [
    ("rga", 3 * W), ("rgb", 3 * W),
])
P3.update(_alloc(OF_T[0], OF_T[4], [
    ("h", W), ("mixa", W // 2), ("mixb", W // 2), ("r1", W), ("xmbf", W // 2),
    ("sig", W), ("x1p", W), ("xga", W // 2), ("xgb", W // 2),
    ("s2t", CH), ("r1b", W),
]))

PC = _alloc(OF_SCR, SCR_END, [
    ("x0a", D), ("x0b", D),
    ("x1A", W), ("x1B", W),
    ("xt", W), ("xtbf", W // 2), ("sqbf", W // 2),
    ("xg0", W // 2), ("xg1", W // 2),
    ("gmt2", 256), ("y", D),
])


EPERM = [4, 5, 6, 7, 0, 1, 2, 3]


class Sched:
    def __init__(self):
        self.cnt = {}
        self.ms = {}

    def inc(self, sem, n, ms=None):
        v = self.cnt[sem] = self.cnt.get(sem, 0) + n
        if ms is not None:
            self.ms[ms] = (sem, v)
        return v


def build(debug=False, upto=4, p01n=None, p01s=5):
    nc = bass.Bass("TRN2", target_bir_lowering=False,
                   detect_race_conditions=False)

    x_in = nc.dram_tensor("x", [TOK, D], F32, kind="ExternalInput").ap()
    wname = ["lin", "conv", "wa", "wx", "lin2", "w1", "w2", "w3"] + \
            [f"e{i}" for i in range(8)]
    wpk = {n: nc.dram_tensor("wpk_" + n, [128, 8192], F32, kind="ExternalInput").ap()
           for n in wname}
    bias1 = nc.dram_tensor("bias1", [1, 4096], F32, kind="ExternalInput").ap()
    bias2 = nc.dram_tensor("bias2", [1, 4096], F32, kind="ExternalInput").ap()
    exb_in = nc.dram_tensor("exb", [8, D], F32, kind="ExternalInput").ap()
    rw_in = nc.dram_tensor("rw", [128, 64], F32, kind="ExternalInput").ap()
    rb_in = nc.dram_tensor("rb", [1, 8], F32, kind="ExternalInput").ap()
    lam_in = nc.dram_tensor("lam", [128, 8], F32, kind="ExternalInput").ap()
    g1_in = nc.dram_tensor("g1c", [128, 8], F32, kind="ExternalInput").ap()
    g2_in = nc.dram_tensor("g2c", [128, 8], F32, kind="ExternalInput").ap()
    sel_in = nc.dram_tensor("sel", [128, 8], F32, kind="ExternalInput").ap()
    isel_in = nc.dram_tensor("isel", [128, 8], F32, kind="ExternalInput").ap()
    oh_in = nc.dram_tensor("oh", [8, D], F32, kind="ExternalInput").ap()
    id_in = nc.dram_tensor("ident", [128, 128], F32, kind="ExternalInput").ap()
    out = nc.dram_tensor("out", [TOK, D], F32, kind="ExternalOutput").ap()

    skind = "ExternalOutput" if debug else "Internal"
    scr_r0 = nc.dram_tensor("scr_r0", [128, NCH * W], BF16, kind=skind).ap()
    scr_a = nc.dram_tensor("scr_a", [128, NCH * W], F32, kind=skind).ap()
    scr_g = nc.dram_tensor("scr_g", [128, NCH * W], F32, kind=skind).ap()
    scr_xg = nc.dram_tensor("scr_xg", [128, NCH * W], BF16, kind=skind).ap()
    scr_x1p = nc.dram_tensor("scr_x1p", [128, NCH * W], F32, kind=skind).ap()
    cc_in = nc.dram_tensor("cc_in", [16, 128], F32).ap()
    cc_out = nc.dram_tensor("cc_out", [64, 128], F32).ap()

    arena_h = nc.alloc_sbuf_tensor("arena", [128, ARENA], F32)
    A = arena_h.ap()
    Ab = A.bitcast(BF16)

    def fv(off, n, p0=0, p1=128):
        return A[p0:p1, off:off + n]

    def bv(off, nbf, p0=0, p1=128):
        return Ab[p0:p1, 2 * off:2 * off + nbf]

    def f3(off):
        return fv(off, W).rearrange("p (a b) -> p a b", a=NCHT)

    def b3(off):
        return bv(off, W).rearrange("p (a b) -> p a b", a=NCHT)

    sch = Sched()

    import contextlib
    SEM_NAMES = (["pe", "ac", "dv", "cc",
                  "dxa", "dxb",
                  "dW", "dw2",
                  "dxg", "dr0", "dx1",
                  "dsa", "dsg",
                  "dr3a", "dr3b",
                  "dpa", "dpb",
                  "dsta", "dstb",
                  "dcc"]
                 + [f"de{i}" for i in range(8)])
    with contextlib.ExitStack() as _es:
        PS = _es.enter_context(nc.psum_tensor("PS", [128, 4096], F32))
        SEM = {n: _es.enter_context(nc.semaphore(n)) for n in SEM_NAMES}
        S_cc = SEM["cc"]
        block = _es.enter_context(nc.Block())
        PSf = PS[:]
        PSb = PSf.bitcast(BF16)

        def bank(b, n=1):
            return PSf[:, b * 512:(b + n) * 512]

        def emit_phase(fn, engines=("sync", "gpsimd", "tensor", "scalar", "vector")):
            snap0 = dict(sch.cnt)
            for e in engines:
                fn(e, None)
            snap1 = dict(sch.cnt)
            emitters = {"sync": block.sync, "gpsimd": block.gpsimd,
                        "tensor": block.tensor, "scalar": block.scalar,
                        "vector": block.vector}
            for e in engines:
                sch.cnt = dict(snap0)
                emitters[e](lambda h, e=e: fn(e, h))
            sch.cnt = dict(snap1)

        def wait(h, ms):
            if h is not None:
                sem, v = sch.ms[ms]
                h.wait_ge(SEM[sem], v)

        def tick(h, inst, sem, n, ms=None):
            if h is not None and inst is not None:
                inst.then_inc(SEM[sem], n)
            return sch.inc(sem, n, ms)

        def fence(h, inst, eng):
            v = sch.inc(eng, 1)
            if h is not None and inst is not None:
                inst.then_inc(SEM[eng], 1)
                h.wait_ge(SEM[eng], v)
            return v

        def full_mm(h, slot_off, rhs_off, b0):
            last = None
            r3 = b3(rhs_off)
            for m in range(8):
                for k in range(8):
                    if h:
                        last = h.matmul(
                            PSf[:, b0 * 512 + m * CH:b0 * 512 + m * CH + CH],
                            bv(slot_off + (k * 8 + m) * 64, 128),
                            r3[:, k, :],
                            start=(k == 0 and m % 2 == 0), stop=False)
            return last

        def bias_mm(h, bslot, b0, last):
            for m in range(8):
                if h:
                    last = h.matmul(
                        PSf[:, b0 * 512 + m * CH:b0 * 512 + m * CH + CH],
                        bv(OF_BIAS + bslot * 512 + m * 64, 128, 0, 1),
                        bv(OF_ONEB, CH, 0, 1),
                        start=False, stop=(m % 2 == 1))
            return last

        def ph_setup(e, h):
            if e == "gpsimd":
                def dm(o, i):
                    return h.dma_start(out=o, in_=i) if h else None
                tick(h, dm(fv(OF_IDF, 128), id_in), "dW", 16)
                tick(h, dm(bv(OF_IDB, 128), id_in), "dW", 16)
                tick(h, dm(fv(OF_RW, 64), rw_in), "dW", 16)
                tick(h, dm(fv(OF_RB, 8, 0, 1), rb_in), "dW", 16)
                tick(h, dm(fv(OF_LAM, 8), lam_in), "dW", 16)
                tick(h, dm(fv(OF_G1, 8), g1_in), "dW", 16)
                tick(h, dm(fv(OF_G2, 8), g2_in), "dW", 16)
                tick(h, dm(fv(OF_SEL, 8), sel_in), "dW", 16)
                tick(h, dm(fv(OF_ISEL, 8), isel_in), "dW", 16)
                tick(h, dm(bv(OF_EXB, 1024, 0, 8), exb_in), "dW", 16)
                tick(h, dm(bv(OF_OH, 1024, 0, 8), oh_in), "dW", 16)
                tick(h, dm(bv(OF_BIAS, 4096, 0, 1), bias1), "dW", 16)
                tick(h, dm(fv(P1["stage"], 4096), wpk["lin"][:, 0:4096]), "dW", 16)
                tick(h, dm(fv(P1["stage"] + 4096, 4096), wpk["lin"][:, 4096:8192]),
                     "dW", 16)
                tick(h, dm(bv(OF_T[1], 8192), wpk["conv"]), "dW", 16)
                tick(h, dm(bv(OF_T[2], 8192), wpk["wa"]), "dW", 16)
                tick(h, dm(bv(OF_T[3], 8192), wpk["wx"]), "dW", 16, "setup_done")
            elif e == "scalar":
                wait(h, "setup_done")
                i = h.activation(fv(OF_TMPC, 8), fv(OF_LAM, 8), AF.Sigmoid) if h else None
                fence(h, i, "ac")
                i = h.activation(fv(OF_LAC, 8), fv(OF_TMPC, 8), AF.Ln) if h else None
                tick(h, i, "ac", 1, "lac_ln")
            elif e == "vector":
                i = None
                if h:
                    h.memset(fv(OF_ONEF, 128), 1.0)
                    h.memset(bv(OF_ONEB, 256), 1.0)
                    i = h.memset(fv(OF_RTS, 8), 0.0)
                tick(h, i, "dv", 1)
                wait(h, "setup_done")
                last = None
                for k in range(8):
                    if h:
                        last = h.tensor_scalar(
                            bv(OF_T[0] + k * 512, 1024),
                            fv(P1["stage"] + k * 1024, 1024),
                            fv(OF_G1 + k, 1), None, ALU.mult)
                tick(h, last, "dv", 1, "fold_lin")
                wait(h, "lac_ln")
                i = h.tensor_scalar(fv(OF_LAC, 8), fv(OF_LAC, 8), 0.125,
                                    None, ALU.mult) if h else None
                tick(h, i, "dv", 1, "lac_done")

        emit_phase(ph_setup, engines=("gpsimd", "scalar", "vector"))

        def x0s(tt):
            return P1["x0a"] if tt % 2 == 0 else P1["x0b"]

        def r0ts(tt):
            return P1["r0ta"] if tt % 2 == 0 else P1["r0tb"]

        def xns(tt):
            return P1["xna"] if tt % 2 == 0 else P1["xnb"]

        def xnfm(c):
            return P1["xnfa"] if c % 2 == 0 else P1["xnfb"]

        def ph01(e, h):
            def fe_act1(cc):
                for tl in range(2):
                    tt = 2 * cc + tl
                    wait(h, f"x0_{tt}")
                    if tt >= 1:
                        wait(h, f"ssq_{tt - 1}")
                    i = h.activation(fv(P1["sqscr"], D), fv(x0s(tt), D),
                                     AF.Square) if h else None
                    tick(h, i, "ac", 1, f"sq_{tt}")

            def fe_act2(cc):
                wait(h, f"var_{cc}")
                i = None
                if h:
                    h.activation(fv(OF_SVE + 2 * cc, 2), fv(OF_VAR + 2 * cc, 2),
                                 AF.Sqrt, bias=np.float32(EPS))
                    i = h.activation(fv(OF_Q1 + 2 * cc, 2), fv(OF_VAR + 2 * cc, 2),
                                     AF.Sqrt, scale=np.float32(D))
                tick(h, i, "ac", 1, f"sveq_{cc}")

            def fe_act3(cc):
                for tl in range(2):
                    tt = 2 * cc + tl
                    wait(h, f"T_{tt}")
                    if cc >= 1 and tl == 0:
                        wait(h, f"spr0_{cc - 1}")
                    rb = 4096 if tt % 2 == 0 else 6144
                    xb = 5120 if tt % 2 == 0 else 7168
                    i = h.activation(
                        bv(P1["r0fm"], W).rearrange("p (a b) -> p a b", a=8)
                        [:, :, tl * 128:tl * 128 + 128],
                        PSb[:, rb:rb + 1024].rearrange("p (a b) -> p a b", a=8),
                        AF.Copy) if h else None
                    tick(h, i, "ac", 1, f"r0e_{tt}")
                    i = h.activation(
                        b3(xnfm(cc))[:, :, tl * 128:tl * 128 + 128],
                        PSb[:, xb:xb + 1024].rearrange("p (a b) -> p a b", a=8),
                        AF.Copy) if h else None
                    if tl == 1:
                        fence(h, i, "ac")
                        sch.ms[f"xne_{tt}"] = ("ac", sch.cnt["ac"])
                    else:
                        tick(h, i, "ac", 1, f"xne_{tt}")
                d = h.dma_start(out=scr_r0[:, cc * W:(cc + 1) * W],
                                in_=bv(P1["r0fm"], W)) if h else None
                tick(h, d, "dr0", 16, f"spr0_{cc}")

            def fe_dve1(cc):
                for tl in range(2):
                    tt = 2 * cc + tl
                    wait(h, f"x0_{tt}")
                    i = h.reduce_sum(fv(OF_SS + tt, 1), fv(x0s(tt), D),
                                     AX.X) if h else None
                    tick(h, i, "dv", 1, f"ssum_{tt}")
                    wait(h, f"sq_{tt}")
                    i = h.reduce_sum(fv(OF_SQ + tt, 1), fv(P1["sqscr"], D),
                                     AX.X) if h else None
                    tick(h, i, "dv", 1, f"ssq_{tt}")
                i = h.tensor_scalar(fv(OF_MC + 2 * cc, 2), fv(OF_SS + 2 * cc, 2),
                                    1.0 / D, None, ALU.mult) if h else None
                fence(h, i, "dv")
                i = h.tensor_tensor(fv(OF_M2T + 2 * cc, 2), fv(OF_MC + 2 * cc, 2),
                                    fv(OF_MC + 2 * cc, 2), ALU.mult) if h else None
                fence(h, i, "dv")
                i = h.scalar_tensor_tensor(
                    fv(OF_VAR + 2 * cc, 2), fv(OF_SQ + 2 * cc, 2), 1.0 / D,
                    fv(OF_M2T + 2 * cc, 2), ALU.mult, ALU.subtract) if h else None
                tick(h, i, "dv", 1, f"var_{cc}")

            def fe_dve2(cc):
                wait(h, f"sveq_{cc}")
                i = h.reciprocal(fv(OF_RS + 2 * cc, 2), fv(OF_SVE + 2 * cc, 2)) if h else None
                fence(h, i, "dv")
                i = h.tensor_tensor(fv(OF_M2T + 2 * cc, 2), fv(OF_Q1 + 2 * cc, 2),
                                    fv(OF_RS + 2 * cc, 2), ALU.mult) if h else None
                fence(h, i, "dv")
                i = h.tensor_scalar(fv(OF_M2T + 2 * cc, 2), fv(OF_M2T + 2 * cc, 2),
                                    1e-12, None, ALU.max) if h else None
                fence(h, i, "dv")
                i = h.reciprocal(fv(OF_M2T + 2 * cc, 2), fv(OF_M2T + 2 * cc, 2)) if h else None
                fence(h, i, "dv")
                i = h.scalar_tensor_tensor(
                    fv(OF_C1 + 2 * cc, 2), fv(OF_M2T + 2 * cc, 2), SQD,
                    fv(OF_RS + 2 * cc, 2), ALU.mult, ALU.mult) if h else None
                fence(h, i, "dv")
                for tl in range(2):
                    tt = 2 * cc + tl
                    if tt >= 2:
                        wait(h, f"T_{tt - 2}")
                    i = None
                    if h:
                        h.tensor_scalar(bv(r0ts(tt), 1024), fv(x0s(tt), D),
                                        fv(OF_MC + tt, 1), fv(OF_RS + tt, 1),
                                        ALU.subtract, ALU.mult)
                        i = h.tensor_scalar(bv(xns(tt), 1024), fv(x0s(tt), D),
                                            fv(OF_MC + tt, 1), fv(OF_C1 + tt, 1),
                                            ALU.subtract, ALU.mult)
                    tick(h, i, "dv", 1, f"apply_{cc}" if tl == 1 else None)

            if e == "sync":
                for tt in range(2 * NCH_RUN):
                    if tt >= 2:
                        wait(h, f"apply_{tt // 2 - 1}")
                        wait(h, f"sq_{tt - 2}")
                    d = h.dma_start(out=fv(x0s(tt), D),
                                    in_=x_in[tt * 128:(tt + 1) * 128, :]) if h else None
                    tick(h, d, "dxa" if tt % 2 == 0 else "dxb", 16, f"x0_{tt}")

            elif e == "vector":
                for s in range(-2, NCH_RUN):
                    if s + 2 < NCH_RUN:
                        fe_dve1(s + 2)
                    if s >= 0:
                        wait(h, f"prebf_{s}")
                        if s >= 1:
                            wait(h, f"spg_{s - 1}")
                        i = h.tensor_copy(fv(P1["B1"], W), bank(0, 4)) if h else None
                        tick(h, i, "dv", 1, f"pref_{s}")
                    if s + 2 < NCH_RUN:
                        fe_dve2(s + 2)
                    if s < 0:
                        continue
                    c = s
                    wait(h, f"rt_{c}")
                    last = None
                    for k in range(NCHT):
                        if h:
                            last = h.tensor_scalar(
                                fv(P1["B2"] + k * CH, CH), fv(P1["B2"] + k * CH, CH),
                                fv(OF_LAC + k, 1), None, ALU.mult)
                    fence(h, last, "dv")
                    sch.ms[f"lart_{c}"] = ("dv", sch.cnt["dv"])
                    i = h.tensor_reduce(fv(OF_TMPC, 8), f3(P1["B2"]), AX.X, ALU.add) if h else None
                    fence(h, i, "dv")
                    i = h.tensor_tensor(fv(OF_RTS, 8), fv(OF_RTS, 8),
                                        fv(OF_TMPC, 8), ALU.add) if h else None
                    tick(h, i, "dv", 1, f"lasum_{c}")
                    wait(h, f"it_{c}")
                    wait(h, f"sq1ma_{c}")
                    i = h.tensor_tensor(fv(P1["B1"], W), fv(P1["B3"], W),
                                        fv(P1["B1"], W), ALU.mult) if h else None
                    fence(h, i, "dv")
                    i = h.tensor_tensor(fv(P1["B1"], W), fv(P1["B1"], W),
                                        fv(P1["B5"], W), ALU.mult) if h else None
                    fence(h, i, "dv")
                    sch.ms[f"g_{c}"] = ("dv", sch.cnt["dv"])
                    last = None
                    for k in range(NCHT):
                        init = 0.0 if c == 0 else fv(OF_HCOL + k, 1)
                        if h:
                            last = h.tensor_tensor_scan(
                                fv(P1["B2"] + k * CH, CH), fv(P1["B4"] + k * CH, CH),
                                fv(P1["B1"] + k * CH, CH), init, ALU.mult, ALU.add)
                    fence(h, last, "dv")
                    sch.ms[f"uscan_{c}"] = ("dv", sch.cnt["dv"])
                    i = h.tensor_copy(fv(OF_HCOL, 8), f3(P1["B2"])[:, :, CH - 1]) if h else None
                    tick(h, i, "dv", 1, f"ucols_{c}")

            elif e == "scalar":
                for s in range(-2, NCH_RUN):
                    if s + 2 < NCH_RUN:
                        fe_act1(s + 2)
                    if s >= 0:
                        c = s
                        wait(h, f"wlin_mm_{c}")
                        i = h.activation(bv(P1["linbf"], W), bank(0, 4), AF.Copy) if h else None
                        tick(h, i, "ac", 1, f"lin_{c}")
                        if c >= 1:
                            wait(h, f"spxg_{c - 1}")
                        i = h.activation(bv(P1["xgelbf"], W), bv(P1["linbf"], W),
                                         AF.Gelu) if h else None
                        fence(h, i, "ac")
                        sch.ms[f"xgel_{c}"] = ("ac", sch.cnt["ac"])
                        d = h.dma_start(out=scr_xg[:, c * W:(c + 1) * W],
                                        in_=bv(P1["xgelbf"], W)) if h else None
                        tick(h, d, "dxg", 16, f"spxg_{c}")
                        wait(h, f"wconv_mm_{c}")
                        i = h.activation(bv(P1["prebf"], W), bank(0, 4), AF.Copy) if h else None
                        tick(h, i, "ac", 1, f"prebf_{c}")
                    if s + 2 < NCH_RUN:
                        fe_act2(s + 2)
                        fe_act3(s + 2)
                    if s < 0:
                        continue
                    c = s
                    wait(h, f"wa_mm_{c}")
                    if c >= 1:
                        wait(h, f"ucols_{c - 1}")
                    i = h.activation(fv(P1["B2"], W), bank(0, 4), AF.Sigmoid) if h else None
                    tick(h, i, "ac", 1, f"rt_{c}")
                    wait(h, f"wx_mm_{c}")
                    if c >= 1:
                        wait(h, f"g_{c - 1}")
                    i = h.activation(fv(P1["B3"], W), bank(0, 4), AF.Sigmoid) if h else None
                    tick(h, i, "ac", 1, f"it_{c}")
                    wait(h, f"lart_{c}")
                    if c >= 1:
                        wait(h, f"spa_{c - 1}")
                        wait(h, f"uscan_{c - 1}")
                    i = h.activation(fv(P1["B4"], W), fv(P1["B2"], W), AF.Exp) if h else None
                    fence(h, i, "ac")
                    sch.ms[f"a_{c}"] = ("ac", sch.cnt["ac"])
                    wait(h, f"lasum_{c}")
                    i = h.activation(fv(P1["B2"], W), fv(P1["B4"], W), AF.Square) if h else None
                    fence(h, i, "ac")
                    sch.ms[f"asq_{c}"] = ("ac", sch.cnt["ac"])
                    i = h.activation(fv(P1["B5"], W), fv(P1["B2"], W), AF.Sqrt,
                                     scale=np.float32(-1.0), bias=np.float32(1.0)) if h else None
                    tick(h, i, "ac", 1, f"sq1ma_{c}")

            elif e == "tensor":
                for s in range(-2, NCH_RUN):
                    if s >= 0:
                        c = s
                        wait(h, f"xne_{2 * c + 1}")
                        if c == 0:
                            wait(h, "fold_lin")
                        else:
                            wait(h, f"it_{c - 1}")
                        last = full_mm(h, OF_T[0], xnfm(c), 0)
                        last = bias_mm(h, 0, 0, last)
                        tick(h, last, "pe", 1, f"wlin_mm_{c}")
                        wait(h, f"lin_{c}")
                        last = full_mm(h, OF_T[1], P1["linbf"], 0)
                        last = bias_mm(h, 1, 0, last)
                        tick(h, last, "pe", 1, f"wconv_mm_{c}")
                        wait(h, f"prebf_{c}")
                        wait(h, f"pref_{c}")
                        last = full_mm(h, OF_T[2], P1["prebf"], 0)
                        last = bias_mm(h, 2, 0, last)
                        tick(h, last, "pe", 1, f"wa_mm_{c}")
                    if s + 2 < NCH_RUN:
                        cc = s + 2
                        for tl in range(2):
                            tt = 2 * cc + tl
                            wait(h, f"apply_{cc}")
                            if tt >= 2:
                                wait(h, f"r0e_{tt - 2}")
                                wait(h, f"xne_{tt - 2}")
                            if cc >= 2 and tl == 0:
                                wait(h, f"wlin_mm_{cc - 2}")
                            rb = 4096 if tt % 2 == 0 else 6144
                            xb = 5120 if tt % 2 == 0 else 7168
                            last = None
                            for k in range(NCHT):
                                if h:
                                    h.transpose(PSb[:, rb + k * 128:rb + k * 128 + 128],
                                                bv(r0ts(tt) + k * 64, 128),
                                                bv(OF_IDB, 128))
                                    last = h.transpose(
                                        PSb[:, xb + k * 128:xb + k * 128 + 128],
                                        bv(xns(tt) + k * 64, 128),
                                        bv(OF_IDB, 128))
                            tick(h, last, "pe", 1, f"T_{tt}")
                    if s >= 0:
                        c = s
                        wait(h, f"rt_{c}")
                        last = full_mm(h, OF_T[3], P1["prebf"], 0)
                        last = bias_mm(h, 3, 0, last)
                        tick(h, last, "pe", 1, f"wx_mm_{c}")

            elif e == "gpsimd":
                for c in range(NCH_RUN):
                    wait(h, f"uscan_{c}")
                    d = h.dma_start(out=scr_a[:, c * W:(c + 1) * W],
                                    in_=fv(P1["B4"], W)) if h else None
                    tick(h, d, "dsa", 16, f"spa_{c}")
                    wait(h, f"g_{c}")
                    d = h.dma_start(out=scr_g[:, c * W:(c + 1) * W],
                                    in_=fv(P1["B1"], W)) if h else None
                    tick(h, d, "dsg", 16, f"spg_{c}")

        NCH_RUN = NCH if p01n is None else p01n
        if upto >= 1:
            emit_phase(ph01)

        def ph2(e, h):
            if e == "scalar":
                wait(h, "lasum_7")
                i = h.activation(fv(OF_AUC, 8), fv(OF_RTS, 8), AF.Exp) if h else None
                tick(h, i, "ac", 1, "Acols")
                wait(h, "Ct")
                i = h.activation(fv(OF_CCS, 128, 0, 16),
                                 PSf[0:16, 3072:3072 + 128], AF.Copy) if h else None
                tick(h, i, "ac", 1, "ccs_done")
                wait(h, "Gt")
                i = h.activation(fv(OF_G, 64), PSf[:, 3072:3072 + 64],
                                 AF.Copy) if h else None
                tick(h, i, "ac", 1, "G_done")
            elif e == "vector":
                wait(h, "ucols_7")
                i = h.tensor_copy(fv(OF_AUC + 8, 8), fv(OF_HCOL, 8)) if h else None
                tick(h, i, "dv", 1, "Ucols")
                wait(h, "G_done")
                wait(h, "setup_done")
                i = h.memset(fv(OF_HIN, 8), 0.0) if h else None
                tick(h, i, "dv", 1)
                for j in range(4):
                    gA = fv(OF_G + j * 16, 8)
                    gU = fv(OF_G + j * 16 + 8, 8)
                    i = h.tensor_scalar(fv(OF_TMPC, 8), gA, fv(OF_SEL + j, 1),
                                        fv(OF_ISEL + j, 1), ALU.mult, ALU.add) if h else None
                    fence(h, i, "dv")
                    i = h.tensor_tensor(fv(OF_HIN, 8), fv(OF_TMPC, 8),
                                        fv(OF_HIN, 8), ALU.mult) if h else None
                    fence(h, i, "dv")
                    i = h.tensor_scalar(fv(OF_TMPC, 8), gU, fv(OF_SEL + j, 1),
                                        None, ALU.mult) if h else None
                    fence(h, i, "dv")
                    i = h.tensor_tensor(fv(OF_HIN, 8), fv(OF_HIN, 8),
                                        fv(OF_TMPC, 8), ALU.add) if h else None
                    fence(h, i, "dv")
                tick(h, None, "dv", 0, "hin_done")
                sch.ms["hin_done"] = ("dv", sch.cnt["dv"])
            elif e == "tensor":
                wait(h, "Acols")
                wait(h, "Ucols")
                wait(h, "xne_15")
                i = h.transpose(PSf[0:16, 3072:3072 + 128],
                                fv(OF_AUC, 16), fv(OF_IDF, 128)) if h else None
                tick(h, i, "pe", 1, "Ct")
                wait(h, "ccl_done")
                wait(h, "ccs_done")
                i = h.transpose(PSf[:, 3072:3072 + 64],
                                fv(OF_CCL, 128, 0, 64),
                                fv(OF_IDF, 64, 0, 64)) if h else None
                tick(h, i, "pe", 1, "Gt")
            elif e == "gpsimd":
                wait(h, "ccs_done")
                d = h.dma_start(out=cc_in[:, :], in_=fv(OF_CCS, 128, 0, 16)) if h else None
                v = tick(h, d, "dcc", 16)
                if h:
                    h.wait_ge(SEM["dcc"], v)
                i = h.collective_compute(
                    "AllGather", ALU.bypass,
                    replica_groups=[[0, 1, 2, 3], [4, 5, 6, 7]],
                    ins=[cc_in.opt()],
                    outs=[cc_out.opt()]) if h else None
                v = tick(h, i, "cc", 1)
                if h:
                    h.wait_ge(S_cc, v)
                d = h.dma_start(out=fv(OF_CCL, 128, 0, 64), in_=cc_out[:, :]) if h else None
                tick(h, d, "dcc", 16, "ccl_done")
                wait(h, "T_15")
                wait(h, "wx_mm_7")
                d = h.dma_start(out=bv(OF_T[4], 8192), in_=wpk["lin2"]) if h else None
                tick(h, d, "dw2", 16)
                d = h.dma_start(out=bv(OF_T[5], 8192), in_=wpk["w1"]) if h else None
                tick(h, d, "dw2", 16)
                d = h.dma_start(out=bv(OF_T[6], 8192), in_=wpk["w2"]) if h else None
                tick(h, d, "dw2", 16)
                d = h.dma_start(out=bv(OF_T[7], 8192), in_=wpk["w3"]) if h else None
                tick(h, d, "dw2", 16)
                d = h.dma_start(out=bv(OF_BIAS, 4096, 0, 1), in_=bias2) if h else None
                tick(h, d, "dw2", 16, "w2b")

        if upto >= 2:
            emit_phase(ph2, engines=("scalar", "vector", "tensor", "gpsimd"))

        def rg(c):
            return P3["rga"] if c % 2 == 0 else P3["rgb"]

        def xg3(c):
            return P3["xga"] if c % 2 == 0 else P3["xgb"]

        def mx3(c):
            return P3["mixa"] if c % 2 == 0 else P3["mixb"]

        def ph3ab(e, h):
            if e == "sync":
                for c in range(NCH):
                    base = rg(c)
                    wait(h, "spa_7")
                    wait(h, "spg_7")
                    wait(h, "spr0_7")
                    wait(h, "spxg_7")
                    if c >= 2:
                        wait(h, f"r1stt_{c - 2}")
                        wait(h, f"mix_{c - 2}")
                    d = h.dma_start(out=fv(base, W),
                                    in_=scr_a[:, c * W:(c + 1) * W]) if h else None
                    tick(h, d, "dr3a" if c % 2 == 0 else "dr3b", 16)
                    d = h.dma_start(out=fv(base + W, W),
                                    in_=scr_g[:, c * W:(c + 1) * W]) if h else None
                    tick(h, d, "dr3a" if c % 2 == 0 else "dr3b", 16)
                    d = h.dma_start(out=bv(base + 2 * W, W),
                                    in_=scr_r0[:, c * W:(c + 1) * W]) if h else None
                    tick(h, d, "dr3a" if c % 2 == 0 else "dr3b", 16)
                    d = h.dma_start(out=bv(xg3(c), W),
                                    in_=scr_xg[:, c * W:(c + 1) * W]) if h else None
                    tick(h, d, "dr3a" if c % 2 == 0 else "dr3b", 16, f"ld3_{c}")

            elif e == "vector":
                def r1b(c):
                    return P3["r1"] if c % 2 == 0 else P3["r1b"]

                def dv_hmix(cc):
                    wait(h, f"ld3_{cc}")
                    if cc == 0:
                        wait(h, "hin_done")
                    last = None
                    for k in range(NCHT):
                        init = fv(OF_HIN + k, 1) if cc == 0 else fv(OF_HCOL + k, 1)
                        if h:
                            last = h.tensor_tensor_scan(
                                fv(P3["h"] + k * CH, CH), fv(rg(cc) + k * CH, CH),
                                fv(rg(cc) + W + k * CH, CH), init, ALU.mult, ALU.add)
                    fence(h, last, "dv")
                    sch.ms[f"hscan_{cc}"] = ("dv", sch.cnt["dv"])
                    i = h.tensor_copy(fv(OF_HCOL, 8), f3(P3["h"])[:, :, CH - 1]) if h else None
                    fence(h, i, "dv")
                    sch.ms[f"hcols_{cc}"] = ("dv", sch.cnt["dv"])
                    if cc >= 2:
                        wait(h, f"stats2_mm_{cc - 2}")
                    i = h.tensor_tensor(bv(mx3(cc), W), bv(xg3(cc), W),
                                        fv(P3["h"], W), ALU.mult) if h else None
                    tick(h, i, "dv", 1, f"mix_{cc}")

                def dv_tail(cc):
                    wait(h, f"gelu_{cc}")
                    wait(h, f"w2_mm_{cc}")
                    i = h.tensor_tensor(bv(P3["xmbf"], W), fv(P3["sig"], W),
                                        bank(0, 4), ALU.mult) if h else None
                    tick(h, i, "dv", 1, f"comb_{cc}")
                    wait(h, f"w3_mm_{cc}")
                    if cc >= 1:
                        wait(h, f"spx1_{cc - 1}")
                    i = h.tensor_tensor(fv(P3["x1p"], W), bank(0, 4),
                                        fv(r1b(cc), W), ALU.add) if h else None
                    tick(h, i, "dv", 1, f"x1p_{cc}")

                dv_hmix(0)
                for c in range(NCH):
                    wait(h, f"wlin2_mm_{c}")
                    i = h.scalar_tensor_tensor(fv(r1b(c), W), PSf[:, 2048:4096],
                                               1.0, bv(rg(c) + 2 * W, W), ALU.mult,
                                               ALU.add) if h else None
                    tick(h, i, "dv", 1, f"r1stt_{c}")
                    if c + 1 < NCH:
                        dv_hmix(c + 1)
                    if c >= 1:
                        dv_tail(c - 1)
                    wait(h, f"sveq2_{c}")
                    i = h.tensor_scalar(fv(P3["s2t"], CH), fv(P3["s2t"], CH),
                                        1e-12 / SQD, None, ALU.max) if h else None
                    fence(h, i, "dv")
                    i = h.reciprocal(fv(P3["s2t"], CH), fv(P3["s2t"], CH)) if h else None
                    fence(h, i, "dv")
                    sch.ms[f"s2row_{c}"] = ("dv", sch.cnt["dv"])
                    last = None
                    for k in range(NCHT):
                        if h:
                            last = h.scalar_tensor_tensor(
                                bv(P3["xmbf"] + k * 128, 256),
                                fv(r1b(c) + k * CH, CH), fv(OF_G2 + k, 1),
                                fv(P3["s2t"], CH), ALU.mult, ALU.mult)
                    tick(h, last, "dv", 1, f"xm_{c}")
                dv_tail(NCH - 1)

            elif e == "tensor":
                def emit_wlin2(c2):
                    wait(h, f"mix_{c2}")
                    if c2 == 0:
                        wait(h, "w2b")
                    if c2 >= 1:
                        wait(h, f"r1stt_{c2 - 1}")
                    last = full_mm(h, OF_T[4], mx3(c2), 4)
                    last = bias_mm(h, 0, 4, last)
                    tick(h, last, "pe", 1, f"wlin2_mm_{c2}")

                emit_wlin2(0)
                for c in range(1, NCH + 1):
                    cc = c - 1
                    wait(h, f"sq2_{cc}")
                    if cc >= 1:
                        wait(h, f"x1p_{cc - 1}")
                        wait(h, f"sveq2_{cc - 1}")
                    last = None
                    for k in range(NCHT):
                        if h:
                            last = h.matmul(
                                PSf[:, 0:256],
                                bv(OF_ONEB, 128),
                                bv(mx3(cc) + k * 128, 256),
                                start=(k == 0), stop=(k == 7))
                    tick(h, last, "pe", 1, f"stats2_mm_{cc}")
                    if cc + 1 < NCH:
                        emit_wlin2(cc + 1)
                    wait(h, f"xm_{cc}")
                    last = full_mm(h, OF_T[5], P3["xmbf"], 0)
                    last = bias_mm(h, 1, 0, last)
                    tick(h, last, "pe", 1, f"w1_mm_{cc}")
                    wait(h, f"sig_{cc}")
                    last = full_mm(h, OF_T[6], P3["xmbf"], 0)
                    last = bias_mm(h, 2, 0, last)
                    tick(h, last, "pe", 1, f"w2_mm_{cc}")
                    wait(h, f"comb_{cc}")
                    last = full_mm(h, OF_T[7], P3["xmbf"], 0)
                    last = bias_mm(h, 3, 0, last)
                    tick(h, last, "pe", 1, f"w3_mm_{cc}")

            elif e == "scalar":
                for c in range(NCH):
                    wait(h, f"r1stt_{c}")
                    if c >= 1:
                        wait(h, f"stats2_mm_{c - 1}")
                    i = h.activation(bv(mx3(c), W),
                                     fv(P3["r1"] if c % 2 == 0 else P3["r1b"], W),
                                     AF.Square) if h else None
                    tick(h, i, "ac", 1, f"sq2_{c}")
                    wait(h, f"stats2_mm_{c}")
                    if c >= 1:
                        wait(h, f"xm_{c - 1}")
                    i = h.activation(fv(P3["s2t"], CH), PSf[:, 0:256],
                                     AF.Sqrt, scale=np.float32(1.0 / D)) if h else None
                    tick(h, i, "ac", 1, f"sveq2_{c}")
                    wait(h, f"w1_mm_{c}")
                    if c >= 1:
                        wait(h, f"comb_{c - 1}")
                    i = h.activation(fv(P3["sig"], W), bank(0, 4), AF.Sigmoid) if h else None
                    fence(h, i, "ac")
                    sch.ms[f"sig_{c}"] = ("ac", sch.cnt["ac"])
                    i = h.activation(fv(P3["sig"], W), fv(P3["sig"], W),
                                     AF.Gelu) if h else None
                    tick(h, i, "ac", 1, f"gelu_{c}")
                    wait(h, f"x1p_{c}")
                    d = h.dma_start(out=scr_x1p[:, c * W:(c + 1) * W],
                                    in_=fv(P3["x1p"], W)) if h else None
                    tick(h, d, "dx1", 16, f"spx1_{c}")

            elif e == "gpsimd":
                slot_free = {4: ["wlin2_mm_7"], 5: ["w1_mm_7"], 6: ["w2_mm_7"],
                             7: ["w3_mm_7"], 0: ["stats2_mm_7"],
                             1: ["comb_7", "x1p_6"],
                             2: ["spx1_7", "mix_7"],
                             3: ["xm_7", "x1p_7"]}
                for ei in range(8):
                    for msn in slot_free[EPERM[ei]]:
                        wait(h, msn)
                    d = h.dma_start(out=bv(OF_T[EPERM[ei]], 8192),
                                    in_=wpk[f"e{ei}"]) if h else None
                    tick(h, d, f"de{ei}", 16, f"ld_e{ei}")

        if upto >= 3:
            emit_phase(ph3ab)

        def x1buf(c):
            return PC["x1A"] if c % 2 == 0 else PC["x1B"]

        def x0cs(tt):
            return PC["x0a"] if tt % 2 == 0 else PC["x0b"]

        def xgs(ei):
            return PC["xg0"] if ei % 2 == 0 else PC["xg1"]

        def gmtb(c):
            return OF_GMT if c % 2 == 0 else PC["gmt2"]

        def ph3c(e, h):
            B6 = 3072
            B7 = 3584

            if e == "sync":
                def ldx1(cc):
                    if cc <= 1:
                        wait(h, "spx1_7")
                    else:
                        wait(h, f"x2T_{cc - 2}_1")
                    d = h.dma_start(out=fv(x1buf(cc), W),
                                    in_=scr_x1p[:, cc * W:(cc + 1) * W]) if h else None
                    tick(h, d, "dpa" if cc % 2 == 0 else "dpb", 16, f"ldx1_{cc}")

                def ldx0(tt):
                    d = h.dma_start(out=fv(x0cs(tt), D),
                                    in_=x_in[tt * 128:(tt + 1) * 128, :]) if h else None
                    tick(h, d, "dxa" if tt % 2 == 0 else "dxb", 16, f"ldx0_{tt}")

                def st(tt):
                    wait(h, f"ev_{tt}")
                    d = h.dma_start(out=out[tt * 128:(tt + 1) * 128, :],
                                    in_=fv(PC["y"], D)) if h else None
                    tick(h, d, "dsta" if tt % 2 == 0 else "dstb", 16, f"st_{tt}")

                ldx1(0)
                ldx0(0)
                ldx0(1)
                wait(h, "x0T_0_0")
                ldx1(1)
                ldx0(2)
                wait(h, "x0T_0_1")
                ldx0(3)
                for c in range(1, NCH):
                    st(2 * c - 2)
                    st(2 * c - 1)
                    if c + 1 < NCH:
                        ldx1(c + 1)
                    if 2 * c + 2 < 2 * NCH:
                        wait(h, f"x0T_{c}_0")
                        ldx0(2 * c + 2)
                        wait(h, f"x0T_{c}_1")
                        ldx0(2 * c + 3)
                st(2 * NCH - 2)
                st(2 * NCH - 1)
                for msn in (f"st_{2 * NCH - 2}", f"st_{2 * NCH - 1}"):
                    sem, v = sch.ms[msn]
                    if h:
                        h.wait_ge(SEM[sem], v)

            elif e == "tensor":
                def pe_x0t(cc, tl):
                    tt = 2 * cc + tl
                    wait(h, f"ldx0_{tt}")
                    if tl == 0:
                        if cc == 0:
                            wait(h, "r1stt_7")
                        elif cc >= 2:
                            wait(h, f"ev_{2 * cc - 3}")
                    else:
                        wait(h, f"x1tta_{cc}")
                    last = None
                    for k in range(NCHT):
                        if h:
                            last = h.transpose(
                                PSf[:, B6 + k * 128:B6 + k * 128 + 128],
                                fv(x0cs(tt) + k * 128, 128),
                                fv(OF_IDF, 128))
                    tick(h, last, "pe", 1, f"x0T_{cc}_{tl}")

                def pe_stats3(cc):
                    wait(h, f"x1tt_{cc}")
                    wait(h, f"sq3_{cc}")
                    last = None
                    for k in range(NCHT):
                        if h:
                            last = h.matmul(PSf[0:1, B6:B6 + 256],
                                            fv(OF_ONEF, 1),
                                            fv(x1buf(cc) + k * CH, CH),
                                            start=(k == 0), stop=(k == 7))
                    for k in range(NCHT):
                        if h:
                            last = h.matmul(PSf[0:1, B6 + 256:B6 + 512],
                                            bv(OF_ONEB, 1),
                                            bv(PC["sqbf"] + k * 128, 256),
                                            start=(k == 0), stop=(k == 7))
                    tick(h, last, "pe", 1, f"stats3_mm_{cc}")

                def pe_bc3(cc):
                    wait(h, f"rows3_{cc}")
                    i = None
                    if h:
                        h.matmul(PSf[:, B7:B7 + 256],
                                 fv(OF_ONEF, 128, 0, 1),
                                 fv(OF_ROWS, 256, 0, 1), start=True, stop=True)
                        i = h.matmul(PSf[:, B7 + 256:B7 + 512],
                                     fv(OF_ONEF, 128, 0, 1),
                                     fv(OF_ROWS + 256, 256, 0, 1),
                                     start=True, stop=True)
                    tick(h, i, "pe", 1, f"bc3_mm_{cc}")

                def pe_rtr(cc):
                    wait(h, f"xt_{cc}")
                    last = None
                    for tl in range(2):
                        for k in range(NCHT):
                            if h:
                                last = h.matmul(
                                    PSf[:, B6 + tl * 8:B6 + tl * 8 + 8],
                                    fv(PC["xt"] + k * CH + tl * 128, 128),
                                    fv(OF_RW + k * 8, 8),
                                    start=(k == 0), stop=False)
                        if h:
                            last = h.matmul(
                                PSf[:, B6 + tl * 8:B6 + tl * 8 + 8],
                                fv(OF_ONEF, 128, 0, 1), fv(OF_RB, 8, 0, 1),
                                start=False, stop=True)
                    tick(h, last, "pe", 1, f"rtr_{cc}")

                def pe_gmt(cc, tl):
                    wait(h, f"gm_{cc}_{tl}")
                    i = h.transpose(
                        PSf[0:8, B6 + 256 + tl * 128:B6 + 256 + tl * 128 + 128],
                        fv(OF_GM + tl * 8, 8), fv(OF_IDF, 128)) if h else None
                    tick(h, i, "pe", 1, f"gmt_{cc}_{tl}")

                def pe_gmbc(cc, ei):
                    boff = 2048 if ei % 2 == 0 else 2560
                    wait(h, f"gmtbf_{cc}")
                    if ei >= 2:
                        wait(h, f"xg_{cc}_{ei - 2}")
                    elif cc >= 1:
                        wait(h, f"xg_{cc - 1}_{ei + 6}")
                    i = h.matmul(PSf[:, boff:boff + CH],
                                 bv(OF_OH + ei * 64, 128, 0, 8),
                                 bv(gmtb(cc), CH, 0, 8),
                                 start=True, stop=True) if h else None
                    tick(h, i, "pe", 1, f"gmbc_{cc}_{ei}")

                def pe_exp(cc, ei):
                    wait(h, f"xg_{cc}_{ei}")
                    wait(h, f"ld_e{ei}")
                    if ei == 0:
                        if cc >= 1:
                            wait(h, f"x2_{cc - 1}")
                        else:
                            wait(h, "x1p_7")
                    last = None
                    for m in range(8):
                        for k in range(8):
                            if h:
                                last = h.matmul(
                                    PSf[:, m * CH:m * CH + CH],
                                    bv(OF_T[EPERM[ei]] + (k * 8 + m) * 64, 128),
                                    b3(xgs(ei))[:, k, :],
                                    start=(ei == 0 and k == 0 and m % 2 == 0),
                                    stop=False)
                    tick(h, last, "pe", 1, f"exp_{cc}_{ei}")

                def pe_expb(cc):
                    last = None
                    for m in range(8):
                        if h:
                            last = h.matmul(PSf[:, m * CH:m * CH + CH],
                                            bv(OF_EXB + m * 64, 128, 0, 8),
                                            bv(gmtb(cc), CH, 0, 8),
                                            start=False, stop=(m % 2 == 1))
                    tick(h, last, "pe", 1, f"expb_{cc}")

                def pe_x2t(cc, tl):
                    if tl == 0:
                        wait(h, f"x2_{cc}")
                        if cc == NCH - 1:
                            wait(h, f"ev_{2 * cc - 1}")
                    else:
                        wait(h, f"ev_{2 * cc}")
                    last = None
                    for k in range(NCHT):
                        if h:
                            last = h.transpose(
                                PSf[:, B6 + k * 128:B6 + k * 128 + 128],
                                fv(x1buf(cc) + k * CH + tl * 128, 128),
                                fv(OF_IDF, 128))
                    tick(h, last, "pe", 1, f"x2T_{cc}_{tl}")

                pe_x0t(0, 0)
                pe_x0t(0, 1)
                pe_stats3(0)
                pe_bc3(0)
                pe_rtr(0)
                pe_gmt(0, 0)
                pe_gmt(0, 1)
                pe_gmbc(0, 0)
                pe_gmbc(0, 1)
                for c in range(NCH):
                    fe = c + 1 if c + 1 < NCH else None
                    pe_exp(c, 0)
                    if c >= 1:
                        pe_x2t(c - 1, 0)
                    pe_gmbc(c, 2)
                    pe_exp(c, 1)
                    if c >= 1:
                        pe_x2t(c - 1, 1)
                    pe_gmbc(c, 3)
                    pe_exp(c, 2)
                    if fe is not None:
                        pe_x0t(fe, 0)
                    pe_gmbc(c, 4)
                    pe_exp(c, 3)
                    if fe is not None:
                        pe_x0t(fe, 1)
                    pe_gmbc(c, 5)
                    pe_exp(c, 4)
                    if fe is not None:
                        pe_stats3(fe)
                    pe_gmbc(c, 6)
                    pe_exp(c, 5)
                    if fe is not None:
                        pe_bc3(fe)
                    pe_gmbc(c, 7)
                    pe_exp(c, 6)
                    if fe is not None:
                        pe_rtr(fe)
                    pe_exp(c, 7)
                    if fe is not None:
                        pe_gmt(fe, 0)
                        pe_gmt(fe, 1)
                        pe_gmbc(fe, 0)
                        pe_gmbc(fe, 1)
                    pe_expb(c)
                pe_x2t(NCH - 1, 0)
                pe_x2t(NCH - 1, 1)

            elif e == "vector":
                def dv_x1tt(cc, tl):
                    if tl == 0:
                        wait(h, f"ldx1_{cc}")
                    wait(h, f"x0T_{cc}_{tl}")
                    dst = f3(x1buf(cc))[:, :, tl * 128:tl * 128 + 128]
                    i = h.tensor_tensor(
                        dst, dst,
                        PSf[:, B6:B6 + 1024].rearrange("p (a b) -> p a b", a=8),
                        ALU.add) if h else None
                    tick(h, i, "dv", 1, f"x1tta_{cc}" if tl == 0 else f"x1tt_{cc}")

                def dv_var3(cc):
                    wait(h, f"stats3_mm_{cc}")
                    i = h.tensor_scalar(fv(OF_ROWS + 512, 256, 0, 1),
                                        PSf[0:1, B6:B6 + 256],
                                        1.0 / D, None, ALU.mult) if h else None
                    fence(h, i, "dv")
                    i = h.tensor_tensor(fv(OF_ROWS + 768, 256, 0, 1),
                                        fv(OF_ROWS + 512, 256, 0, 1),
                                        fv(OF_ROWS + 512, 256, 0, 1), ALU.mult) if h else None
                    fence(h, i, "dv")
                    i = h.scalar_tensor_tensor(
                        fv(OF_ROWS + 768, 256, 0, 1),
                        PSf[0:1, B6 + 256:B6 + 512], 1.0 / D,
                        fv(OF_ROWS + 768, 256, 0, 1), ALU.mult, ALU.subtract) if h else None
                    tick(h, i, "dv", 1, f"var3_{cc}")

                def dv_rows3(cc):
                    wait(h, f"sve3_{cc}")
                    i = h.reciprocal(fv(OF_ROWS, 256, 0, 1),
                                     fv(OF_ROWS + 1024, 256, 0, 1)) if h else None
                    fence(h, i, "dv")
                    i = h.tensor_tensor(fv(OF_ROWS + 256, 256, 0, 1),
                                        fv(OF_ROWS + 512, 256, 0, 1),
                                        fv(OF_ROWS, 256, 0, 1), ALU.mult) if h else None
                    tick(h, i, "dv", 1, f"rows3_{cc}")

                def dv_xt(cc):
                    wait(h, f"bc3_mm_{cc}")
                    i = h.tensor_tensor(f3(PC["xt"]), f3(x1buf(cc)),
                                        PSf[:, B7:B7 + 256].unsqueeze(1)
                                        .broadcast_to([128, 8, 256]), ALU.mult) if h else None
                    fence(h, i, "dv")
                    i = h.tensor_tensor(f3(PC["xt"]), f3(PC["xt"]),
                                        PSf[:, B7 + 256:B7 + 512]
                                        .unsqueeze(1).broadcast_to([128, 8, 256]),
                                        ALU.subtract) if h else None
                    fence(h, i, "dv")
                    sch.ms[f"xt_{cc}"] = ("dv", sch.cnt["dv"])

                def dv_softmax(cc):
                    for tl in range(2):
                        wait(h, f"rtr_{cc}")
                        lg = PSf[:, B6 + tl * 8:B6 + tl * 8 + 8]
                        i = h.reduce_max(fv(OF_SCOL + tl * 16, 1), lg, AX.X) if h else None
                        fence(h, i, "dv")
                        i = h.tensor_scalar(fv(OF_SCOL + tl * 16 + 1, 1),
                                            fv(OF_SCOL + tl * 16, 1), -1.0,
                                            None, ALU.mult) if h else None
                        tick(h, i, "dv", 1, f"lmax_{cc}_{tl}")
                        wait(h, f"sme_{cc}_{tl}")
                        i = h.reciprocal(fv(OF_SCOL + tl * 16 + 3, 1),
                                         fv(OF_SCOL + tl * 16 + 2, 1)) if h else None
                        fence(h, i, "dv")
                        i = h.tensor_scalar(fv(OF_GM + tl * 8, 8), lg,
                                            fv(OF_SCOL + tl * 16, 1), None,
                                            ALU.is_ge) if h else None
                        fence(h, i, "dv")
                        i = h.tensor_scalar(fv(OF_GM + tl * 8, 8),
                                            fv(OF_GM + tl * 8, 8),
                                            fv(OF_SCOL + tl * 16 + 3, 1),
                                            None, ALU.mult) if h else None
                        tick(h, i, "dv", 1, f"gm_{cc}_{tl}")

                def dv_xtbf(cc):
                    i = h.tensor_copy(bv(PC["xtbf"], W), fv(PC["xt"], W)) if h else None
                    tick(h, i, "dv", 1, f"xtbf_{cc}")

                def dv_xg(cc, ei):
                    boff = 2048 if ei % 2 == 0 else 2560
                    wait(h, f"gmbc_{cc}_{ei}")
                    if ei >= 2:
                        wait(h, f"exp_{cc}_{ei - 2}")
                    elif cc >= 1:
                        wait(h, f"exp_{cc - 1}_{ei + 6}")
                    i = h.tensor_tensor(
                        b3(xgs(ei)), b3(PC["xtbf"]),
                        PSf[:, boff:boff + CH].unsqueeze(1)
                        .broadcast_to([128, 8, 256]), ALU.mult) if h else None
                    tick(h, i, "dv", 1, f"xg_{cc}_{ei}")

                def dv_x2(cc):
                    wait(h, f"expb_{cc}")
                    i = h.tensor_tensor(fv(x1buf(cc), W), bank(0, 4),
                                        fv(x1buf(cc), W), ALU.add) if h else None
                    tick(h, i, "dv", 1, f"x2_{cc}")

                dv_x1tt(0, 0)
                dv_x1tt(0, 1)
                dv_var3(0)
                dv_rows3(0)
                dv_xt(0)
                dv_softmax(0)
                dv_xtbf(0)
                dv_xg(0, 0)
                dv_xg(0, 1)
                for c in range(NCH):
                    fe = c + 1 if c + 1 < NCH else None
                    dv_xg(c, 2)
                    dv_xg(c, 3)
                    if fe is not None:
                        dv_x1tt(fe, 0)
                    dv_xg(c, 4)
                    if fe is not None:
                        dv_x1tt(fe, 1)
                    dv_xg(c, 5)
                    if fe is not None:
                        dv_var3(fe)
                        dv_rows3(fe)
                    dv_xg(c, 6)
                    if fe is not None:
                        dv_xt(fe)
                    dv_xg(c, 7)
                    if fe is not None:
                        dv_softmax(fe)
                        dv_xtbf(fe)
                        dv_xg(fe, 0)
                    dv_x2(c)
                    if fe is not None:
                        dv_xg(fe, 1)

            elif e == "scalar":
                def ev(tt):
                    wait(h, f"x2T_{tt // 2}_{tt % 2}")
                    if tt >= 1:
                        wait(h, f"st_{tt - 1}")
                    i = h.activation(fv(PC["y"], D), PSf[:, B6:B6 + 1024],
                                     AF.Copy) if h else None
                    tick(h, i, "ac", 1, f"ev_{tt}")

                for c in range(NCH):
                    if c >= 2:
                        ev(2 * (c - 2))
                        ev(2 * (c - 2) + 1)
                    wait(h, f"x1tt_{c}")
                    if c >= 1:
                        wait(h, f"stats3_mm_{c - 1}")
                    i = h.activation(bv(PC["sqbf"], W), fv(x1buf(c), W),
                                     AF.Square) if h else None
                    tick(h, i, "ac", 1, f"sq3_{c}")
                    wait(h, f"var3_{c}")
                    i = h.activation(fv(OF_ROWS + 1024, 256, 0, 1),
                                     fv(OF_ROWS + 768, 256, 0, 1), AF.Sqrt,
                                     bias=np.float32(EPS)) if h else None
                    tick(h, i, "ac", 1, f"sve3_{c}")
                    for tl in range(2):
                        wait(h, f"lmax_{c}_{tl}")
                        i = h.activation(fv(OF_EXPS + tl * 8, 8),
                                         PSf[:, B6 + tl * 8:B6 + tl * 8 + 8],
                                         AF.Exp, bias=fv(OF_SCOL + tl * 16 + 1, 1),
                                         accum_out=fv(OF_SCOL + tl * 16 + 2, 1)) if h else None
                        tick(h, i, "ac", 1, f"sme_{c}_{tl}")
                    wait(h, f"gmt_{c}_1")
                    if c >= 2:
                        wait(h, f"expb_{c - 2}")
                    i = h.activation(bv(gmtb(c), 256, 0, 8),
                                     PSf[0:8, B6 + 256:B6 + 512],
                                     AF.Copy) if h else None
                    tick(h, i, "ac", 1, f"gmtbf_{c}")
                for tt in range(4 * NCH - 4, 4 * NCH):
                    pass
                for tt in (2 * NCH - 4, 2 * NCH - 3, 2 * NCH - 2, 2 * NCH - 1):
                    ev(tt)

            elif e == "gpsimd":
                pass

        if upto >= 4:
            emit_phase(ph3c)
        if upto < 4:
            def ph_fin(e, h):
                if e != "sync":
                    return
                fins = {0: ["setup_done"],
                        1: ([f"r0e_{2*NCH_RUN-1}", f"xne_{2*NCH_RUN-1}",
                             f"T_{2*NCH_RUN-1}", f"apply_{NCH_RUN-1}",
                             f"spr0_{NCH_RUN-1}"]
                            + ([f"wlin_mm_{NCH_RUN-1}", f"spxg_{NCH_RUN-1}",
                                f"spr0_{NCH_RUN-1}"] if p01s >= 1 else [])
                            + (([f"wconv_mm_{NCH_RUN-1}"]
                                + ([f"pref_{NCH_RUN-1}"] if "pref" not in _SKIP else [])
                                + ([f"prebf_{NCH_RUN-1}"] if "prebf" not in _SKIP else []))
                               if p01s >= 2 else [])
                            + ([f"wa_mm_{NCH_RUN-1}", f"rt_{NCH_RUN-1}"] if p01s >= 3 else [])
                            + ([f"wx_mm_{NCH_RUN-1}", f"it_{NCH_RUN-1}"] if p01s >= 4 else [])
                            + ([f"spa_{NCH_RUN-1}", f"spg_{NCH_RUN-1}",
                                f"ucols_{NCH_RUN-1}"] if p01s >= 5 else [])),
                        2: ["hin_done", "G_done", "w2b"],
                        3: ["spx1_7", "w3_mm_7", "ld_e7"]}
                for u in range(upto + 1):
                    for msn in fins.get(u, []):
                        wait(h, msn)
                dump_off = getattr(sys.modules[__name__], "_DUMP", None)
                src_ap = fv(OF_IDF, 128) if dump_off is None else fv(dump_off, 128)
                d = h.dma_start(out=out[0:128, 0:128], in_=src_ap) if h else None
                v = tick(h, d, "dsta", 16)
                if h:
                    h.wait_ge(SEM["dsta"], v)
            emit_phase(ph_fin, engines=("sync",))

    return nc


def _pack_T(Wm):
    return np.ascontiguousarray(
        Wm.reshape(8, 128, 8, 128).transpose(3, 2, 0, 1).reshape(128, 8192))


def _pack_N(Wm):
    return np.ascontiguousarray(
        Wm.reshape(8, 128, 8, 128).transpose(1, 0, 2, 3).reshape(128, 8192))


def _prep_inputs(inputs):
    def f(k):
        return np.asarray(inputs[k], dtype=np.float32)
    x = np.ascontiguousarray(f("L0_attn_x").reshape(-1, D))
    com = {}
    com["wpk_lin"] = _pack_T(f("L0_attn_W_lin"))
    com["wpk_conv"] = _pack_T(f("L0_attn_W_conv"))
    com["wpk_wa"] = _pack_T(f("L0_attn_Wa"))
    com["wpk_wx"] = _pack_T(f("L0_attn_Wx"))
    com["wpk_lin2"] = _pack_T(f("L0_attn_W_lin2"))
    com["wpk_w1"] = _pack_T(f("L0_attn_W1"))
    com["wpk_w2"] = _pack_T(f("L0_attn_W2"))
    com["wpk_w3"] = _pack_T(f("L0_attn_W3"))
    ew = f("L0_ffn_expert_w")
    for e in range(8):
        com[f"wpk_e{e}"] = _pack_N(ew[e])
    com["bias1"] = np.concatenate(
        [f("L0_attn_b_lin"), f("L0_attn_b_conv"),
         f("L0_attn_ba"), f("L0_attn_bx")]).reshape(1, 4096)
    com["bias2"] = np.concatenate(
        [f("L0_attn_b_lin2"), f("L0_attn_b1"),
         f("L0_attn_b2"), f("L0_attn_b3")]).reshape(1, 4096)
    com["exb"] = np.ascontiguousarray(f("L0_ffn_expert_b"))
    com["rw"] = np.ascontiguousarray(
        f("L0_ffn_router_w").reshape(8, 128, 8).transpose(1, 0, 2).reshape(128, 64))
    com["rb"] = f("L0_ffn_router_b").reshape(1, 8)
    com["lam"] = np.ascontiguousarray(f("L0_attn_Lam").reshape(8, 128).T)
    com["g1c"] = np.ascontiguousarray(f("L0_attn_g1").reshape(8, 128).T)
    com["g2c"] = np.ascontiguousarray(f("L0_attn_g2").reshape(8, 128).T)
    oh = np.zeros((8, D), dtype=np.float32)
    for e in range(8):
        oh[e, e * 128:(e + 1) * 128] = 1.0
    com["oh"] = oh
    com["ident"] = np.eye(128, dtype=np.float32)
    in_maps = []
    for k in range(NCORES):
        m = dict(com)
        m["x"] = np.ascontiguousarray(x[k * TOK:(k + 1) * TOK, :])
        r = k % 4
        sel = np.zeros((128, 8), dtype=np.float32)
        sel[:, :r] = 1.0
        m["sel"] = sel
        m["isel"] = np.ascontiguousarray(1.0 - sel)
        in_maps.append(m)
    return in_maps


def kernel(**inputs):
    in_maps = _prep_inputs(inputs)
    nc = build(debug=False)
    res = run_bass_kernel_spmd(nc, in_maps, core_ids=list(range(NCORES)))
    outs = [res.results[k]["out"] for k in range(NCORES)]
    full = np.concatenate(outs, axis=0).reshape(2, 8192, 1024)
    return np.ascontiguousarray(full.astype(np.float32))


# version 13
# speedup vs baseline: 1.4178x; 1.0361x over previous
import sys
import numpy as np
import concourse.bass as bass
import concourse.mybir as mybir
from concourse.bass_utils import run_bass_kernel_spmd

F32 = mybir.dt.float32
BF16 = mybir.dt.bfloat16
AF = mybir.ActivationFunctionType
ALU = mybir.AluOpType
AX = mybir.AxisListType

NCORES = 8
_SKIP = set()
TOK = 2048
D = 1024
NCHT = 8
CH = 256
NCH = TOK // CH
W = NCHT * CH
EPS = 1e-5
SQD = 32.0

ARENA = 52224

OF_IDF = 0
OF_IDB = 128
OF_ONEF = 192
OF_ONEB = 320
OF_RW = 448
OF_LAM = 512
OF_G1 = 520
OF_G2 = 528
OF_SEL = 536
OF_ISEL = 544
OF_LAC = 552
OF_RTS = 560
OF_TMPC = 568
OF_AUC = 600
OF_G = 616
OF_HIN = 680
OF_CCS = 688
OF_CCL = 816
OF_EXB = 944
OF_OH = 1456
OF_BIAS = 1968
OF_RB = 4016
OF_SS = 4032
OF_SQ = 4048
OF_MC = 4064
OF_VAR = 4080
OF_SVE = 4096
OF_Q1 = 4112
OF_RS = 4128
OF_C1 = 4144
OF_M2T = 4160
OF_ROWS = 4176
OF_GM = 5456
OF_SCOL = 5472
OF_GMT = 5504
OF_HCOL = 5760
OF_EXPS = 5768
OF_PERS_END = 5784

OF_T = [5792 + i * 4096 for i in range(8)]
OF_SCR = OF_T[7] + 4096
assert OF_SCR == 38560
SCR_END = ARENA


def _alloc(start, end, sizes):
    out = {}
    o = start
    for name, n in sizes:
        out[name] = o
        o += n
    assert o <= end, (o, end)
    return out


P1 = _alloc(OF_SCR, SCR_END, [
    ("B2", W), ("B3", W), ("B4", W), ("B5", W),
    ("r0fm", W // 2), ("xgelbf", W // 2),
])
P1.update(_alloc(OF_T[4], OF_T[6], [
    ("x0a", D), ("x0b", D), ("r0ta", D // 2), ("r0tb", D // 2),
    ("xna", D // 2), ("xnb", D // 2), ("sqscr", D),
    ("xnfa", W // 2), ("xnfb", W // 2),
]))
P1["stage"] = OF_T[6]
P1.update(_alloc(OF_T[6], OF_T[7] + 4096, [
    ("B1", W), ("linbf", W // 2), ("prebf", W // 2),
]))

P3 = _alloc(OF_SCR, SCR_END, [
    ("rga", 3 * W), ("rgb", 3 * W),
])
P3.update(_alloc(OF_T[0], OF_T[4], [
    ("h", W), ("mixa", W // 2), ("mixb", W // 2), ("r1", W), ("xmbf", W // 2),
    ("sig", W), ("x1p", W), ("xga", W // 2), ("xgb", W // 2),
    ("s2t", CH), ("r1b", W),
]))

PC = _alloc(OF_SCR, SCR_END, [
    ("x0a", D), ("x0b", D),
    ("x1A", W), ("x1B", W),
    ("xt", W), ("xtbf", W // 2), ("sqbf", W // 2),
    ("xg0", W // 2), ("xg1", W // 2),
    ("gmt2", 256), ("y", D),
])


EPERM = [4, 5, 6, 7, 0, 1, 2, 3]


class Sched:
    def __init__(self):
        self.cnt = {}
        self.ms = {}

    def inc(self, sem, n, ms=None):
        v = self.cnt[sem] = self.cnt.get(sem, 0) + n
        if ms is not None:
            self.ms[ms] = (sem, v)
        return v


def build(debug=False, upto=4, p01n=None, p01s=5):
    nc = bass.Bass("TRN2", target_bir_lowering=False,
                   detect_race_conditions=False)

    x_in = nc.dram_tensor("x", [TOK, D], F32, kind="ExternalInput").ap()
    wname = ["lin", "conv", "wa", "wx", "lin2", "w1", "w2", "w3"] + \
            [f"e{i}" for i in range(8)]
    wpk = {n: nc.dram_tensor("wpk_" + n, [128, 8192], F32, kind="ExternalInput").ap()
           for n in wname}
    bias1 = nc.dram_tensor("bias1", [1, 4096], F32, kind="ExternalInput").ap()
    bias2 = nc.dram_tensor("bias2", [1, 4096], F32, kind="ExternalInput").ap()
    exb_in = nc.dram_tensor("exb", [8, D], F32, kind="ExternalInput").ap()
    rw_in = nc.dram_tensor("rw", [128, 64], F32, kind="ExternalInput").ap()
    rb_in = nc.dram_tensor("rb", [1, 8], F32, kind="ExternalInput").ap()
    lam_in = nc.dram_tensor("lam", [128, 8], F32, kind="ExternalInput").ap()
    g1_in = nc.dram_tensor("g1c", [128, 8], F32, kind="ExternalInput").ap()
    g2_in = nc.dram_tensor("g2c", [128, 8], F32, kind="ExternalInput").ap()
    sel_in = nc.dram_tensor("sel", [128, 8], F32, kind="ExternalInput").ap()
    isel_in = nc.dram_tensor("isel", [128, 8], F32, kind="ExternalInput").ap()
    oh_in = nc.dram_tensor("oh", [8, D], F32, kind="ExternalInput").ap()
    id_in = nc.dram_tensor("ident", [128, 128], F32, kind="ExternalInput").ap()
    out = nc.dram_tensor("out", [TOK, D], F32, kind="ExternalOutput").ap()

    skind = "ExternalOutput" if debug else "Internal"
    scr_r0 = nc.dram_tensor("scr_r0", [128, NCH * W], BF16, kind=skind).ap()
    scr_a = nc.dram_tensor("scr_a", [128, NCH * W], F32, kind=skind).ap()
    scr_g = nc.dram_tensor("scr_g", [128, NCH * W], F32, kind=skind).ap()
    scr_xg = nc.dram_tensor("scr_xg", [128, NCH * W], BF16, kind=skind).ap()
    scr_x1p = nc.dram_tensor("scr_x1p", [128, NCH * W], F32, kind=skind).ap()
    cc_in = nc.dram_tensor("cc_in", [16, 128], F32).ap()
    cc_out = nc.dram_tensor("cc_out", [64, 128], F32).ap()

    arena_h = nc.alloc_sbuf_tensor("arena", [128, ARENA], F32)
    A = arena_h.ap()
    Ab = A.bitcast(BF16)

    def fv(off, n, p0=0, p1=128):
        return A[p0:p1, off:off + n]

    def bv(off, nbf, p0=0, p1=128):
        return Ab[p0:p1, 2 * off:2 * off + nbf]

    def f3(off):
        return fv(off, W).rearrange("p (a b) -> p a b", a=NCHT)

    def b3(off):
        return bv(off, W).rearrange("p (a b) -> p a b", a=NCHT)

    sch = Sched()

    import contextlib
    SEM_NAMES = (["pe", "ac", "dv", "cc",
                  "dxa", "dxb",
                  "dW", "dw2",
                  "dxg", "dr0", "dx1",
                  "dsa", "dsg",
                  "dr3a", "dr3b",
                  "dpa", "dpb",
                  "dsta", "dstb",
                  "dcc"]
                 + [f"de{i}" for i in range(8)])
    with contextlib.ExitStack() as _es:
        PS = _es.enter_context(nc.psum_tensor("PS", [128, 4096], F32))
        SEM = {n: _es.enter_context(nc.semaphore(n)) for n in SEM_NAMES}
        S_cc = SEM["cc"]
        block = _es.enter_context(nc.Block())
        PSf = PS[:]
        PSb = PSf.bitcast(BF16)

        def bank(b, n=1):
            return PSf[:, b * 512:(b + n) * 512]

        def emit_phase(fn, engines=("sync", "gpsimd", "tensor", "scalar", "vector")):
            snap0 = dict(sch.cnt)
            for e in engines:
                fn(e, None)
            snap1 = dict(sch.cnt)
            emitters = {"sync": block.sync, "gpsimd": block.gpsimd,
                        "tensor": block.tensor, "scalar": block.scalar,
                        "vector": block.vector}
            for e in engines:
                sch.cnt = dict(snap0)
                emitters[e](lambda h, e=e: fn(e, h))
            sch.cnt = dict(snap1)

        def wait(h, ms):
            if h is not None:
                sem, v = sch.ms[ms]
                h.wait_ge(SEM[sem], v)

        def tick(h, inst, sem, n, ms=None):
            if h is not None and inst is not None:
                inst.then_inc(SEM[sem], n)
            return sch.inc(sem, n, ms)

        def fence(h, inst, eng):
            v = sch.inc(eng, 1)
            if h is not None and inst is not None:
                inst.then_inc(SEM[eng], 1)
                h.wait_ge(SEM[eng], v)
            return v

        def full_mm(h, slot_off, rhs_off, b0):
            last = None
            r3 = b3(rhs_off)
            for m in range(8):
                for k in range(8):
                    if h:
                        last = h.matmul(
                            PSf[:, b0 * 512 + m * CH:b0 * 512 + m * CH + CH],
                            bv(slot_off + (k * 8 + m) * 64, 128),
                            r3[:, k, :],
                            start=(k == 0 and m % 2 == 0), stop=False)
            return last

        def bias_mm(h, bslot, b0, last):
            for m in range(8):
                if h:
                    last = h.matmul(
                        PSf[:, b0 * 512 + m * CH:b0 * 512 + m * CH + CH],
                        bv(OF_BIAS + bslot * 512 + m * 64, 128, 0, 1),
                        bv(OF_ONEB, CH, 0, 1),
                        start=False, stop=(m % 2 == 1))
            return last

        def ph_setup(e, h):
            if e == "gpsimd":
                def dm(o, i):
                    return h.dma_start(out=o, in_=i) if h else None
                tick(h, dm(fv(OF_IDF, 128), id_in), "dW", 16)
                tick(h, dm(bv(OF_IDB, 128), id_in), "dW", 16)
                tick(h, dm(fv(OF_RW, 64), rw_in), "dW", 16)
                tick(h, dm(fv(OF_RB, 8, 0, 1), rb_in), "dW", 16)
                tick(h, dm(fv(OF_LAM, 8), lam_in), "dW", 16)
                tick(h, dm(fv(OF_G1, 8), g1_in), "dW", 16)
                tick(h, dm(fv(OF_G2, 8), g2_in), "dW", 16)
                tick(h, dm(fv(OF_SEL, 8), sel_in), "dW", 16)
                tick(h, dm(fv(OF_ISEL, 8), isel_in), "dW", 16)
                tick(h, dm(bv(OF_EXB, 1024, 0, 8), exb_in), "dW", 16)
                tick(h, dm(bv(OF_OH, 1024, 0, 8), oh_in), "dW", 16)
                tick(h, dm(bv(OF_BIAS, 4096, 0, 1), bias1), "dW", 16)
                tick(h, dm(fv(P1["stage"], 4096), wpk["lin"][:, 0:4096]), "dW", 16)
                tick(h, dm(fv(P1["stage"] + 4096, 4096), wpk["lin"][:, 4096:8192]),
                     "dW", 16)
                tick(h, dm(bv(OF_T[1], 8192), wpk["conv"]), "dW", 16)
                tick(h, dm(bv(OF_T[2], 8192), wpk["wa"]), "dW", 16)
                tick(h, dm(bv(OF_T[3], 8192), wpk["wx"]), "dW", 16, "setup_done")
            elif e == "scalar":
                wait(h, "setup_done")
                i = h.activation(fv(OF_TMPC, 8), fv(OF_LAM, 8), AF.Sigmoid) if h else None
                fence(h, i, "ac")
                i = h.activation(fv(OF_LAC, 8), fv(OF_TMPC, 8), AF.Ln) if h else None
                tick(h, i, "ac", 1, "lac_ln")
            elif e == "vector":
                i = None
                if h:
                    h.memset(fv(OF_ONEF, 128), 1.0)
                    h.memset(bv(OF_ONEB, 256), 1.0)
                    i = h.memset(fv(OF_RTS, 8), 0.0)
                tick(h, i, "dv", 1)
                wait(h, "setup_done")
                last = None
                for k in range(8):
                    if h:
                        last = h.tensor_scalar(
                            bv(OF_T[0] + k * 512, 1024),
                            fv(P1["stage"] + k * 1024, 1024),
                            fv(OF_G1 + k, 1), None, ALU.mult)
                tick(h, last, "dv", 1, "fold_lin")
                wait(h, "lac_ln")
                i = h.tensor_scalar(fv(OF_LAC, 8), fv(OF_LAC, 8), 0.125,
                                    None, ALU.mult) if h else None
                tick(h, i, "dv", 1, "lac_done")

        emit_phase(ph_setup, engines=("gpsimd", "scalar", "vector"))

        def x0s(tt):
            return P1["x0a"] if tt % 2 == 0 else P1["x0b"]

        def r0ts(tt):
            return P1["r0ta"] if tt % 2 == 0 else P1["r0tb"]

        def xns(tt):
            return P1["xna"] if tt % 2 == 0 else P1["xnb"]

        def xnfm(c):
            return P1["xnfa"] if c % 2 == 0 else P1["xnfb"]

        def ph01(e, h):
            def fe_act1(cc):
                for tl in range(2):
                    tt = 2 * cc + tl
                    wait(h, f"x0_{tt}")
                    if tt >= 1:
                        wait(h, f"ssq_{tt - 1}")
                    i = h.activation(fv(P1["sqscr"], D), fv(x0s(tt), D),
                                     AF.Square) if h else None
                    tick(h, i, "ac", 1, f"sq_{tt}")

            def fe_act2(cc):
                wait(h, f"var_{cc}")
                i = None
                if h:
                    h.activation(fv(OF_SVE + 2 * cc, 2), fv(OF_VAR + 2 * cc, 2),
                                 AF.Sqrt, bias=np.float32(EPS))
                    i = h.activation(fv(OF_Q1 + 2 * cc, 2), fv(OF_VAR + 2 * cc, 2),
                                     AF.Sqrt, scale=np.float32(D))
                tick(h, i, "ac", 1, f"sveq_{cc}")

            def fe_act3(cc):
                for tl in range(2):
                    tt = 2 * cc + tl
                    wait(h, f"T_{tt}")
                    if cc >= 1 and tl == 0:
                        wait(h, f"spr0_{cc - 1}")
                    rb = 4096 if tt % 2 == 0 else 6144
                    xb = 5120 if tt % 2 == 0 else 7168
                    i = h.activation(
                        bv(P1["r0fm"], W).rearrange("p (a b) -> p a b", a=8)
                        [:, :, tl * 128:tl * 128 + 128],
                        PSb[:, rb:rb + 1024].rearrange("p (a b) -> p a b", a=8),
                        AF.Copy) if h else None
                    tick(h, i, "ac", 1, f"r0e_{tt}")
                    i = h.activation(
                        b3(xnfm(cc))[:, :, tl * 128:tl * 128 + 128],
                        PSb[:, xb:xb + 1024].rearrange("p (a b) -> p a b", a=8),
                        AF.Copy) if h else None
                    if tl == 1:
                        fence(h, i, "ac")
                        sch.ms[f"xne_{tt}"] = ("ac", sch.cnt["ac"])
                    else:
                        tick(h, i, "ac", 1, f"xne_{tt}")
                d = h.dma_start(out=scr_r0[:, cc * W:(cc + 1) * W],
                                in_=bv(P1["r0fm"], W)) if h else None
                tick(h, d, "dr0", 16, f"spr0_{cc}")

            def fe_dve1(cc):
                for tl in range(2):
                    tt = 2 * cc + tl
                    wait(h, f"x0_{tt}")
                    i = h.reduce_sum(fv(OF_SS + tt, 1), fv(x0s(tt), D),
                                     AX.X) if h else None
                    tick(h, i, "dv", 1, f"ssum_{tt}")
                    wait(h, f"sq_{tt}")
                    i = h.reduce_sum(fv(OF_SQ + tt, 1), fv(P1["sqscr"], D),
                                     AX.X) if h else None
                    tick(h, i, "dv", 1, f"ssq_{tt}")
                i = h.tensor_scalar(fv(OF_MC + 2 * cc, 2), fv(OF_SS + 2 * cc, 2),
                                    1.0 / D, None, ALU.mult) if h else None
                fence(h, i, "dv")
                i = h.tensor_tensor(fv(OF_M2T + 2 * cc, 2), fv(OF_MC + 2 * cc, 2),
                                    fv(OF_MC + 2 * cc, 2), ALU.mult) if h else None
                fence(h, i, "dv")
                i = h.scalar_tensor_tensor(
                    fv(OF_VAR + 2 * cc, 2), fv(OF_SQ + 2 * cc, 2), 1.0 / D,
                    fv(OF_M2T + 2 * cc, 2), ALU.mult, ALU.subtract) if h else None
                tick(h, i, "dv", 1, f"var_{cc}")

            def fe_dve2(cc):
                wait(h, f"sveq_{cc}")
                i = h.reciprocal(fv(OF_RS + 2 * cc, 2), fv(OF_SVE + 2 * cc, 2)) if h else None
                fence(h, i, "dv")
                i = h.tensor_tensor(fv(OF_M2T + 2 * cc, 2), fv(OF_Q1 + 2 * cc, 2),
                                    fv(OF_RS + 2 * cc, 2), ALU.mult) if h else None
                fence(h, i, "dv")
                i = h.tensor_scalar(fv(OF_M2T + 2 * cc, 2), fv(OF_M2T + 2 * cc, 2),
                                    1e-12, None, ALU.max) if h else None
                fence(h, i, "dv")
                i = h.reciprocal(fv(OF_M2T + 2 * cc, 2), fv(OF_M2T + 2 * cc, 2)) if h else None
                fence(h, i, "dv")
                i = h.scalar_tensor_tensor(
                    fv(OF_C1 + 2 * cc, 2), fv(OF_M2T + 2 * cc, 2), SQD,
                    fv(OF_RS + 2 * cc, 2), ALU.mult, ALU.mult) if h else None
                fence(h, i, "dv")
                for tl in range(2):
                    tt = 2 * cc + tl
                    if tt >= 2:
                        wait(h, f"T_{tt - 2}")
                    i = None
                    if h:
                        h.tensor_scalar(bv(r0ts(tt), 1024), fv(x0s(tt), D),
                                        fv(OF_MC + tt, 1), fv(OF_RS + tt, 1),
                                        ALU.subtract, ALU.mult)
                        i = h.tensor_scalar(bv(xns(tt), 1024), fv(x0s(tt), D),
                                            fv(OF_MC + tt, 1), fv(OF_C1 + tt, 1),
                                            ALU.subtract, ALU.mult)
                    tick(h, i, "dv", 1, f"apply_{cc}" if tl == 1 else None)

            if e == "sync":
                for tt in range(2 * NCH_RUN):
                    if tt >= 2:
                        wait(h, f"apply_{tt // 2 - 1}")
                        wait(h, f"sq_{tt - 2}")
                    d = h.dma_start(out=fv(x0s(tt), D),
                                    in_=x_in[tt * 128:(tt + 1) * 128, :]) if h else None
                    tick(h, d, "dxa" if tt % 2 == 0 else "dxb", 16, f"x0_{tt}")

            elif e == "vector":
                for s in range(-2, NCH_RUN):
                    if s + 2 < NCH_RUN:
                        fe_dve1(s + 2)
                    if s >= 0:
                        wait(h, f"prebf_{s}")
                        if s >= 1:
                            wait(h, f"spg_{s - 1}")
                        i = h.tensor_copy(fv(P1["B1"], W), bank(0, 4)) if h else None
                        tick(h, i, "dv", 1, f"pref_{s}")
                    if s + 2 < NCH_RUN:
                        fe_dve2(s + 2)
                    if s < 0:
                        continue
                    c = s
                    wait(h, f"rt_{c}")
                    last = None
                    for k in range(NCHT):
                        if h:
                            last = h.tensor_scalar(
                                fv(P1["B2"] + k * CH, CH), fv(P1["B2"] + k * CH, CH),
                                fv(OF_LAC + k, 1), None, ALU.mult)
                    fence(h, last, "dv")
                    sch.ms[f"lart_{c}"] = ("dv", sch.cnt["dv"])
                    i = h.tensor_reduce(fv(OF_TMPC, 8), f3(P1["B2"]), AX.X, ALU.add) if h else None
                    fence(h, i, "dv")
                    i = h.tensor_tensor(fv(OF_RTS, 8), fv(OF_RTS, 8),
                                        fv(OF_TMPC, 8), ALU.add) if h else None
                    tick(h, i, "dv", 1, f"lasum_{c}")
                    wait(h, f"it_{c}")
                    wait(h, f"sq1ma_{c}")
                    i = h.tensor_tensor(fv(P1["B1"], W), fv(P1["B3"], W),
                                        fv(P1["B1"], W), ALU.mult) if h else None
                    fence(h, i, "dv")
                    i = h.tensor_tensor(fv(P1["B1"], W), fv(P1["B1"], W),
                                        fv(P1["B5"], W), ALU.mult) if h else None
                    fence(h, i, "dv")
                    sch.ms[f"g_{c}"] = ("dv", sch.cnt["dv"])
                    last = None
                    for k in range(NCHT):
                        init = 0.0 if c == 0 else fv(OF_HCOL + k, 1)
                        if h:
                            last = h.tensor_tensor_scan(
                                fv(P1["B2"] + k * CH, CH), fv(P1["B4"] + k * CH, CH),
                                fv(P1["B1"] + k * CH, CH), init, ALU.mult, ALU.add)
                    fence(h, last, "dv")
                    sch.ms[f"uscan_{c}"] = ("dv", sch.cnt["dv"])
                    i = h.tensor_copy(fv(OF_HCOL, 8), f3(P1["B2"])[:, :, CH - 1]) if h else None
                    tick(h, i, "dv", 1, f"ucols_{c}")

            elif e == "scalar":
                for s in range(-2, NCH_RUN):
                    if s + 2 < NCH_RUN:
                        fe_act1(s + 2)
                    if s >= 0:
                        c = s
                        wait(h, f"wlin_mm_{c}")
                        i = h.activation(bv(P1["linbf"], W), bank(0, 4), AF.Copy) if h else None
                        tick(h, i, "ac", 1, f"lin_{c}")
                        if c >= 1:
                            wait(h, f"spxg_{c - 1}")
                        i = h.activation(bv(P1["xgelbf"], W), bv(P1["linbf"], W),
                                         AF.Gelu) if h else None
                        fence(h, i, "ac")
                        sch.ms[f"xgel_{c}"] = ("ac", sch.cnt["ac"])
                        d = h.dma_start(out=scr_xg[:, c * W:(c + 1) * W],
                                        in_=bv(P1["xgelbf"], W)) if h else None
                        tick(h, d, "dxg", 16, f"spxg_{c}")
                        wait(h, f"wconv_mm_{c}")
                        i = h.activation(bv(P1["prebf"], W), bank(0, 4), AF.Copy) if h else None
                        tick(h, i, "ac", 1, f"prebf_{c}")
                    if s + 2 < NCH_RUN:
                        fe_act2(s + 2)
                        fe_act3(s + 2)
                    if s < 0:
                        continue
                    c = s
                    wait(h, f"wa_mm_{c}")
                    if c >= 1:
                        wait(h, f"ucols_{c - 1}")
                    i = h.activation(fv(P1["B2"], W), bank(0, 4), AF.Sigmoid) if h else None
                    tick(h, i, "ac", 1, f"rt_{c}")
                    wait(h, f"wx_mm_{c}")
                    if c >= 1:
                        wait(h, f"g_{c - 1}")
                    i = h.activation(fv(P1["B3"], W), bank(0, 4), AF.Sigmoid) if h else None
                    tick(h, i, "ac", 1, f"it_{c}")
                    wait(h, f"lart_{c}")
                    if c >= 1:
                        wait(h, f"spa_{c - 1}")
                        wait(h, f"uscan_{c - 1}")
                    i = h.activation(fv(P1["B4"], W), fv(P1["B2"], W), AF.Exp) if h else None
                    fence(h, i, "ac")
                    sch.ms[f"a_{c}"] = ("ac", sch.cnt["ac"])
                    wait(h, f"lasum_{c}")
                    i = h.activation(fv(P1["B2"], W), fv(P1["B4"], W), AF.Square) if h else None
                    fence(h, i, "ac")
                    sch.ms[f"asq_{c}"] = ("ac", sch.cnt["ac"])
                    i = h.activation(fv(P1["B5"], W), fv(P1["B2"], W), AF.Sqrt,
                                     scale=np.float32(-1.0), bias=np.float32(1.0)) if h else None
                    tick(h, i, "ac", 1, f"sq1ma_{c}")

            elif e == "tensor":
                for s in range(-2, NCH_RUN):
                    if s >= 0:
                        c = s
                        wait(h, f"xne_{2 * c + 1}")
                        if c == 0:
                            wait(h, "fold_lin")
                        else:
                            wait(h, f"it_{c - 1}")
                        last = full_mm(h, OF_T[0], xnfm(c), 0)
                        last = bias_mm(h, 0, 0, last)
                        tick(h, last, "pe", 1, f"wlin_mm_{c}")
                        wait(h, f"lin_{c}")
                        last = full_mm(h, OF_T[1], P1["linbf"], 0)
                        last = bias_mm(h, 1, 0, last)
                        tick(h, last, "pe", 1, f"wconv_mm_{c}")
                        wait(h, f"prebf_{c}")
                        wait(h, f"pref_{c}")
                        last = full_mm(h, OF_T[2], P1["prebf"], 0)
                        last = bias_mm(h, 2, 0, last)
                        tick(h, last, "pe", 1, f"wa_mm_{c}")
                    if s + 2 < NCH_RUN:
                        cc = s + 2
                        for tl in range(2):
                            tt = 2 * cc + tl
                            wait(h, f"apply_{cc}")
                            if tt >= 2:
                                wait(h, f"r0e_{tt - 2}")
                                wait(h, f"xne_{tt - 2}")
                            if cc >= 2 and tl == 0:
                                wait(h, f"wlin_mm_{cc - 2}")
                            rb = 4096 if tt % 2 == 0 else 6144
                            xb = 5120 if tt % 2 == 0 else 7168
                            last = None
                            for k in range(NCHT):
                                if h:
                                    h.transpose(PSb[:, rb + k * 128:rb + k * 128 + 128],
                                                bv(r0ts(tt) + k * 64, 128),
                                                bv(OF_IDB, 128))
                                    last = h.transpose(
                                        PSb[:, xb + k * 128:xb + k * 128 + 128],
                                        bv(xns(tt) + k * 64, 128),
                                        bv(OF_IDB, 128))
                            tick(h, last, "pe", 1, f"T_{tt}")
                    if s >= 0:
                        c = s
                        wait(h, f"rt_{c}")
                        last = full_mm(h, OF_T[3], P1["prebf"], 0)
                        last = bias_mm(h, 3, 0, last)
                        tick(h, last, "pe", 1, f"wx_mm_{c}")

            elif e == "gpsimd":
                for c in range(NCH_RUN):
                    wait(h, f"uscan_{c}")
                    d = h.dma_start(out=scr_a[:, c * W:(c + 1) * W],
                                    in_=fv(P1["B4"], W)) if h else None
                    tick(h, d, "dsa", 16, f"spa_{c}")
                    wait(h, f"g_{c}")
                    d = h.dma_start(out=scr_g[:, c * W:(c + 1) * W],
                                    in_=fv(P1["B1"], W)) if h else None
                    tick(h, d, "dsg", 16, f"spg_{c}")

        NCH_RUN = NCH if p01n is None else p01n
        if upto >= 1:
            emit_phase(ph01)

        def ph2(e, h):
            if e == "scalar":
                wait(h, "lasum_7")
                i = h.activation(fv(OF_AUC, 8), fv(OF_RTS, 8), AF.Exp) if h else None
                tick(h, i, "ac", 1, "Acols")
                wait(h, "Ct")
                i = h.activation(fv(OF_CCS, 128, 0, 16),
                                 PSf[0:16, 3072:3072 + 128], AF.Copy) if h else None
                tick(h, i, "ac", 1, "ccs_done")
                wait(h, "Gt")
                i = h.activation(fv(OF_G, 64), PSf[:, 3072:3072 + 64],
                                 AF.Copy) if h else None
                tick(h, i, "ac", 1, "G_done")
            elif e == "vector":
                wait(h, "ucols_7")
                i = h.tensor_copy(fv(OF_AUC + 8, 8), fv(OF_HCOL, 8)) if h else None
                tick(h, i, "dv", 1, "Ucols")
                wait(h, "G_done")
                wait(h, "setup_done")
                i = h.memset(fv(OF_HIN, 8), 0.0) if h else None
                tick(h, i, "dv", 1)
                for j in range(4):
                    gA = fv(OF_G + j * 16, 8)
                    gU = fv(OF_G + j * 16 + 8, 8)
                    i = h.tensor_scalar(fv(OF_TMPC, 8), gA, fv(OF_SEL + j, 1),
                                        fv(OF_ISEL + j, 1), ALU.mult, ALU.add) if h else None
                    fence(h, i, "dv")
                    i = h.tensor_tensor(fv(OF_HIN, 8), fv(OF_TMPC, 8),
                                        fv(OF_HIN, 8), ALU.mult) if h else None
                    fence(h, i, "dv")
                    i = h.tensor_scalar(fv(OF_TMPC, 8), gU, fv(OF_SEL + j, 1),
                                        None, ALU.mult) if h else None
                    fence(h, i, "dv")
                    i = h.tensor_tensor(fv(OF_HIN, 8), fv(OF_HIN, 8),
                                        fv(OF_TMPC, 8), ALU.add) if h else None
                    fence(h, i, "dv")
                tick(h, None, "dv", 0, "hin_done")
                sch.ms["hin_done"] = ("dv", sch.cnt["dv"])
            elif e == "tensor":
                wait(h, "Acols")
                wait(h, "Ucols")
                wait(h, "xne_15")
                i = h.transpose(PSf[0:16, 3072:3072 + 128],
                                fv(OF_AUC, 16), fv(OF_IDF, 128)) if h else None
                tick(h, i, "pe", 1, "Ct")
                wait(h, "ccl_done")
                wait(h, "ccs_done")
                i = h.transpose(PSf[:, 3072:3072 + 64],
                                fv(OF_CCL, 128, 0, 64),
                                fv(OF_IDF, 64, 0, 64)) if h else None
                tick(h, i, "pe", 1, "Gt")
            elif e == "gpsimd":
                wait(h, "ccs_done")
                d = h.dma_start(out=cc_in[:, :], in_=fv(OF_CCS, 128, 0, 16)) if h else None
                v = tick(h, d, "dcc", 16)
                if h:
                    h.wait_ge(SEM["dcc"], v)
                i = h.collective_compute(
                    "AllGather", ALU.bypass,
                    replica_groups=[[0, 1, 2, 3], [4, 5, 6, 7]],
                    ins=[cc_in.opt()],
                    outs=[cc_out.opt()]) if h else None
                v = tick(h, i, "cc", 1)
                if h:
                    h.wait_ge(S_cc, v)
                d = h.dma_start(out=fv(OF_CCL, 128, 0, 64), in_=cc_out[:, :]) if h else None
                tick(h, d, "dcc", 16, "ccl_done")
                wait(h, "T_15")
                wait(h, "wx_mm_7")
                d = h.dma_start(out=bv(OF_T[4], 8192), in_=wpk["lin2"]) if h else None
                tick(h, d, "dw2", 16)
                d = h.dma_start(out=bv(OF_T[5], 8192), in_=wpk["w1"]) if h else None
                tick(h, d, "dw2", 16)
                d = h.dma_start(out=bv(OF_T[6], 8192), in_=wpk["w2"]) if h else None
                tick(h, d, "dw2", 16)
                d = h.dma_start(out=bv(OF_T[7], 8192), in_=wpk["w3"]) if h else None
                tick(h, d, "dw2", 16)
                d = h.dma_start(out=bv(OF_BIAS, 4096, 0, 1), in_=bias2) if h else None
                tick(h, d, "dw2", 16, "w2b")

        if upto >= 2:
            emit_phase(ph2, engines=("scalar", "vector", "tensor", "gpsimd"))

        def rg(c):
            return P3["rga"] if c % 2 == 0 else P3["rgb"]

        def xg3(c):
            return P3["xga"] if c % 2 == 0 else P3["xgb"]

        def mx3(c):
            return P3["mixa"] if c % 2 == 0 else P3["mixb"]

        def ph3ab(e, h):
            if e == "sync":
                for c in range(NCH):
                    base = rg(c)
                    wait(h, "spa_7")
                    wait(h, "spg_7")
                    wait(h, "spr0_7")
                    wait(h, "spxg_7")
                    if c >= 2:
                        wait(h, f"r1stt_{c - 2}")
                        wait(h, f"mix_{c - 2}")
                    d = h.dma_start(out=fv(base, W),
                                    in_=scr_a[:, c * W:(c + 1) * W]) if h else None
                    tick(h, d, "dr3a" if c % 2 == 0 else "dr3b", 16)
                    d = h.dma_start(out=fv(base + W, W),
                                    in_=scr_g[:, c * W:(c + 1) * W]) if h else None
                    tick(h, d, "dr3a" if c % 2 == 0 else "dr3b", 16)
                    d = h.dma_start(out=bv(base + 2 * W, W),
                                    in_=scr_r0[:, c * W:(c + 1) * W]) if h else None
                    tick(h, d, "dr3a" if c % 2 == 0 else "dr3b", 16)
                    d = h.dma_start(out=bv(xg3(c), W),
                                    in_=scr_xg[:, c * W:(c + 1) * W]) if h else None
                    tick(h, d, "dr3a" if c % 2 == 0 else "dr3b", 16, f"ld3_{c}")

            elif e == "vector":
                def r1b(c):
                    return P3["r1"] if c % 2 == 0 else P3["r1b"]

                def dv_hmix(cc):
                    wait(h, f"ld3_{cc}")
                    if cc == 0:
                        wait(h, "hin_done")
                    last = None
                    for k in range(NCHT):
                        init = fv(OF_HIN + k, 1) if cc == 0 else fv(OF_HCOL + k, 1)
                        if h:
                            last = h.tensor_tensor_scan(
                                fv(P3["h"] + k * CH, CH), fv(rg(cc) + k * CH, CH),
                                fv(rg(cc) + W + k * CH, CH), init, ALU.mult, ALU.add)
                    fence(h, last, "dv")
                    sch.ms[f"hscan_{cc}"] = ("dv", sch.cnt["dv"])
                    i = h.tensor_copy(fv(OF_HCOL, 8), f3(P3["h"])[:, :, CH - 1]) if h else None
                    fence(h, i, "dv")
                    sch.ms[f"hcols_{cc}"] = ("dv", sch.cnt["dv"])
                    if cc >= 2:
                        wait(h, f"stats2_mm_{cc - 2}")
                    i = h.tensor_tensor(bv(mx3(cc), W), bv(xg3(cc), W),
                                        fv(P3["h"], W), ALU.mult) if h else None
                    tick(h, i, "dv", 1, f"mix_{cc}")

                def dv_tail(cc):
                    wait(h, f"gelu_{cc}")
                    wait(h, f"w2_mm_{cc}")
                    i = h.tensor_tensor(bv(P3["xmbf"], W), fv(P3["sig"], W),
                                        bank(0, 4), ALU.mult) if h else None
                    tick(h, i, "dv", 1, f"comb_{cc}")
                    wait(h, f"w3_mm_{cc}")
                    if cc >= 1:
                        wait(h, f"spx1_{cc - 1}")
                    i = h.tensor_tensor(fv(P3["x1p"], W), bank(0, 4),
                                        fv(r1b(cc), W), ALU.add) if h else None
                    tick(h, i, "dv", 1, f"x1p_{cc}")

                dv_hmix(0)
                for c in range(NCH):
                    wait(h, f"wlin2_mm_{c}")
                    i = h.scalar_tensor_tensor(fv(r1b(c), W), PSf[:, 2048:4096],
                                               1.0, bv(rg(c) + 2 * W, W), ALU.mult,
                                               ALU.add) if h else None
                    tick(h, i, "dv", 1, f"r1stt_{c}")
                    if c + 1 < NCH:
                        dv_hmix(c + 1)
                    if c >= 1:
                        dv_tail(c - 1)
                    wait(h, f"sveq2_{c}")
                    i = h.tensor_scalar(fv(P3["s2t"], CH), fv(P3["s2t"], CH),
                                        1e-12 / SQD, None, ALU.max) if h else None
                    fence(h, i, "dv")
                    i = h.reciprocal(fv(P3["s2t"], CH), fv(P3["s2t"], CH)) if h else None
                    fence(h, i, "dv")
                    sch.ms[f"s2row_{c}"] = ("dv", sch.cnt["dv"])
                    last = None
                    for k in range(NCHT):
                        if h:
                            last = h.scalar_tensor_tensor(
                                bv(P3["xmbf"] + k * 128, 256),
                                fv(r1b(c) + k * CH, CH), fv(OF_G2 + k, 1),
                                fv(P3["s2t"], CH), ALU.mult, ALU.mult)
                    tick(h, last, "dv", 1, f"xm_{c}")
                dv_tail(NCH - 1)

            elif e == "tensor":
                def emit_wlin2(c2):
                    wait(h, f"mix_{c2}")
                    if c2 == 0:
                        wait(h, "w2b")
                    if c2 >= 1:
                        wait(h, f"r1stt_{c2 - 1}")
                    last = full_mm(h, OF_T[4], mx3(c2), 4)
                    last = bias_mm(h, 0, 4, last)
                    tick(h, last, "pe", 1, f"wlin2_mm_{c2}")

                emit_wlin2(0)
                for c in range(1, NCH + 1):
                    cc = c - 1
                    wait(h, f"sq2_{cc}")
                    if cc >= 1:
                        wait(h, f"x1p_{cc - 1}")
                        wait(h, f"sveq2_{cc - 1}")
                    last = None
                    for k in range(NCHT):
                        if h:
                            last = h.matmul(
                                PSf[:, 0:256],
                                bv(OF_ONEB, 128),
                                bv(mx3(cc) + k * 128, 256),
                                start=(k == 0), stop=(k == 7))
                    tick(h, last, "pe", 1, f"stats2_mm_{cc}")
                    if cc + 1 < NCH:
                        emit_wlin2(cc + 1)
                    wait(h, f"xm_{cc}")
                    last = full_mm(h, OF_T[5], P3["xmbf"], 0)
                    last = bias_mm(h, 1, 0, last)
                    tick(h, last, "pe", 1, f"w1_mm_{cc}")
                    wait(h, f"sig_{cc}")
                    last = full_mm(h, OF_T[6], P3["xmbf"], 0)
                    last = bias_mm(h, 2, 0, last)
                    tick(h, last, "pe", 1, f"w2_mm_{cc}")
                    wait(h, f"comb_{cc}")
                    last = full_mm(h, OF_T[7], P3["xmbf"], 0)
                    last = bias_mm(h, 3, 0, last)
                    tick(h, last, "pe", 1, f"w3_mm_{cc}")

            elif e == "scalar":
                for c in range(NCH):
                    wait(h, f"r1stt_{c}")
                    if c >= 1:
                        wait(h, f"stats2_mm_{c - 1}")
                    i = h.activation(bv(mx3(c), W),
                                     fv(P3["r1"] if c % 2 == 0 else P3["r1b"], W),
                                     AF.Square) if h else None
                    tick(h, i, "ac", 1, f"sq2_{c}")
                    wait(h, f"stats2_mm_{c}")
                    if c >= 1:
                        wait(h, f"xm_{c - 1}")
                    i = h.activation(fv(P3["s2t"], CH), PSf[:, 0:256],
                                     AF.Sqrt, scale=np.float32(1.0 / D)) if h else None
                    tick(h, i, "ac", 1, f"sveq2_{c}")
                    wait(h, f"w1_mm_{c}")
                    if c >= 1:
                        wait(h, f"comb_{c - 1}")
                    i = h.activation(fv(P3["sig"], W), bank(0, 4), AF.Sigmoid) if h else None
                    fence(h, i, "ac")
                    sch.ms[f"sig_{c}"] = ("ac", sch.cnt["ac"])
                    i = h.activation(fv(P3["sig"], W), fv(P3["sig"], W),
                                     AF.Gelu) if h else None
                    tick(h, i, "ac", 1, f"gelu_{c}")
                    wait(h, f"x1p_{c}")
                    d = h.dma_start(out=scr_x1p[:, c * W:(c + 1) * W],
                                    in_=fv(P3["x1p"], W)) if h else None
                    tick(h, d, "dx1", 16, f"spx1_{c}")

            elif e == "gpsimd":
                slot_free = {4: ["wlin2_mm_7"], 5: ["w1_mm_7"], 6: ["w2_mm_7"],
                             7: ["w3_mm_7"], 0: ["stats2_mm_7"],
                             1: ["comb_7", "x1p_6"],
                             2: ["spx1_7", "mix_7"],
                             3: ["xm_7", "x1p_7"]}
                for ei in range(8):
                    for msn in slot_free[EPERM[ei]]:
                        wait(h, msn)
                    d = h.dma_start(out=bv(OF_T[EPERM[ei]], 8192),
                                    in_=wpk[f"e{ei}"]) if h else None
                    tick(h, d, f"de{ei}", 16, f"ld_e{ei}")

        if upto >= 3:
            emit_phase(ph3ab)

        def x1buf(c):
            return PC["x1A"] if c % 2 == 0 else PC["x1B"]

        def x0cs(tt):
            return PC["x0a"] if tt % 2 == 0 else PC["x0b"]

        def xgs(ei):
            return PC["xg0"] if ei % 2 == 0 else PC["xg1"]

        def gmtb(c):
            return OF_GMT if c % 2 == 0 else PC["gmt2"]

        def ph3c(e, h):
            B6 = 3072
            B7 = 3584

            if e == "sync":
                def ldx1(cc):
                    if cc == 0:
                        wait(h, "spx1_0")
                        wait(h, "hscan_6")
                    elif cc == 1:
                        wait(h, "spx1_1")
                        wait(h, "r1stt_6")
                    else:
                        wait(h, f"x2T_{cc - 2}_1")
                    d = h.dma_start(out=fv(x1buf(cc), W),
                                    in_=scr_x1p[:, cc * W:(cc + 1) * W]) if h else None
                    tick(h, d, "dpa" if cc % 2 == 0 else "dpb", 16, f"ldx1_{cc}")

                def ldx0(tt):
                    d = h.dma_start(out=fv(x0cs(tt), D),
                                    in_=x_in[tt * 128:(tt + 1) * 128, :]) if h else None
                    tick(h, d, "dxa" if tt % 2 == 0 else "dxb", 16, f"ldx0_{tt}")

                def st(tt):
                    wait(h, f"ev_{tt}")
                    d = h.dma_start(out=out[tt * 128:(tt + 1) * 128, :],
                                    in_=fv(PC["y"], D)) if h else None
                    tick(h, d, "dsta" if tt % 2 == 0 else "dstb", 16, f"st_{tt}")

                ldx1(0)
                ldx0(0)
                ldx0(1)
                wait(h, "x0T_0_0")
                ldx1(1)
                ldx0(2)
                wait(h, "x0T_0_1")
                ldx0(3)
                for c in range(1, NCH):
                    st(2 * c - 2)
                    st(2 * c - 1)
                    if c + 1 < NCH:
                        ldx1(c + 1)
                    if 2 * c + 2 < 2 * NCH:
                        wait(h, f"x0T_{c}_0")
                        ldx0(2 * c + 2)
                        wait(h, f"x0T_{c}_1")
                        ldx0(2 * c + 3)
                st(2 * NCH - 2)
                st(2 * NCH - 1)
                for msn in (f"st_{2 * NCH - 2}", f"st_{2 * NCH - 1}"):
                    sem, v = sch.ms[msn]
                    if h:
                        h.wait_ge(SEM[sem], v)

            elif e == "tensor":
                def pe_x0t(cc, tl):
                    tt = 2 * cc + tl
                    wait(h, f"ldx0_{tt}")
                    if tl == 0:
                        if cc == 0:
                            wait(h, "r1stt_7")
                        elif cc >= 2:
                            wait(h, f"ev_{2 * cc - 3}")
                    else:
                        wait(h, f"x1tta_{cc}")
                    last = None
                    for k in range(NCHT):
                        if h:
                            last = h.transpose(
                                PSf[:, B6 + k * 128:B6 + k * 128 + 128],
                                fv(x0cs(tt) + k * 128, 128),
                                fv(OF_IDF, 128))
                    tick(h, last, "pe", 1, f"x0T_{cc}_{tl}")

                def pe_stats3(cc):
                    wait(h, f"x1tt_{cc}")
                    wait(h, f"sq3_{cc}")
                    last = None
                    for k in range(NCHT):
                        if h:
                            last = h.matmul(PSf[0:1, B6:B6 + 256],
                                            fv(OF_ONEF, 1),
                                            fv(x1buf(cc) + k * CH, CH),
                                            start=(k == 0), stop=(k == 7))
                    for k in range(NCHT):
                        if h:
                            last = h.matmul(PSf[0:1, B6 + 256:B6 + 512],
                                            bv(OF_ONEB, 1),
                                            bv(PC["sqbf"] + k * 128, 256),
                                            start=(k == 0), stop=(k == 7))
                    tick(h, last, "pe", 1, f"stats3_mm_{cc}")

                def pe_bc3(cc):
                    wait(h, f"rows3_{cc}")
                    i = None
                    if h:
                        h.matmul(PSf[:, B7:B7 + 256],
                                 fv(OF_ONEF, 128, 0, 1),
                                 fv(OF_ROWS, 256, 0, 1), start=True, stop=True)
                        i = h.matmul(PSf[:, B7 + 256:B7 + 512],
                                     fv(OF_ONEF, 128, 0, 1),
                                     fv(OF_ROWS + 256, 256, 0, 1),
                                     start=True, stop=True)
                    tick(h, i, "pe", 1, f"bc3_mm_{cc}")

                def pe_rtr(cc):
                    wait(h, f"xt_{cc}")
                    last = None
                    for tl in range(2):
                        for k in range(NCHT):
                            if h:
                                last = h.matmul(
                                    PSf[:, B6 + tl * 8:B6 + tl * 8 + 8],
                                    fv(PC["xt"] + k * CH + tl * 128, 128),
                                    fv(OF_RW + k * 8, 8),
                                    start=(k == 0), stop=False)
                        if h:
                            last = h.matmul(
                                PSf[:, B6 + tl * 8:B6 + tl * 8 + 8],
                                fv(OF_ONEF, 128, 0, 1), fv(OF_RB, 8, 0, 1),
                                start=False, stop=True)
                    tick(h, last, "pe", 1, f"rtr_{cc}")

                def pe_gmt(cc, tl):
                    wait(h, f"gm_{cc}_{tl}")
                    i = h.transpose(
                        PSf[0:8, B6 + 256 + tl * 128:B6 + 256 + tl * 128 + 128],
                        fv(OF_GM + tl * 8, 8), fv(OF_IDF, 128)) if h else None
                    tick(h, i, "pe", 1, f"gmt_{cc}_{tl}")

                def pe_gmbc(cc, ei):
                    boff = 2048 if ei % 2 == 0 else 2560
                    wait(h, f"gmtbf_{cc}")
                    if ei >= 2:
                        wait(h, f"xg_{cc}_{ei - 2}")
                    elif cc >= 1:
                        wait(h, f"xg_{cc - 1}_{ei + 6}")
                    i = h.matmul(PSf[:, boff:boff + CH],
                                 bv(OF_OH + ei * 64, 128, 0, 8),
                                 bv(gmtb(cc), CH, 0, 8),
                                 start=True, stop=True) if h else None
                    tick(h, i, "pe", 1, f"gmbc_{cc}_{ei}")

                def pe_exp(cc, ei):
                    wait(h, f"xg_{cc}_{ei}")
                    wait(h, f"ld_e{ei}")
                    if ei == 0:
                        if cc >= 1:
                            wait(h, f"x2_{cc - 1}")
                        else:
                            wait(h, "x1p_7")
                    last = None
                    for m in range(8):
                        for k in range(8):
                            if h:
                                last = h.matmul(
                                    PSf[:, m * CH:m * CH + CH],
                                    bv(OF_T[EPERM[ei]] + (k * 8 + m) * 64, 128),
                                    b3(xgs(ei))[:, k, :],
                                    start=(ei == 0 and k == 0 and m % 2 == 0),
                                    stop=False)
                    tick(h, last, "pe", 1, f"exp_{cc}_{ei}")

                def pe_expb(cc):
                    last = None
                    for m in range(8):
                        if h:
                            last = h.matmul(PSf[:, m * CH:m * CH + CH],
                                            bv(OF_EXB + m * 64, 128, 0, 8),
                                            bv(gmtb(cc), CH, 0, 8),
                                            start=False, stop=(m % 2 == 1))
                    tick(h, last, "pe", 1, f"expb_{cc}")

                def pe_x2t(cc, tl):
                    if tl == 0:
                        wait(h, f"x2_{cc}")
                        if cc == NCH - 1:
                            wait(h, f"ev_{2 * cc - 1}")
                    else:
                        wait(h, f"ev_{2 * cc}")
                    last = None
                    for k in range(NCHT):
                        if h:
                            last = h.transpose(
                                PSf[:, B6 + k * 128:B6 + k * 128 + 128],
                                fv(x1buf(cc) + k * CH + tl * 128, 128),
                                fv(OF_IDF, 128))
                    tick(h, last, "pe", 1, f"x2T_{cc}_{tl}")

                pe_x0t(0, 0)
                pe_x0t(0, 1)
                pe_stats3(0)
                pe_bc3(0)
                pe_rtr(0)
                pe_gmt(0, 0)
                pe_gmt(0, 1)
                pe_gmbc(0, 0)
                pe_gmbc(0, 1)
                for c in range(NCH):
                    fe = c + 1 if c + 1 < NCH else None
                    pe_exp(c, 0)
                    if c >= 1:
                        pe_x2t(c - 1, 0)
                    pe_gmbc(c, 2)
                    pe_exp(c, 1)
                    if c >= 1:
                        pe_x2t(c - 1, 1)
                    pe_gmbc(c, 3)
                    pe_exp(c, 2)
                    if fe is not None:
                        pe_x0t(fe, 0)
                    pe_gmbc(c, 4)
                    pe_exp(c, 3)
                    if fe is not None:
                        pe_x0t(fe, 1)
                    pe_gmbc(c, 5)
                    pe_exp(c, 4)
                    if fe is not None:
                        pe_stats3(fe)
                    pe_gmbc(c, 6)
                    pe_exp(c, 5)
                    if fe is not None:
                        pe_bc3(fe)
                    pe_gmbc(c, 7)
                    pe_exp(c, 6)
                    if fe is not None:
                        pe_rtr(fe)
                    pe_exp(c, 7)
                    if fe is not None:
                        pe_gmt(fe, 0)
                        pe_gmt(fe, 1)
                        pe_gmbc(fe, 0)
                        pe_gmbc(fe, 1)
                    pe_expb(c)
                pe_x2t(NCH - 1, 0)
                pe_x2t(NCH - 1, 1)

            elif e == "vector":
                def dv_x1tt(cc, tl):
                    if tl == 0:
                        wait(h, f"ldx1_{cc}")
                    wait(h, f"x0T_{cc}_{tl}")
                    dst = f3(x1buf(cc))[:, :, tl * 128:tl * 128 + 128]
                    i = h.tensor_tensor(
                        dst, dst,
                        PSf[:, B6:B6 + 1024].rearrange("p (a b) -> p a b", a=8),
                        ALU.add) if h else None
                    tick(h, i, "dv", 1, f"x1tta_{cc}" if tl == 0 else f"x1tt_{cc}")

                def dv_var3(cc):
                    wait(h, f"stats3_mm_{cc}")
                    i = h.tensor_scalar(fv(OF_ROWS + 512, 256, 0, 1),
                                        PSf[0:1, B6:B6 + 256],
                                        1.0 / D, None, ALU.mult) if h else None
                    fence(h, i, "dv")
                    i = h.tensor_tensor(fv(OF_ROWS + 768, 256, 0, 1),
                                        fv(OF_ROWS + 512, 256, 0, 1),
                                        fv(OF_ROWS + 512, 256, 0, 1), ALU.mult) if h else None
                    fence(h, i, "dv")
                    i = h.scalar_tensor_tensor(
                        fv(OF_ROWS + 768, 256, 0, 1),
                        PSf[0:1, B6 + 256:B6 + 512], 1.0 / D,
                        fv(OF_ROWS + 768, 256, 0, 1), ALU.mult, ALU.subtract) if h else None
                    tick(h, i, "dv", 1, f"var3_{cc}")

                def dv_rows3(cc):
                    wait(h, f"sve3_{cc}")
                    i = h.reciprocal(fv(OF_ROWS, 256, 0, 1),
                                     fv(OF_ROWS + 1024, 256, 0, 1)) if h else None
                    fence(h, i, "dv")
                    i = h.tensor_tensor(fv(OF_ROWS + 256, 256, 0, 1),
                                        fv(OF_ROWS + 512, 256, 0, 1),
                                        fv(OF_ROWS, 256, 0, 1), ALU.mult) if h else None
                    tick(h, i, "dv", 1, f"rows3_{cc}")

                def dv_xt(cc):
                    wait(h, f"bc3_mm_{cc}")
                    i = h.tensor_tensor(f3(PC["xt"]), f3(x1buf(cc)),
                                        PSf[:, B7:B7 + 256].unsqueeze(1)
                                        .broadcast_to([128, 8, 256]), ALU.mult) if h else None
                    fence(h, i, "dv")
                    i = h.tensor_tensor(f3(PC["xt"]), f3(PC["xt"]),
                                        PSf[:, B7 + 256:B7 + 512]
                                        .unsqueeze(1).broadcast_to([128, 8, 256]),
                                        ALU.subtract) if h else None
                    fence(h, i, "dv")
                    sch.ms[f"xt_{cc}"] = ("dv", sch.cnt["dv"])

                def dv_softmax(cc):
                    for tl in range(2):
                        wait(h, f"rtr_{cc}")
                        lg = PSf[:, B6 + tl * 8:B6 + tl * 8 + 8]
                        i = h.reduce_max(fv(OF_SCOL + tl * 16, 1), lg, AX.X) if h else None
                        fence(h, i, "dv")
                        i = h.tensor_scalar(fv(OF_SCOL + tl * 16 + 1, 1),
                                            fv(OF_SCOL + tl * 16, 1), -1.0,
                                            None, ALU.mult) if h else None
                        tick(h, i, "dv", 1, f"lmax_{cc}_{tl}")
                        wait(h, f"sme_{cc}_{tl}")
                        i = h.reciprocal(fv(OF_SCOL + tl * 16 + 3, 1),
                                         fv(OF_SCOL + tl * 16 + 2, 1)) if h else None
                        fence(h, i, "dv")
                        i = h.tensor_scalar(fv(OF_GM + tl * 8, 8), lg,
                                            fv(OF_SCOL + tl * 16, 1), None,
                                            ALU.is_ge) if h else None
                        fence(h, i, "dv")
                        i = h.tensor_scalar(fv(OF_GM + tl * 8, 8),
                                            fv(OF_GM + tl * 8, 8),
                                            fv(OF_SCOL + tl * 16 + 3, 1),
                                            None, ALU.mult) if h else None
                        tick(h, i, "dv", 1, f"gm_{cc}_{tl}")

                def dv_xtbf(cc):
                    i = h.tensor_copy(bv(PC["xtbf"], W), fv(PC["xt"], W)) if h else None
                    tick(h, i, "dv", 1, f"xtbf_{cc}")

                def dv_xg(cc, ei):
                    boff = 2048 if ei % 2 == 0 else 2560
                    wait(h, f"gmbc_{cc}_{ei}")
                    if ei >= 2:
                        wait(h, f"exp_{cc}_{ei - 2}")
                    elif cc >= 1:
                        wait(h, f"exp_{cc - 1}_{ei + 6}")
                    i = h.tensor_tensor(
                        b3(xgs(ei)), b3(PC["xtbf"]),
                        PSf[:, boff:boff + CH].unsqueeze(1)
                        .broadcast_to([128, 8, 256]), ALU.mult) if h else None
                    tick(h, i, "dv", 1, f"xg_{cc}_{ei}")

                def dv_x2(cc):
                    wait(h, f"expb_{cc}")
                    i = h.tensor_tensor(fv(x1buf(cc), W), bank(0, 4),
                                        fv(x1buf(cc), W), ALU.add) if h else None
                    tick(h, i, "dv", 1, f"x2_{cc}")

                dv_x1tt(0, 0)
                dv_x1tt(0, 1)
                dv_var3(0)
                dv_rows3(0)
                dv_xt(0)
                dv_softmax(0)
                dv_xtbf(0)
                dv_xg(0, 0)
                dv_xg(0, 1)
                for c in range(NCH):
                    fe = c + 1 if c + 1 < NCH else None
                    dv_xg(c, 2)
                    dv_xg(c, 3)
                    if fe is not None:
                        dv_x1tt(fe, 0)
                    dv_xg(c, 4)
                    if fe is not None:
                        dv_x1tt(fe, 1)
                    dv_xg(c, 5)
                    if fe is not None:
                        dv_var3(fe)
                        dv_rows3(fe)
                    dv_xg(c, 6)
                    if fe is not None:
                        dv_xt(fe)
                    dv_xg(c, 7)
                    if fe is not None:
                        dv_softmax(fe)
                        dv_xtbf(fe)
                        dv_xg(fe, 0)
                    dv_x2(c)
                    if fe is not None:
                        dv_xg(fe, 1)

            elif e == "scalar":
                def ev(tt):
                    wait(h, f"x2T_{tt // 2}_{tt % 2}")
                    if tt >= 1:
                        wait(h, f"st_{tt - 1}")
                    i = h.activation(fv(PC["y"], D), PSf[:, B6:B6 + 1024],
                                     AF.Copy) if h else None
                    tick(h, i, "ac", 1, f"ev_{tt}")

                for c in range(NCH):
                    if c >= 2:
                        ev(2 * (c - 2))
                        ev(2 * (c - 2) + 1)
                    wait(h, f"x1tt_{c}")
                    if c >= 1:
                        wait(h, f"stats3_mm_{c - 1}")
                    i = h.activation(bv(PC["sqbf"], W), fv(x1buf(c), W),
                                     AF.Square) if h else None
                    tick(h, i, "ac", 1, f"sq3_{c}")
                    wait(h, f"var3_{c}")
                    i = h.activation(fv(OF_ROWS + 1024, 256, 0, 1),
                                     fv(OF_ROWS + 768, 256, 0, 1), AF.Sqrt,
                                     bias=np.float32(EPS)) if h else None
                    tick(h, i, "ac", 1, f"sve3_{c}")
                    for tl in range(2):
                        wait(h, f"lmax_{c}_{tl}")
                        i = h.activation(fv(OF_EXPS + tl * 8, 8),
                                         PSf[:, B6 + tl * 8:B6 + tl * 8 + 8],
                                         AF.Exp, bias=fv(OF_SCOL + tl * 16 + 1, 1),
                                         accum_out=fv(OF_SCOL + tl * 16 + 2, 1)) if h else None
                        tick(h, i, "ac", 1, f"sme_{c}_{tl}")
                    wait(h, f"gmt_{c}_1")
                    if c >= 2:
                        wait(h, f"expb_{c - 2}")
                    i = h.activation(bv(gmtb(c), 256, 0, 8),
                                     PSf[0:8, B6 + 256:B6 + 512],
                                     AF.Copy) if h else None
                    tick(h, i, "ac", 1, f"gmtbf_{c}")
                for tt in range(4 * NCH - 4, 4 * NCH):
                    pass
                for tt in (2 * NCH - 4, 2 * NCH - 3, 2 * NCH - 2, 2 * NCH - 1):
                    ev(tt)

            elif e == "gpsimd":
                pass

        if upto >= 4:
            emit_phase(ph3c)
        if upto < 4:
            def ph_fin(e, h):
                if e != "sync":
                    return
                fins = {0: ["setup_done"],
                        1: ([f"r0e_{2*NCH_RUN-1}", f"xne_{2*NCH_RUN-1}",
                             f"T_{2*NCH_RUN-1}", f"apply_{NCH_RUN-1}",
                             f"spr0_{NCH_RUN-1}"]
                            + ([f"wlin_mm_{NCH_RUN-1}", f"spxg_{NCH_RUN-1}",
                                f"spr0_{NCH_RUN-1}"] if p01s >= 1 else [])
                            + (([f"wconv_mm_{NCH_RUN-1}"]
                                + ([f"pref_{NCH_RUN-1}"] if "pref" not in _SKIP else [])
                                + ([f"prebf_{NCH_RUN-1}"] if "prebf" not in _SKIP else []))
                               if p01s >= 2 else [])
                            + ([f"wa_mm_{NCH_RUN-1}", f"rt_{NCH_RUN-1}"] if p01s >= 3 else [])
                            + ([f"wx_mm_{NCH_RUN-1}", f"it_{NCH_RUN-1}"] if p01s >= 4 else [])
                            + ([f"spa_{NCH_RUN-1}", f"spg_{NCH_RUN-1}",
                                f"ucols_{NCH_RUN-1}"] if p01s >= 5 else [])),
                        2: ["hin_done", "G_done", "w2b"],
                        3: ["spx1_7", "w3_mm_7", "ld_e7"]}
                for u in range(upto + 1):
                    for msn in fins.get(u, []):
                        wait(h, msn)
                dump_off = getattr(sys.modules[__name__], "_DUMP", None)
                src_ap = fv(OF_IDF, 128) if dump_off is None else fv(dump_off, 128)
                d = h.dma_start(out=out[0:128, 0:128], in_=src_ap) if h else None
                v = tick(h, d, "dsta", 16)
                if h:
                    h.wait_ge(SEM["dsta"], v)
            emit_phase(ph_fin, engines=("sync",))

    return nc


def _pack_T(Wm):
    return np.ascontiguousarray(
        Wm.reshape(8, 128, 8, 128).transpose(3, 2, 0, 1).reshape(128, 8192))


def _pack_N(Wm):
    return np.ascontiguousarray(
        Wm.reshape(8, 128, 8, 128).transpose(1, 0, 2, 3).reshape(128, 8192))


def _prep_inputs(inputs):
    def f(k):
        return np.asarray(inputs[k], dtype=np.float32)
    x = np.ascontiguousarray(f("L0_attn_x").reshape(-1, D))
    com = {}
    com["wpk_lin"] = _pack_T(f("L0_attn_W_lin"))
    com["wpk_conv"] = _pack_T(f("L0_attn_W_conv"))
    com["wpk_wa"] = _pack_T(f("L0_attn_Wa"))
    com["wpk_wx"] = _pack_T(f("L0_attn_Wx"))
    com["wpk_lin2"] = _pack_T(f("L0_attn_W_lin2"))
    com["wpk_w1"] = _pack_T(f("L0_attn_W1"))
    com["wpk_w2"] = _pack_T(f("L0_attn_W2"))
    com["wpk_w3"] = _pack_T(f("L0_attn_W3"))
    ew = f("L0_ffn_expert_w")
    for e in range(8):
        com[f"wpk_e{e}"] = _pack_N(ew[e])
    com["bias1"] = np.concatenate(
        [f("L0_attn_b_lin"), f("L0_attn_b_conv"),
         f("L0_attn_ba"), f("L0_attn_bx")]).reshape(1, 4096)
    com["bias2"] = np.concatenate(
        [f("L0_attn_b_lin2"), f("L0_attn_b1"),
         f("L0_attn_b2"), f("L0_attn_b3")]).reshape(1, 4096)
    com["exb"] = np.ascontiguousarray(f("L0_ffn_expert_b"))
    com["rw"] = np.ascontiguousarray(
        f("L0_ffn_router_w").reshape(8, 128, 8).transpose(1, 0, 2).reshape(128, 64))
    com["rb"] = f("L0_ffn_router_b").reshape(1, 8)
    com["lam"] = np.ascontiguousarray(f("L0_attn_Lam").reshape(8, 128).T)
    com["g1c"] = np.ascontiguousarray(f("L0_attn_g1").reshape(8, 128).T)
    com["g2c"] = np.ascontiguousarray(f("L0_attn_g2").reshape(8, 128).T)
    oh = np.zeros((8, D), dtype=np.float32)
    for e in range(8):
        oh[e, e * 128:(e + 1) * 128] = 1.0
    com["oh"] = oh
    com["ident"] = np.eye(128, dtype=np.float32)
    in_maps = []
    for k in range(NCORES):
        m = dict(com)
        m["x"] = np.ascontiguousarray(x[k * TOK:(k + 1) * TOK, :])
        r = k % 4
        sel = np.zeros((128, 8), dtype=np.float32)
        sel[:, :r] = 1.0
        m["sel"] = sel
        m["isel"] = np.ascontiguousarray(1.0 - sel)
        in_maps.append(m)
    return in_maps


def kernel(**inputs):
    in_maps = _prep_inputs(inputs)
    nc = build(debug=False)
    res = run_bass_kernel_spmd(nc, in_maps, core_ids=list(range(NCORES)))
    outs = [res.results[k]["out"] for k in range(NCORES)]
    full = np.concatenate(outs, axis=0).reshape(2, 8192, 1024)
    return np.ascontiguousarray(full.astype(np.float32))


# version 14
# speedup vs baseline: 1.4454x; 1.0195x over previous
import sys
import numpy as np
import concourse.bass as bass
import concourse.mybir as mybir
from concourse.bass_utils import run_bass_kernel_spmd

F32 = mybir.dt.float32
BF16 = mybir.dt.bfloat16
AF = mybir.ActivationFunctionType
ALU = mybir.AluOpType
AX = mybir.AxisListType

NCORES = 8
_SKIP = set()
TOK = 2048
D = 1024
NCHT = 8
CH = 256
NCH = TOK // CH
W = NCHT * CH
EPS = 1e-5
SQD = 32.0

ARENA = 52224

OF_IDF = 0
OF_IDB = 128
OF_ONEF = 192
OF_ONEB = 320
OF_RW = 448
OF_LAM = 512
OF_G1 = 520
OF_G2 = 528
OF_SEL = 536
OF_ISEL = 544
OF_LAC = 552
OF_RTS = 560
OF_TMPC = 568
OF_AUC = 600
OF_G = 616
OF_HIN = 680
OF_CCS = 688
OF_CCL = 816
OF_EXB = 944
OF_OH = 1456
OF_BIAS = 1968
OF_RB = 4016
OF_SS = 4032
OF_SQ = 4048
OF_MC = 4064
OF_VAR = 4080
OF_SVE = 4096
OF_Q1 = 4112
OF_RS = 4128
OF_C1 = 4144
OF_M2T = 4160
OF_ROWS = 4176
OF_GM = 5456
OF_SCOL = 5472
OF_GMT = 5504
OF_HCOL = 5760
OF_EXPS = 5768
OF_PERS_END = 5784

OF_T = [5792 + i * 4096 for i in range(8)]
OF_SCR = OF_T[7] + 4096
assert OF_SCR == 38560
SCR_END = ARENA


def _alloc(start, end, sizes):
    out = {}
    o = start
    for name, n in sizes:
        out[name] = o
        o += n
    assert o <= end, (o, end)
    return out


P1 = _alloc(OF_SCR, SCR_END, [
    ("B2", W), ("B3", W), ("B4", W), ("B5", W),
    ("r0fm", W // 2), ("xgelbf", W // 2),
])
P1.update(_alloc(OF_T[4], OF_T[6], [
    ("x0a", D), ("x0b", D), ("r0ta", D // 2), ("r0tb", D // 2),
    ("xna", D // 2), ("xnb", D // 2), ("sqscr", D),
    ("xnfa", W // 2), ("xnfb", W // 2),
]))
P1["stage"] = OF_T[6]
P1.update(_alloc(OF_T[6], OF_T[7] + 4096, [
    ("B1", W), ("linbf", W // 2), ("prebf", W // 2),
]))

P3 = _alloc(OF_SCR, SCR_END, [
    ("rga", 3 * W), ("rgb", 3 * W),
])
P3.update(_alloc(OF_T[0], OF_T[4], [
    ("h", W), ("mixa", W // 2), ("mixb", W // 2), ("r1", W), ("xmbf", W // 2),
    ("sig", W), ("x1p", W), ("xga", W // 2), ("xgb", W // 2),
    ("s2t", CH), ("r1b", W),
]))

PC = _alloc(OF_SCR, SCR_END, [
    ("x0a", D), ("x0b", D),
    ("x1A", W), ("x1B", W),
    ("xt", W), ("xtbf", W // 2), ("sqbf", W // 2),
    ("xg0", W // 2), ("xg1", W // 2),
    ("gmt2", 256), ("y", D),
])


EPERM = [4, 5, 6, 7, 0, 1, 2, 3]


class Sched:
    def __init__(self):
        self.cnt = {}
        self.ms = {}

    def inc(self, sem, n, ms=None):
        v = self.cnt[sem] = self.cnt.get(sem, 0) + n
        if ms is not None:
            self.ms[ms] = (sem, v)
        return v


def build(debug=False, upto=4, p01n=None, p01s=5):
    nc = bass.Bass("TRN2", target_bir_lowering=False,
                   detect_race_conditions=False)

    x_in = nc.dram_tensor("x", [TOK, D], F32, kind="ExternalInput").ap()
    wname = ["lin", "conv", "wa", "wx", "lin2", "w1", "w2", "w3"] + \
            [f"e{i}" for i in range(8)]
    wpk = {n: nc.dram_tensor("wpk_" + n, [128, 8192], F32, kind="ExternalInput").ap()
           for n in wname}
    bias1 = nc.dram_tensor("bias1", [1, 4096], F32, kind="ExternalInput").ap()
    bias2 = nc.dram_tensor("bias2", [1, 4096], F32, kind="ExternalInput").ap()
    exb_in = nc.dram_tensor("exb", [8, D], F32, kind="ExternalInput").ap()
    rw_in = nc.dram_tensor("rw", [128, 64], F32, kind="ExternalInput").ap()
    rb_in = nc.dram_tensor("rb", [1, 8], F32, kind="ExternalInput").ap()
    lam_in = nc.dram_tensor("lam", [128, 8], F32, kind="ExternalInput").ap()
    g1_in = nc.dram_tensor("g1c", [128, 8], F32, kind="ExternalInput").ap()
    g2_in = nc.dram_tensor("g2c", [128, 8], F32, kind="ExternalInput").ap()
    sel_in = nc.dram_tensor("sel", [128, 8], F32, kind="ExternalInput").ap()
    isel_in = nc.dram_tensor("isel", [128, 8], F32, kind="ExternalInput").ap()
    oh_in = nc.dram_tensor("oh", [8, D], F32, kind="ExternalInput").ap()
    id_in = nc.dram_tensor("ident", [128, 128], F32, kind="ExternalInput").ap()
    out = nc.dram_tensor("out", [TOK, D], F32, kind="ExternalOutput").ap()

    skind = "ExternalOutput" if debug else "Internal"
    scr_r0 = nc.dram_tensor("scr_r0", [128, NCH * W], BF16, kind=skind).ap()
    scr_a = nc.dram_tensor("scr_a", [128, NCH * W], F32, kind=skind).ap()
    scr_g = nc.dram_tensor("scr_g", [128, NCH * W], F32, kind=skind).ap()
    scr_xg = nc.dram_tensor("scr_xg", [128, NCH * W], BF16, kind=skind).ap()
    scr_x1p = nc.dram_tensor("scr_x1p", [128, NCH * W], F32, kind=skind).ap()
    cc_in = nc.dram_tensor("cc_in", [16, 128], F32).ap()
    cc_out = nc.dram_tensor("cc_out", [64, 128], F32).ap()

    arena_h = nc.alloc_sbuf_tensor("arena", [128, ARENA], F32)
    A = arena_h.ap()
    Ab = A.bitcast(BF16)

    def fv(off, n, p0=0, p1=128):
        return A[p0:p1, off:off + n]

    def bv(off, nbf, p0=0, p1=128):
        return Ab[p0:p1, 2 * off:2 * off + nbf]

    def f3(off):
        return fv(off, W).rearrange("p (a b) -> p a b", a=NCHT)

    def b3(off):
        return bv(off, W).rearrange("p (a b) -> p a b", a=NCHT)

    sch = Sched()

    import contextlib
    SEM_NAMES = (["pe", "ac", "dv", "cc",
                  "dxa", "dxb",
                  "dW", "dw2",
                  "dxg", "dr0", "dx1",
                  "dsa", "dsg",
                  "dr3a", "dr3b",
                  "dpa", "dpb",
                  "dsta", "dstb",
                  "dcc"]
                 + [f"de{i}" for i in range(8)])
    with contextlib.ExitStack() as _es:
        PS = _es.enter_context(nc.psum_tensor("PS", [128, 4096], F32))
        SEM = {n: _es.enter_context(nc.semaphore(n)) for n in SEM_NAMES}
        S_cc = SEM["cc"]
        block = _es.enter_context(nc.Block())
        PSf = PS[:]
        PSb = PSf.bitcast(BF16)

        def bank(b, n=1):
            return PSf[:, b * 512:(b + n) * 512]

        def emit_phase(fn, engines=("sync", "gpsimd", "tensor", "scalar", "vector")):
            snap0 = dict(sch.cnt)
            for e in engines:
                fn(e, None)
            snap1 = dict(sch.cnt)
            emitters = {"sync": block.sync, "gpsimd": block.gpsimd,
                        "tensor": block.tensor, "scalar": block.scalar,
                        "vector": block.vector}
            for e in engines:
                sch.cnt = dict(snap0)
                emitters[e](lambda h, e=e: fn(e, h))
            sch.cnt = dict(snap1)

        def wait(h, ms):
            if h is not None:
                sem, v = sch.ms[ms]
                h.wait_ge(SEM[sem], v)

        def tick(h, inst, sem, n, ms=None):
            if h is not None and inst is not None:
                inst.then_inc(SEM[sem], n)
            return sch.inc(sem, n, ms)

        def fence(h, inst, eng):
            v = sch.inc(eng, 1)
            if h is not None and inst is not None:
                inst.then_inc(SEM[eng], 1)
                h.wait_ge(SEM[eng], v)
            return v

        def full_mm(h, slot_off, rhs_off, b0):
            last = None
            r3 = b3(rhs_off)
            for m in range(8):
                for k in range(8):
                    if h:
                        last = h.matmul(
                            PSf[:, b0 * 512 + m * CH:b0 * 512 + m * CH + CH],
                            bv(slot_off + (k * 8 + m) * 64, 128),
                            r3[:, k, :],
                            start=(k == 0 and m % 2 == 0), stop=False)
            return last

        def full_mm_h(h, slot_off, rhs_off, b0, ms0, ms1):
            last = None
            r3 = b3(rhs_off)
            for m in range(8):
                if m == 0:
                    wait(h, ms0)
                if m == 4:
                    wait(h, ms1)
                for k in range(8):
                    if h:
                        last = h.matmul(
                            PSf[:, b0 * 512 + m * CH:b0 * 512 + m * CH + CH],
                            bv(slot_off + (k * 8 + m) * 64, 128),
                            r3[:, k, :],
                            start=(k == 0 and m % 2 == 0), stop=False)
            return last

        def bias_mm(h, bslot, b0, last):
            for m in range(8):
                if h:
                    last = h.matmul(
                        PSf[:, b0 * 512 + m * CH:b0 * 512 + m * CH + CH],
                        bv(OF_BIAS + bslot * 512 + m * 64, 128, 0, 1),
                        bv(OF_ONEB, CH, 0, 1),
                        start=False, stop=(m % 2 == 1))
            return last

        def ph_setup(e, h):
            if e == "gpsimd":
                def dm(o, i):
                    return h.dma_start(out=o, in_=i) if h else None
                tick(h, dm(fv(OF_IDF, 128), id_in), "dW", 16)
                tick(h, dm(bv(OF_IDB, 128), id_in), "dW", 16)
                tick(h, dm(fv(OF_RW, 64), rw_in), "dW", 16)
                tick(h, dm(fv(OF_RB, 8, 0, 1), rb_in), "dW", 16)
                tick(h, dm(fv(OF_LAM, 8), lam_in), "dW", 16)
                tick(h, dm(fv(OF_G1, 8), g1_in), "dW", 16)
                tick(h, dm(fv(OF_G2, 8), g2_in), "dW", 16)
                tick(h, dm(fv(OF_SEL, 8), sel_in), "dW", 16)
                tick(h, dm(fv(OF_ISEL, 8), isel_in), "dW", 16)
                tick(h, dm(bv(OF_EXB, 1024, 0, 8), exb_in), "dW", 16)
                tick(h, dm(bv(OF_OH, 1024, 0, 8), oh_in), "dW", 16)
                tick(h, dm(bv(OF_BIAS, 4096, 0, 1), bias1), "dW", 16)
                tick(h, dm(fv(P1["stage"], 4096), wpk["lin"][:, 0:4096]), "dW", 16)
                tick(h, dm(fv(P1["stage"] + 4096, 4096), wpk["lin"][:, 4096:8192]),
                     "dW", 16)
                tick(h, dm(bv(OF_T[1], 8192), wpk["conv"]), "dW", 16)
                tick(h, dm(bv(OF_T[2], 8192), wpk["wa"]), "dW", 16)
                tick(h, dm(bv(OF_T[3], 8192), wpk["wx"]), "dW", 16, "setup_done")
            elif e == "scalar":
                wait(h, "setup_done")
                i = h.activation(fv(OF_TMPC, 8), fv(OF_LAM, 8), AF.Sigmoid) if h else None
                fence(h, i, "ac")
                i = h.activation(fv(OF_LAC, 8), fv(OF_TMPC, 8), AF.Ln) if h else None
                tick(h, i, "ac", 1, "lac_ln")
            elif e == "vector":
                i = None
                if h:
                    h.memset(fv(OF_ONEF, 128), 1.0)
                    h.memset(bv(OF_ONEB, 256), 1.0)
                    i = h.memset(fv(OF_RTS, 8), 0.0)
                tick(h, i, "dv", 1)
                wait(h, "setup_done")
                last = None
                for k in range(8):
                    if h:
                        last = h.tensor_scalar(
                            bv(OF_T[0] + k * 512, 1024),
                            fv(P1["stage"] + k * 1024, 1024),
                            fv(OF_G1 + k, 1), None, ALU.mult)
                tick(h, last, "dv", 1, "fold_lin")
                wait(h, "lac_ln")
                i = h.tensor_scalar(fv(OF_LAC, 8), fv(OF_LAC, 8), 0.125,
                                    None, ALU.mult) if h else None
                tick(h, i, "dv", 1, "lac_done")

        emit_phase(ph_setup, engines=("gpsimd", "scalar", "vector"))

        def x0s(tt):
            return P1["x0a"] if tt % 2 == 0 else P1["x0b"]

        def r0ts(tt):
            return P1["r0ta"] if tt % 2 == 0 else P1["r0tb"]

        def xns(tt):
            return P1["xna"] if tt % 2 == 0 else P1["xnb"]

        def xnfm(c):
            return P1["xnfa"] if c % 2 == 0 else P1["xnfb"]

        def ph01(e, h):
            def fe_act1(cc):
                for tl in range(2):
                    tt = 2 * cc + tl
                    wait(h, f"x0_{tt}")
                    if tt >= 1:
                        wait(h, f"ssq_{tt - 1}")
                    i = h.activation(fv(P1["sqscr"], D), fv(x0s(tt), D),
                                     AF.Square) if h else None
                    tick(h, i, "ac", 1, f"sq_{tt}")

            def fe_act2(cc):
                wait(h, f"var_{cc}")
                i = None
                if h:
                    h.activation(fv(OF_SVE + 2 * cc, 2), fv(OF_VAR + 2 * cc, 2),
                                 AF.Sqrt, bias=np.float32(EPS))
                    i = h.activation(fv(OF_Q1 + 2 * cc, 2), fv(OF_VAR + 2 * cc, 2),
                                     AF.Sqrt, scale=np.float32(D))
                tick(h, i, "ac", 1, f"sveq_{cc}")

            def fe_act3(cc):
                for tl in range(2):
                    tt = 2 * cc + tl
                    wait(h, f"T_{tt}")
                    if cc >= 1 and tl == 0:
                        wait(h, f"spr0_{cc - 1}")
                    rb = 4096 if tt % 2 == 0 else 6144
                    xb = 5120 if tt % 2 == 0 else 7168
                    i = h.activation(
                        bv(P1["r0fm"], W).rearrange("p (a b) -> p a b", a=8)
                        [:, :, tl * 128:tl * 128 + 128],
                        PSb[:, rb:rb + 1024].rearrange("p (a b) -> p a b", a=8),
                        AF.Copy) if h else None
                    tick(h, i, "ac", 1, f"r0e_{tt}")
                    i = h.activation(
                        b3(xnfm(cc))[:, :, tl * 128:tl * 128 + 128],
                        PSb[:, xb:xb + 1024].rearrange("p (a b) -> p a b", a=8),
                        AF.Copy) if h else None
                    if tl == 1:
                        fence(h, i, "ac")
                        sch.ms[f"xne_{tt}"] = ("ac", sch.cnt["ac"])
                    else:
                        tick(h, i, "ac", 1, f"xne_{tt}")
                d = h.dma_start(out=scr_r0[:, cc * W:(cc + 1) * W],
                                in_=bv(P1["r0fm"], W)) if h else None
                tick(h, d, "dr0", 16, f"spr0_{cc}")

            def fe_dve1(cc):
                for tl in range(2):
                    tt = 2 * cc + tl
                    wait(h, f"x0_{tt}")
                    i = h.reduce_sum(fv(OF_SS + tt, 1), fv(x0s(tt), D),
                                     AX.X) if h else None
                    tick(h, i, "dv", 1, f"ssum_{tt}")
                    wait(h, f"sq_{tt}")
                    i = h.reduce_sum(fv(OF_SQ + tt, 1), fv(P1["sqscr"], D),
                                     AX.X) if h else None
                    tick(h, i, "dv", 1, f"ssq_{tt}")
                i = h.tensor_scalar(fv(OF_MC + 2 * cc, 2), fv(OF_SS + 2 * cc, 2),
                                    1.0 / D, None, ALU.mult) if h else None
                fence(h, i, "dv")
                i = h.tensor_tensor(fv(OF_M2T + 2 * cc, 2), fv(OF_MC + 2 * cc, 2),
                                    fv(OF_MC + 2 * cc, 2), ALU.mult) if h else None
                fence(h, i, "dv")
                i = h.scalar_tensor_tensor(
                    fv(OF_VAR + 2 * cc, 2), fv(OF_SQ + 2 * cc, 2), 1.0 / D,
                    fv(OF_M2T + 2 * cc, 2), ALU.mult, ALU.subtract) if h else None
                tick(h, i, "dv", 1, f"var_{cc}")

            def fe_dve2(cc):
                wait(h, f"sveq_{cc}")
                i = h.reciprocal(fv(OF_RS + 2 * cc, 2), fv(OF_SVE + 2 * cc, 2)) if h else None
                fence(h, i, "dv")
                i = h.tensor_tensor(fv(OF_M2T + 2 * cc, 2), fv(OF_Q1 + 2 * cc, 2),
                                    fv(OF_RS + 2 * cc, 2), ALU.mult) if h else None
                fence(h, i, "dv")
                i = h.tensor_scalar(fv(OF_M2T + 2 * cc, 2), fv(OF_M2T + 2 * cc, 2),
                                    1e-12, None, ALU.max) if h else None
                fence(h, i, "dv")
                i = h.reciprocal(fv(OF_M2T + 2 * cc, 2), fv(OF_M2T + 2 * cc, 2)) if h else None
                fence(h, i, "dv")
                i = h.scalar_tensor_tensor(
                    fv(OF_C1 + 2 * cc, 2), fv(OF_M2T + 2 * cc, 2), SQD,
                    fv(OF_RS + 2 * cc, 2), ALU.mult, ALU.mult) if h else None
                fence(h, i, "dv")
                for tl in range(2):
                    tt = 2 * cc + tl
                    if tt >= 2:
                        wait(h, f"T_{tt - 2}")
                    i = None
                    if h:
                        h.tensor_scalar(bv(r0ts(tt), 1024), fv(x0s(tt), D),
                                        fv(OF_MC + tt, 1), fv(OF_RS + tt, 1),
                                        ALU.subtract, ALU.mult)
                        i = h.tensor_scalar(bv(xns(tt), 1024), fv(x0s(tt), D),
                                            fv(OF_MC + tt, 1), fv(OF_C1 + tt, 1),
                                            ALU.subtract, ALU.mult)
                    tick(h, i, "dv", 1, f"apply_{cc}" if tl == 1 else None)

            if e == "sync":
                for tt in range(2 * NCH_RUN):
                    if tt >= 2:
                        wait(h, f"apply_{tt // 2 - 1}")
                        wait(h, f"sq_{tt - 2}")
                    d = h.dma_start(out=fv(x0s(tt), D),
                                    in_=x_in[tt * 128:(tt + 1) * 128, :]) if h else None
                    tick(h, d, "dxa" if tt % 2 == 0 else "dxb", 16, f"x0_{tt}")

            elif e == "vector":
                for s in range(-2, NCH_RUN):
                    if s + 2 < NCH_RUN:
                        fe_dve1(s + 2)
                    if s >= 0:
                        wait(h, f"prebf_{s}")
                        if s >= 1:
                            wait(h, f"spg_{s - 1}")
                        i = h.tensor_copy(fv(P1["B1"], W), bank(0, 4)) if h else None
                        tick(h, i, "dv", 1, f"pref_{s}")
                    if s + 2 < NCH_RUN:
                        fe_dve2(s + 2)
                    if s < 0:
                        continue
                    c = s
                    wait(h, f"rt_{c}")
                    last = None
                    for k in range(NCHT):
                        if h:
                            last = h.tensor_scalar(
                                fv(P1["B2"] + k * CH, CH), fv(P1["B2"] + k * CH, CH),
                                fv(OF_LAC + k, 1), None, ALU.mult)
                    fence(h, last, "dv")
                    sch.ms[f"lart_{c}"] = ("dv", sch.cnt["dv"])
                    i = h.tensor_reduce(fv(OF_TMPC, 8), f3(P1["B2"]), AX.X, ALU.add) if h else None
                    fence(h, i, "dv")
                    i = h.tensor_tensor(fv(OF_RTS, 8), fv(OF_RTS, 8),
                                        fv(OF_TMPC, 8), ALU.add) if h else None
                    tick(h, i, "dv", 1, f"lasum_{c}")
                    wait(h, f"it_{c}")
                    wait(h, f"sq1ma_{c}")
                    i = h.tensor_tensor(fv(P1["B1"], W), fv(P1["B3"], W),
                                        fv(P1["B1"], W), ALU.mult) if h else None
                    fence(h, i, "dv")
                    i = h.tensor_tensor(fv(P1["B1"], W), fv(P1["B1"], W),
                                        fv(P1["B5"], W), ALU.mult) if h else None
                    fence(h, i, "dv")
                    sch.ms[f"g_{c}"] = ("dv", sch.cnt["dv"])
                    last = None
                    for k in range(NCHT):
                        init = 0.0 if c == 0 else fv(OF_HCOL + k, 1)
                        if h:
                            last = h.tensor_tensor_scan(
                                fv(P1["B2"] + k * CH, CH), fv(P1["B4"] + k * CH, CH),
                                fv(P1["B1"] + k * CH, CH), init, ALU.mult, ALU.add)
                    fence(h, last, "dv")
                    sch.ms[f"uscan_{c}"] = ("dv", sch.cnt["dv"])
                    i = h.tensor_copy(fv(OF_HCOL, 8), f3(P1["B2"])[:, :, CH - 1]) if h else None
                    tick(h, i, "dv", 1, f"ucols_{c}")

            elif e == "scalar":
                for s in range(-2, NCH_RUN):
                    if s + 2 < NCH_RUN:
                        fe_act1(s + 2)
                    if s >= 0:
                        c = s
                        wait(h, f"wlin_mm_{c}")
                        i = h.activation(bv(P1["linbf"], W), bank(0, 4), AF.Copy) if h else None
                        tick(h, i, "ac", 1, f"lin_{c}")
                        if c >= 1:
                            wait(h, f"spxg_{c - 1}")
                        i = h.activation(bv(P1["xgelbf"], W), bv(P1["linbf"], W),
                                         AF.Gelu) if h else None
                        fence(h, i, "ac")
                        sch.ms[f"xgel_{c}"] = ("ac", sch.cnt["ac"])
                        d = h.dma_start(out=scr_xg[:, c * W:(c + 1) * W],
                                        in_=bv(P1["xgelbf"], W)) if h else None
                        tick(h, d, "dxg", 16, f"spxg_{c}")
                        wait(h, f"wconv_mm_{c}")
                        i = h.activation(bv(P1["prebf"], W), bank(0, 4), AF.Copy) if h else None
                        tick(h, i, "ac", 1, f"prebf_{c}")
                    if s + 2 < NCH_RUN:
                        fe_act2(s + 2)
                        fe_act3(s + 2)
                    if s < 0:
                        continue
                    c = s
                    wait(h, f"wa_mm_{c}")
                    if c >= 1:
                        wait(h, f"ucols_{c - 1}")
                    i = h.activation(fv(P1["B2"], W // 2), bank(0, 2),
                                     AF.Sigmoid) if h else None
                    tick(h, i, "ac", 1, f"rt0_{c}")
                    i = h.activation(fv(P1["B2"] + W // 2, W // 2), bank(2, 2),
                                     AF.Sigmoid) if h else None
                    tick(h, i, "ac", 1, f"rt_{c}")
                    wait(h, f"wx_mm_{c}")
                    if c >= 1:
                        wait(h, f"g_{c - 1}")
                    i = h.activation(fv(P1["B3"], W // 2), bank(0, 2),
                                     AF.Sigmoid) if h else None
                    tick(h, i, "ac", 1, f"it0_{c}")
                    i = h.activation(fv(P1["B3"] + W // 2, W // 2), bank(2, 2),
                                     AF.Sigmoid) if h else None
                    tick(h, i, "ac", 1, f"it_{c}")
                    wait(h, f"lart_{c}")
                    if c >= 1:
                        wait(h, f"spa_{c - 1}")
                        wait(h, f"uscan_{c - 1}")
                    i = h.activation(fv(P1["B4"], W), fv(P1["B2"], W), AF.Exp) if h else None
                    fence(h, i, "ac")
                    sch.ms[f"a_{c}"] = ("ac", sch.cnt["ac"])
                    wait(h, f"lasum_{c}")
                    i = h.activation(fv(P1["B2"], W), fv(P1["B4"], W), AF.Square) if h else None
                    fence(h, i, "ac")
                    sch.ms[f"asq_{c}"] = ("ac", sch.cnt["ac"])
                    i = h.activation(fv(P1["B5"], W), fv(P1["B2"], W), AF.Sqrt,
                                     scale=np.float32(-1.0), bias=np.float32(1.0)) if h else None
                    tick(h, i, "ac", 1, f"sq1ma_{c}")

            elif e == "tensor":
                for s in range(-2, NCH_RUN):
                    if s >= 0:
                        c = s
                        wait(h, f"xne_{2 * c + 1}")
                        if c == 0:
                            wait(h, "fold_lin")
                            last = full_mm(h, OF_T[0], xnfm(c), 0)
                        else:
                            last = full_mm_h(h, OF_T[0], xnfm(c), 0,
                                             f"it0_{c - 1}", f"it_{c - 1}")
                        last = bias_mm(h, 0, 0, last)
                        tick(h, last, "pe", 1, f"wlin_mm_{c}")
                        wait(h, f"lin_{c}")
                        last = full_mm(h, OF_T[1], P1["linbf"], 0)
                        last = bias_mm(h, 1, 0, last)
                        tick(h, last, "pe", 1, f"wconv_mm_{c}")
                        wait(h, f"prebf_{c}")
                        wait(h, f"pref_{c}")
                        last = full_mm(h, OF_T[2], P1["prebf"], 0)
                        last = bias_mm(h, 2, 0, last)
                        tick(h, last, "pe", 1, f"wa_mm_{c}")
                    if s + 2 < NCH_RUN:
                        cc = s + 2
                        for tl in range(2):
                            tt = 2 * cc + tl
                            wait(h, f"apply_{cc}")
                            if tt >= 2:
                                wait(h, f"r0e_{tt - 2}")
                                wait(h, f"xne_{tt - 2}")
                            if cc >= 2 and tl == 0:
                                wait(h, f"wlin_mm_{cc - 2}")
                            rb = 4096 if tt % 2 == 0 else 6144
                            xb = 5120 if tt % 2 == 0 else 7168
                            last = None
                            for k in range(NCHT):
                                if h:
                                    h.transpose(PSb[:, rb + k * 128:rb + k * 128 + 128],
                                                bv(r0ts(tt) + k * 64, 128),
                                                bv(OF_IDB, 128))
                                    last = h.transpose(
                                        PSb[:, xb + k * 128:xb + k * 128 + 128],
                                        bv(xns(tt) + k * 64, 128),
                                        bv(OF_IDB, 128))
                            tick(h, last, "pe", 1, f"T_{tt}")
                    if s >= 0:
                        c = s
                        last = full_mm_h(h, OF_T[3], P1["prebf"], 0,
                                         f"rt0_{c}", f"rt_{c}")
                        last = bias_mm(h, 3, 0, last)
                        tick(h, last, "pe", 1, f"wx_mm_{c}")

            elif e == "gpsimd":
                for c in range(NCH_RUN):
                    wait(h, f"uscan_{c}")
                    d = h.dma_start(out=scr_a[:, c * W:(c + 1) * W],
                                    in_=fv(P1["B4"], W)) if h else None
                    tick(h, d, "dsa", 16, f"spa_{c}")
                    wait(h, f"g_{c}")
                    d = h.dma_start(out=scr_g[:, c * W:(c + 1) * W],
                                    in_=fv(P1["B1"], W)) if h else None
                    tick(h, d, "dsg", 16, f"spg_{c}")

        NCH_RUN = NCH if p01n is None else p01n
        if upto >= 1:
            emit_phase(ph01)

        def ph2(e, h):
            if e == "scalar":
                wait(h, "lasum_7")
                i = h.activation(fv(OF_AUC, 8), fv(OF_RTS, 8), AF.Exp) if h else None
                tick(h, i, "ac", 1, "Acols")
                wait(h, "Ct")
                i = h.activation(fv(OF_CCS, 128, 0, 16),
                                 PSf[0:16, 3072:3072 + 128], AF.Copy) if h else None
                tick(h, i, "ac", 1, "ccs_done")
                wait(h, "Gt")
                i = h.activation(fv(OF_G, 64), PSf[:, 3072:3072 + 64],
                                 AF.Copy) if h else None
                tick(h, i, "ac", 1, "G_done")
            elif e == "vector":
                wait(h, "ucols_7")
                i = h.tensor_copy(fv(OF_AUC + 8, 8), fv(OF_HCOL, 8)) if h else None
                tick(h, i, "dv", 1, "Ucols")
                wait(h, "G_done")
                wait(h, "setup_done")
                i = h.memset(fv(OF_HIN, 8), 0.0) if h else None
                tick(h, i, "dv", 1)
                for j in range(4):
                    gA = fv(OF_G + j * 16, 8)
                    gU = fv(OF_G + j * 16 + 8, 8)
                    i = h.tensor_scalar(fv(OF_TMPC, 8), gA, fv(OF_SEL + j, 1),
                                        fv(OF_ISEL + j, 1), ALU.mult, ALU.add) if h else None
                    fence(h, i, "dv")
                    i = h.tensor_tensor(fv(OF_HIN, 8), fv(OF_TMPC, 8),
                                        fv(OF_HIN, 8), ALU.mult) if h else None
                    fence(h, i, "dv")
                    i = h.tensor_scalar(fv(OF_TMPC, 8), gU, fv(OF_SEL + j, 1),
                                        None, ALU.mult) if h else None
                    fence(h, i, "dv")
                    i = h.tensor_tensor(fv(OF_HIN, 8), fv(OF_HIN, 8),
                                        fv(OF_TMPC, 8), ALU.add) if h else None
                    fence(h, i, "dv")
                tick(h, None, "dv", 0, "hin_done")
                sch.ms["hin_done"] = ("dv", sch.cnt["dv"])
            elif e == "tensor":
                wait(h, "Acols")
                wait(h, "Ucols")
                wait(h, "xne_15")
                i = h.transpose(PSf[0:16, 3072:3072 + 128],
                                fv(OF_AUC, 16), fv(OF_IDF, 128)) if h else None
                tick(h, i, "pe", 1, "Ct")
                wait(h, "ccl_done")
                wait(h, "ccs_done")
                i = h.transpose(PSf[:, 3072:3072 + 64],
                                fv(OF_CCL, 128, 0, 64),
                                fv(OF_IDF, 64, 0, 64)) if h else None
                tick(h, i, "pe", 1, "Gt")
            elif e == "gpsimd":
                wait(h, "ccs_done")
                d = h.dma_start(out=cc_in[:, :], in_=fv(OF_CCS, 128, 0, 16)) if h else None
                v = tick(h, d, "dcc", 16)
                if h:
                    h.wait_ge(SEM["dcc"], v)
                i = h.collective_compute(
                    "AllGather", ALU.bypass,
                    replica_groups=[[0, 1, 2, 3], [4, 5, 6, 7]],
                    ins=[cc_in.opt()],
                    outs=[cc_out.opt()]) if h else None
                v = tick(h, i, "cc", 1)
                if h:
                    h.wait_ge(S_cc, v)
                d = h.dma_start(out=fv(OF_CCL, 128, 0, 64), in_=cc_out[:, :]) if h else None
                tick(h, d, "dcc", 16, "ccl_done")
                wait(h, "T_15")
                wait(h, "wx_mm_7")
                d = h.dma_start(out=bv(OF_T[4], 8192), in_=wpk["lin2"]) if h else None
                tick(h, d, "dw2", 16)
                d = h.dma_start(out=bv(OF_T[5], 8192), in_=wpk["w1"]) if h else None
                tick(h, d, "dw2", 16)
                d = h.dma_start(out=bv(OF_T[6], 8192), in_=wpk["w2"]) if h else None
                tick(h, d, "dw2", 16)
                d = h.dma_start(out=bv(OF_T[7], 8192), in_=wpk["w3"]) if h else None
                tick(h, d, "dw2", 16)
                d = h.dma_start(out=bv(OF_BIAS, 4096, 0, 1), in_=bias2) if h else None
                tick(h, d, "dw2", 16, "w2b")

        if upto >= 2:
            emit_phase(ph2, engines=("scalar", "vector", "tensor", "gpsimd"))

        def rg(c):
            return P3["rga"] if c % 2 == 0 else P3["rgb"]

        def xg3(c):
            return P3["xga"] if c % 2 == 0 else P3["xgb"]

        def mx3(c):
            return P3["mixa"] if c % 2 == 0 else P3["mixb"]

        def ph3ab(e, h):
            if e == "sync":
                for c in range(NCH):
                    base = rg(c)
                    wait(h, "spa_7")
                    wait(h, "spg_7")
                    wait(h, "spr0_7")
                    wait(h, "spxg_7")
                    if c >= 2:
                        wait(h, f"r1stt_{c - 2}")
                        wait(h, f"mix_{c - 2}")
                    d = h.dma_start(out=fv(base, W),
                                    in_=scr_a[:, c * W:(c + 1) * W]) if h else None
                    tick(h, d, "dr3a" if c % 2 == 0 else "dr3b", 16)
                    d = h.dma_start(out=fv(base + W, W),
                                    in_=scr_g[:, c * W:(c + 1) * W]) if h else None
                    tick(h, d, "dr3a" if c % 2 == 0 else "dr3b", 16)
                    d = h.dma_start(out=bv(base + 2 * W, W),
                                    in_=scr_r0[:, c * W:(c + 1) * W]) if h else None
                    tick(h, d, "dr3a" if c % 2 == 0 else "dr3b", 16)
                    d = h.dma_start(out=bv(xg3(c), W),
                                    in_=scr_xg[:, c * W:(c + 1) * W]) if h else None
                    tick(h, d, "dr3a" if c % 2 == 0 else "dr3b", 16, f"ld3_{c}")

            elif e == "vector":
                def r1b(c):
                    return P3["r1"] if c % 2 == 0 else P3["r1b"]

                def dv_hmix(cc):
                    wait(h, f"ld3_{cc}")
                    if cc == 0:
                        wait(h, "hin_done")
                    last = None
                    for k in range(NCHT):
                        init = fv(OF_HIN + k, 1) if cc == 0 else fv(OF_HCOL + k, 1)
                        if h:
                            last = h.tensor_tensor_scan(
                                fv(P3["h"] + k * CH, CH), fv(rg(cc) + k * CH, CH),
                                fv(rg(cc) + W + k * CH, CH), init, ALU.mult, ALU.add)
                    fence(h, last, "dv")
                    sch.ms[f"hscan_{cc}"] = ("dv", sch.cnt["dv"])
                    i = h.tensor_copy(fv(OF_HCOL, 8), f3(P3["h"])[:, :, CH - 1]) if h else None
                    fence(h, i, "dv")
                    sch.ms[f"hcols_{cc}"] = ("dv", sch.cnt["dv"])
                    if cc >= 2:
                        wait(h, f"stats2_mm_{cc - 2}")
                    i = h.tensor_tensor(bv(mx3(cc), W), bv(xg3(cc), W),
                                        fv(P3["h"], W), ALU.mult) if h else None
                    tick(h, i, "dv", 1, f"mix_{cc}")

                def dv_tail(cc):
                    wait(h, f"gelu_{cc}")
                    wait(h, f"w2_mm_{cc}")
                    i = h.tensor_tensor(bv(P3["xmbf"], W), fv(P3["sig"], W),
                                        bank(0, 4), ALU.mult) if h else None
                    tick(h, i, "dv", 1, f"comb_{cc}")
                    wait(h, f"w3_mm_{cc}")
                    if cc >= 1:
                        wait(h, f"spx1_{cc - 1}")
                    i = h.tensor_tensor(fv(P3["x1p"], W), bank(0, 4),
                                        fv(r1b(cc), W), ALU.add) if h else None
                    tick(h, i, "dv", 1, f"x1p_{cc}")

                dv_hmix(0)
                for c in range(NCH):
                    wait(h, f"wlin2_mm_{c}")
                    i = h.scalar_tensor_tensor(fv(r1b(c), W), PSf[:, 2048:4096],
                                               1.0, bv(rg(c) + 2 * W, W), ALU.mult,
                                               ALU.add) if h else None
                    tick(h, i, "dv", 1, f"r1stt_{c}")
                    if c + 1 < NCH:
                        dv_hmix(c + 1)
                    if c >= 1:
                        dv_tail(c - 1)
                    wait(h, f"sveq2_{c}")
                    i = h.tensor_scalar(fv(P3["s2t"], CH), fv(P3["s2t"], CH),
                                        1e-12 / SQD, None, ALU.max) if h else None
                    fence(h, i, "dv")
                    i = h.reciprocal(fv(P3["s2t"], CH), fv(P3["s2t"], CH)) if h else None
                    fence(h, i, "dv")
                    sch.ms[f"s2row_{c}"] = ("dv", sch.cnt["dv"])
                    last = None
                    for k in range(NCHT):
                        if h:
                            last = h.scalar_tensor_tensor(
                                bv(P3["xmbf"] + k * 128, 256),
                                fv(r1b(c) + k * CH, CH), fv(OF_G2 + k, 1),
                                fv(P3["s2t"], CH), ALU.mult, ALU.mult)
                    tick(h, last, "dv", 1, f"xm_{c}")
                dv_tail(NCH - 1)

            elif e == "tensor":
                def emit_wlin2(c2):
                    wait(h, f"mix_{c2}")
                    if c2 == 0:
                        wait(h, "w2b")
                    if c2 >= 1:
                        wait(h, f"r1stt_{c2 - 1}")
                    last = full_mm(h, OF_T[4], mx3(c2), 4)
                    last = bias_mm(h, 0, 4, last)
                    tick(h, last, "pe", 1, f"wlin2_mm_{c2}")

                emit_wlin2(0)
                for c in range(1, NCH + 1):
                    cc = c - 1
                    wait(h, f"sq2_{cc}")
                    if cc >= 1:
                        wait(h, f"x1p_{cc - 1}")
                        wait(h, f"sveq2_{cc - 1}")
                    last = None
                    for k in range(NCHT):
                        if h:
                            last = h.matmul(
                                PSf[:, 0:256],
                                bv(OF_ONEB, 128),
                                bv(mx3(cc) + k * 128, 256),
                                start=(k == 0), stop=(k == 7))
                    tick(h, last, "pe", 1, f"stats2_mm_{cc}")
                    if cc + 1 < NCH:
                        emit_wlin2(cc + 1)
                    wait(h, f"xm_{cc}")
                    last = full_mm(h, OF_T[5], P3["xmbf"], 0)
                    last = bias_mm(h, 1, 0, last)
                    tick(h, last, "pe", 1, f"w1_mm_{cc}")
                    last = full_mm_h(h, OF_T[6], P3["xmbf"], 0,
                                     f"sig0_{cc}", f"sig_{cc}")
                    last = bias_mm(h, 2, 0, last)
                    tick(h, last, "pe", 1, f"w2_mm_{cc}")
                    wait(h, f"comb_{cc}")
                    last = full_mm(h, OF_T[7], P3["xmbf"], 0)
                    last = bias_mm(h, 3, 0, last)
                    tick(h, last, "pe", 1, f"w3_mm_{cc}")

            elif e == "scalar":
                for c in range(NCH):
                    wait(h, f"r1stt_{c}")
                    if c >= 1:
                        wait(h, f"stats2_mm_{c - 1}")
                    i = h.activation(bv(mx3(c), W),
                                     fv(P3["r1"] if c % 2 == 0 else P3["r1b"], W),
                                     AF.Square) if h else None
                    tick(h, i, "ac", 1, f"sq2_{c}")
                    wait(h, f"stats2_mm_{c}")
                    if c >= 1:
                        wait(h, f"xm_{c - 1}")
                    i = h.activation(fv(P3["s2t"], CH), PSf[:, 0:256],
                                     AF.Sqrt, scale=np.float32(1.0 / D)) if h else None
                    tick(h, i, "ac", 1, f"sveq2_{c}")
                    wait(h, f"w1_mm_{c}")
                    if c >= 1:
                        wait(h, f"comb_{c - 1}")
                    i = h.activation(fv(P3["sig"], W // 2), bank(0, 2),
                                     AF.Sigmoid) if h else None
                    tick(h, i, "ac", 1, f"sig0_{c}")
                    i = h.activation(fv(P3["sig"] + W // 2, W // 2), bank(2, 2),
                                     AF.Sigmoid) if h else None
                    fence(h, i, "ac")
                    sch.ms[f"sig_{c}"] = ("ac", sch.cnt["ac"])
                    i = h.activation(fv(P3["sig"], W), fv(P3["sig"], W),
                                     AF.Gelu) if h else None
                    tick(h, i, "ac", 1, f"gelu_{c}")
                    wait(h, f"x1p_{c}")
                    d = h.dma_start(out=scr_x1p[:, c * W:(c + 1) * W],
                                    in_=fv(P3["x1p"], W)) if h else None
                    tick(h, d, "dx1", 16, f"spx1_{c}")

            elif e == "gpsimd":
                slot_free = {4: ["wlin2_mm_7"], 5: ["w1_mm_7"], 6: ["w2_mm_7"],
                             7: ["w3_mm_7"], 0: ["stats2_mm_7"],
                             1: ["comb_7", "x1p_6"],
                             2: ["spx1_7", "mix_7"],
                             3: ["xm_7", "x1p_7"]}
                for ei in range(8):
                    for msn in slot_free[EPERM[ei]]:
                        wait(h, msn)
                    d = h.dma_start(out=bv(OF_T[EPERM[ei]], 8192),
                                    in_=wpk[f"e{ei}"]) if h else None
                    tick(h, d, f"de{ei}", 16, f"ld_e{ei}")

        if upto >= 3:
            emit_phase(ph3ab)

        def x1buf(c):
            return PC["x1A"] if c % 2 == 0 else PC["x1B"]

        def x0cs(tt):
            return PC["x0a"] if tt % 2 == 0 else PC["x0b"]

        def xgs(ei):
            return PC["xg0"] if ei % 2 == 0 else PC["xg1"]

        def gmtb(c):
            return OF_GMT if c % 2 == 0 else PC["gmt2"]

        def ph3c(e, h):
            B6 = 3072
            B7 = 3584

            if e == "sync":
                def ldx1(cc):
                    if cc == 0:
                        wait(h, "spx1_0")
                        wait(h, "hscan_6")
                    elif cc == 1:
                        wait(h, "spx1_1")
                        wait(h, "r1stt_6")
                    else:
                        wait(h, f"x2T_{cc - 2}_1")
                    d = h.dma_start(out=fv(x1buf(cc), W),
                                    in_=scr_x1p[:, cc * W:(cc + 1) * W]) if h else None
                    tick(h, d, "dpa" if cc % 2 == 0 else "dpb", 16, f"ldx1_{cc}")

                def ldx0(tt):
                    d = h.dma_start(out=fv(x0cs(tt), D),
                                    in_=x_in[tt * 128:(tt + 1) * 128, :]) if h else None
                    tick(h, d, "dxa" if tt % 2 == 0 else "dxb", 16, f"ldx0_{tt}")

                def st(tt):
                    wait(h, f"ev_{tt}")
                    d = h.dma_start(out=out[tt * 128:(tt + 1) * 128, :],
                                    in_=fv(PC["y"], D)) if h else None
                    tick(h, d, "dsta" if tt % 2 == 0 else "dstb", 16, f"st_{tt}")

                ldx1(0)
                ldx0(0)
                ldx0(1)
                wait(h, "x0T_0_0")
                ldx1(1)
                ldx0(2)
                wait(h, "x0T_0_1")
                ldx0(3)
                for c in range(1, NCH):
                    st(2 * c - 2)
                    st(2 * c - 1)
                    if c + 1 < NCH:
                        ldx1(c + 1)
                    if 2 * c + 2 < 2 * NCH:
                        wait(h, f"x0T_{c}_0")
                        ldx0(2 * c + 2)
                        wait(h, f"x0T_{c}_1")
                        ldx0(2 * c + 3)
                st(2 * NCH - 2)
                st(2 * NCH - 1)
                for msn in (f"st_{2 * NCH - 2}", f"st_{2 * NCH - 1}"):
                    sem, v = sch.ms[msn]
                    if h:
                        h.wait_ge(SEM[sem], v)

            elif e == "tensor":
                def pe_x0t(cc, tl):
                    tt = 2 * cc + tl
                    wait(h, f"ldx0_{tt}")
                    if tl == 0:
                        if cc == 0:
                            wait(h, "r1stt_7")
                        elif cc >= 2:
                            wait(h, f"ev_{2 * cc - 3}")
                    else:
                        wait(h, f"x1tta_{cc}")
                    last = None
                    for k in range(NCHT):
                        if h:
                            last = h.transpose(
                                PSf[:, B6 + k * 128:B6 + k * 128 + 128],
                                fv(x0cs(tt) + k * 128, 128),
                                fv(OF_IDF, 128))
                    tick(h, last, "pe", 1, f"x0T_{cc}_{tl}")

                def pe_stats3(cc):
                    wait(h, f"x1tt_{cc}")
                    wait(h, f"sq3_{cc}")
                    last = None
                    for k in range(NCHT):
                        if h:
                            last = h.matmul(PSf[0:1, B6:B6 + 256],
                                            fv(OF_ONEF, 1),
                                            fv(x1buf(cc) + k * CH, CH),
                                            start=(k == 0), stop=(k == 7))
                    for k in range(NCHT):
                        if h:
                            last = h.matmul(PSf[0:1, B6 + 256:B6 + 512],
                                            bv(OF_ONEB, 1),
                                            bv(PC["sqbf"] + k * 128, 256),
                                            start=(k == 0), stop=(k == 7))
                    tick(h, last, "pe", 1, f"stats3_mm_{cc}")

                def pe_bc3(cc):
                    wait(h, f"rows3_{cc}")
                    i = None
                    if h:
                        h.matmul(PSf[:, B7:B7 + 256],
                                 fv(OF_ONEF, 128, 0, 1),
                                 fv(OF_ROWS, 256, 0, 1), start=True, stop=True)
                        i = h.matmul(PSf[:, B7 + 256:B7 + 512],
                                     fv(OF_ONEF, 128, 0, 1),
                                     fv(OF_ROWS + 256, 256, 0, 1),
                                     start=True, stop=True)
                    tick(h, i, "pe", 1, f"bc3_mm_{cc}")

                def pe_rtr(cc):
                    wait(h, f"xt_{cc}")
                    last = None
                    for tl in range(2):
                        for k in range(NCHT):
                            if h:
                                last = h.matmul(
                                    PSf[:, B6 + tl * 8:B6 + tl * 8 + 8],
                                    fv(PC["xt"] + k * CH + tl * 128, 128),
                                    fv(OF_RW + k * 8, 8),
                                    start=(k == 0), stop=False)
                        if h:
                            last = h.matmul(
                                PSf[:, B6 + tl * 8:B6 + tl * 8 + 8],
                                fv(OF_ONEF, 128, 0, 1), fv(OF_RB, 8, 0, 1),
                                start=False, stop=True)
                    tick(h, last, "pe", 1, f"rtr_{cc}")

                def pe_gmt(cc, tl):
                    wait(h, f"gm_{cc}_{tl}")
                    i = h.transpose(
                        PSf[0:8, B6 + 256 + tl * 128:B6 + 256 + tl * 128 + 128],
                        fv(OF_GM + tl * 8, 8), fv(OF_IDF, 128)) if h else None
                    tick(h, i, "pe", 1, f"gmt_{cc}_{tl}")

                def pe_gmbc(cc, ei):
                    boff = 2048 if ei % 2 == 0 else 2560
                    wait(h, f"gmtbf_{cc}")
                    if ei >= 2:
                        wait(h, f"xg_{cc}_{ei - 2}")
                    elif cc >= 1:
                        wait(h, f"xg_{cc - 1}_{ei + 6}")
                    i = h.matmul(PSf[:, boff:boff + CH],
                                 bv(OF_OH + ei * 64, 128, 0, 8),
                                 bv(gmtb(cc), CH, 0, 8),
                                 start=True, stop=True) if h else None
                    tick(h, i, "pe", 1, f"gmbc_{cc}_{ei}")

                def pe_exp(cc, ei):
                    wait(h, f"xg_{cc}_{ei}")
                    wait(h, f"ld_e{ei}")
                    if ei == 0:
                        if cc >= 1:
                            wait(h, f"x2_{cc - 1}")
                        else:
                            wait(h, "x1p_7")
                    last = None
                    for m in range(8):
                        for k in range(8):
                            if h:
                                last = h.matmul(
                                    PSf[:, m * CH:m * CH + CH],
                                    bv(OF_T[EPERM[ei]] + (k * 8 + m) * 64, 128),
                                    b3(xgs(ei))[:, k, :],
                                    start=(ei == 0 and k == 0 and m % 2 == 0),
                                    stop=False)
                    tick(h, last, "pe", 1, f"exp_{cc}_{ei}")

                def pe_expb(cc):
                    last = None
                    for m in range(8):
                        if h:
                            last = h.matmul(PSf[:, m * CH:m * CH + CH],
                                            bv(OF_EXB + m * 64, 128, 0, 8),
                                            bv(gmtb(cc), CH, 0, 8),
                                            start=False, stop=(m % 2 == 1))
                    tick(h, last, "pe", 1, f"expb_{cc}")

                def pe_x2t(cc, tl):
                    if tl == 0:
                        wait(h, f"x2_{cc}")
                        if cc == NCH - 1:
                            wait(h, f"ev_{2 * cc - 1}")
                    else:
                        wait(h, f"ev_{2 * cc}")
                    last = None
                    for k in range(NCHT):
                        if h:
                            last = h.transpose(
                                PSf[:, B6 + k * 128:B6 + k * 128 + 128],
                                fv(x1buf(cc) + k * CH + tl * 128, 128),
                                fv(OF_IDF, 128))
                    tick(h, last, "pe", 1, f"x2T_{cc}_{tl}")

                pe_x0t(0, 0)
                pe_x0t(0, 1)
                pe_stats3(0)
                pe_bc3(0)
                pe_rtr(0)
                pe_gmt(0, 0)
                pe_gmt(0, 1)
                pe_gmbc(0, 0)
                pe_gmbc(0, 1)
                for c in range(NCH):
                    fe = c + 1 if c + 1 < NCH else None
                    pe_exp(c, 0)
                    if c >= 1:
                        pe_x2t(c - 1, 0)
                    pe_gmbc(c, 2)
                    pe_exp(c, 1)
                    if c >= 1:
                        pe_x2t(c - 1, 1)
                    pe_gmbc(c, 3)
                    pe_exp(c, 2)
                    if fe is not None:
                        pe_x0t(fe, 0)
                    pe_gmbc(c, 4)
                    pe_exp(c, 3)
                    if fe is not None:
                        pe_x0t(fe, 1)
                    pe_gmbc(c, 5)
                    pe_exp(c, 4)
                    if fe is not None:
                        pe_stats3(fe)
                    pe_gmbc(c, 6)
                    pe_exp(c, 5)
                    if fe is not None:
                        pe_bc3(fe)
                    pe_gmbc(c, 7)
                    pe_exp(c, 6)
                    if fe is not None:
                        pe_rtr(fe)
                    pe_exp(c, 7)
                    if fe is not None:
                        pe_gmt(fe, 0)
                        pe_gmt(fe, 1)
                        pe_gmbc(fe, 0)
                        pe_gmbc(fe, 1)
                    pe_expb(c)
                pe_x2t(NCH - 1, 0)
                pe_x2t(NCH - 1, 1)

            elif e == "vector":
                def dv_x1tt(cc, tl):
                    if tl == 0:
                        wait(h, f"ldx1_{cc}")
                    wait(h, f"x0T_{cc}_{tl}")
                    dst = f3(x1buf(cc))[:, :, tl * 128:tl * 128 + 128]
                    i = h.tensor_tensor(
                        dst, dst,
                        PSf[:, B6:B6 + 1024].rearrange("p (a b) -> p a b", a=8),
                        ALU.add) if h else None
                    tick(h, i, "dv", 1, f"x1tta_{cc}" if tl == 0 else f"x1tt_{cc}")

                def dv_var3(cc):
                    wait(h, f"stats3_mm_{cc}")
                    i = h.tensor_scalar(fv(OF_ROWS + 512, 256, 0, 1),
                                        PSf[0:1, B6:B6 + 256],
                                        1.0 / D, None, ALU.mult) if h else None
                    fence(h, i, "dv")
                    i = h.tensor_tensor(fv(OF_ROWS + 768, 256, 0, 1),
                                        fv(OF_ROWS + 512, 256, 0, 1),
                                        fv(OF_ROWS + 512, 256, 0, 1), ALU.mult) if h else None
                    fence(h, i, "dv")
                    i = h.scalar_tensor_tensor(
                        fv(OF_ROWS + 768, 256, 0, 1),
                        PSf[0:1, B6 + 256:B6 + 512], 1.0 / D,
                        fv(OF_ROWS + 768, 256, 0, 1), ALU.mult, ALU.subtract) if h else None
                    tick(h, i, "dv", 1, f"var3_{cc}")

                def dv_rows3(cc):
                    wait(h, f"sve3_{cc}")
                    i = h.reciprocal(fv(OF_ROWS, 256, 0, 1),
                                     fv(OF_ROWS + 1024, 256, 0, 1)) if h else None
                    fence(h, i, "dv")
                    i = h.tensor_tensor(fv(OF_ROWS + 256, 256, 0, 1),
                                        fv(OF_ROWS + 512, 256, 0, 1),
                                        fv(OF_ROWS, 256, 0, 1), ALU.mult) if h else None
                    tick(h, i, "dv", 1, f"rows3_{cc}")

                def dv_xt(cc):
                    wait(h, f"bc3_mm_{cc}")
                    i = h.tensor_tensor(f3(PC["xt"]), f3(x1buf(cc)),
                                        PSf[:, B7:B7 + 256].unsqueeze(1)
                                        .broadcast_to([128, 8, 256]), ALU.mult) if h else None
                    fence(h, i, "dv")
                    i = h.tensor_tensor(f3(PC["xt"]), f3(PC["xt"]),
                                        PSf[:, B7 + 256:B7 + 512]
                                        .unsqueeze(1).broadcast_to([128, 8, 256]),
                                        ALU.subtract) if h else None
                    fence(h, i, "dv")
                    sch.ms[f"xt_{cc}"] = ("dv", sch.cnt["dv"])

                def dv_softmax(cc):
                    for tl in range(2):
                        wait(h, f"rtr_{cc}")
                        lg = PSf[:, B6 + tl * 8:B6 + tl * 8 + 8]
                        i = h.reduce_max(fv(OF_SCOL + tl * 16, 1), lg, AX.X) if h else None
                        fence(h, i, "dv")
                        i = h.tensor_scalar(fv(OF_SCOL + tl * 16 + 1, 1),
                                            fv(OF_SCOL + tl * 16, 1), -1.0,
                                            None, ALU.mult) if h else None
                        tick(h, i, "dv", 1, f"lmax_{cc}_{tl}")
                        wait(h, f"sme_{cc}_{tl}")
                        i = h.reciprocal(fv(OF_SCOL + tl * 16 + 3, 1),
                                         fv(OF_SCOL + tl * 16 + 2, 1)) if h else None
                        fence(h, i, "dv")
                        i = h.tensor_scalar(fv(OF_GM + tl * 8, 8), lg,
                                            fv(OF_SCOL + tl * 16, 1), None,
                                            ALU.is_ge) if h else None
                        fence(h, i, "dv")
                        i = h.tensor_scalar(fv(OF_GM + tl * 8, 8),
                                            fv(OF_GM + tl * 8, 8),
                                            fv(OF_SCOL + tl * 16 + 3, 1),
                                            None, ALU.mult) if h else None
                        tick(h, i, "dv", 1, f"gm_{cc}_{tl}")

                def dv_xtbf(cc):
                    i = h.tensor_copy(bv(PC["xtbf"], W), fv(PC["xt"], W)) if h else None
                    tick(h, i, "dv", 1, f"xtbf_{cc}")

                def dv_xg(cc, ei):
                    boff = 2048 if ei % 2 == 0 else 2560
                    wait(h, f"gmbc_{cc}_{ei}")
                    if ei >= 2:
                        wait(h, f"exp_{cc}_{ei - 2}")
                    elif cc >= 1:
                        wait(h, f"exp_{cc - 1}_{ei + 6}")
                    i = h.tensor_tensor(
                        b3(xgs(ei)), b3(PC["xtbf"]),
                        PSf[:, boff:boff + CH].unsqueeze(1)
                        .broadcast_to([128, 8, 256]), ALU.mult) if h else None
                    tick(h, i, "dv", 1, f"xg_{cc}_{ei}")

                def dv_x2(cc):
                    wait(h, f"expb_{cc}")
                    i = h.tensor_tensor(fv(x1buf(cc), W), bank(0, 4),
                                        fv(x1buf(cc), W), ALU.add) if h else None
                    tick(h, i, "dv", 1, f"x2_{cc}")

                dv_x1tt(0, 0)
                dv_x1tt(0, 1)
                dv_var3(0)
                dv_rows3(0)
                dv_xt(0)
                dv_softmax(0)
                dv_xtbf(0)
                dv_xg(0, 0)
                dv_xg(0, 1)
                for c in range(NCH):
                    fe = c + 1 if c + 1 < NCH else None
                    dv_xg(c, 2)
                    dv_xg(c, 3)
                    if fe is not None:
                        dv_x1tt(fe, 0)
                    dv_xg(c, 4)
                    if fe is not None:
                        dv_x1tt(fe, 1)
                    dv_xg(c, 5)
                    if fe is not None:
                        dv_var3(fe)
                        dv_rows3(fe)
                    dv_xg(c, 6)
                    if fe is not None:
                        dv_xt(fe)
                    dv_xg(c, 7)
                    if fe is not None:
                        dv_softmax(fe)
                        dv_xtbf(fe)
                        dv_xg(fe, 0)
                    dv_x2(c)
                    if fe is not None:
                        dv_xg(fe, 1)

            elif e == "scalar":
                def ev(tt):
                    wait(h, f"x2T_{tt // 2}_{tt % 2}")
                    if tt >= 1:
                        wait(h, f"st_{tt - 1}")
                    i = h.activation(fv(PC["y"], D), PSf[:, B6:B6 + 1024],
                                     AF.Copy) if h else None
                    tick(h, i, "ac", 1, f"ev_{tt}")

                for c in range(NCH):
                    if c >= 2:
                        ev(2 * (c - 2))
                        ev(2 * (c - 2) + 1)
                    wait(h, f"x1tt_{c}")
                    if c >= 1:
                        wait(h, f"stats3_mm_{c - 1}")
                    i = h.activation(bv(PC["sqbf"], W), fv(x1buf(c), W),
                                     AF.Square) if h else None
                    tick(h, i, "ac", 1, f"sq3_{c}")
                    wait(h, f"var3_{c}")
                    i = h.activation(fv(OF_ROWS + 1024, 256, 0, 1),
                                     fv(OF_ROWS + 768, 256, 0, 1), AF.Sqrt,
                                     bias=np.float32(EPS)) if h else None
                    tick(h, i, "ac", 1, f"sve3_{c}")
                    for tl in range(2):
                        wait(h, f"lmax_{c}_{tl}")
                        i = h.activation(fv(OF_EXPS + tl * 8, 8),
                                         PSf[:, B6 + tl * 8:B6 + tl * 8 + 8],
                                         AF.Exp, bias=fv(OF_SCOL + tl * 16 + 1, 1),
                                         accum_out=fv(OF_SCOL + tl * 16 + 2, 1)) if h else None
                        tick(h, i, "ac", 1, f"sme_{c}_{tl}")
                    wait(h, f"gmt_{c}_1")
                    if c >= 2:
                        wait(h, f"expb_{c - 2}")
                    i = h.activation(bv(gmtb(c), 256, 0, 8),
                                     PSf[0:8, B6 + 256:B6 + 512],
                                     AF.Copy) if h else None
                    tick(h, i, "ac", 1, f"gmtbf_{c}")
                for tt in range(4 * NCH - 4, 4 * NCH):
                    pass
                for tt in (2 * NCH - 4, 2 * NCH - 3, 2 * NCH - 2, 2 * NCH - 1):
                    ev(tt)

            elif e == "gpsimd":
                pass

        if upto >= 4:
            emit_phase(ph3c)
        if upto < 4:
            def ph_fin(e, h):
                if e != "sync":
                    return
                fins = {0: ["setup_done"],
                        1: ([f"r0e_{2*NCH_RUN-1}", f"xne_{2*NCH_RUN-1}",
                             f"T_{2*NCH_RUN-1}", f"apply_{NCH_RUN-1}",
                             f"spr0_{NCH_RUN-1}"]
                            + ([f"wlin_mm_{NCH_RUN-1}", f"spxg_{NCH_RUN-1}",
                                f"spr0_{NCH_RUN-1}"] if p01s >= 1 else [])
                            + (([f"wconv_mm_{NCH_RUN-1}"]
                                + ([f"pref_{NCH_RUN-1}"] if "pref" not in _SKIP else [])
                                + ([f"prebf_{NCH_RUN-1}"] if "prebf" not in _SKIP else []))
                               if p01s >= 2 else [])
                            + ([f"wa_mm_{NCH_RUN-1}", f"rt_{NCH_RUN-1}"] if p01s >= 3 else [])
                            + ([f"wx_mm_{NCH_RUN-1}", f"it_{NCH_RUN-1}"] if p01s >= 4 else [])
                            + ([f"spa_{NCH_RUN-1}", f"spg_{NCH_RUN-1}",
                                f"ucols_{NCH_RUN-1}"] if p01s >= 5 else [])),
                        2: ["hin_done", "G_done", "w2b"],
                        3: ["spx1_7", "w3_mm_7", "ld_e7"]}
                for u in range(upto + 1):
                    for msn in fins.get(u, []):
                        wait(h, msn)
                dump_off = getattr(sys.modules[__name__], "_DUMP", None)
                src_ap = fv(OF_IDF, 128) if dump_off is None else fv(dump_off, 128)
                d = h.dma_start(out=out[0:128, 0:128], in_=src_ap) if h else None
                v = tick(h, d, "dsta", 16)
                if h:
                    h.wait_ge(SEM["dsta"], v)
            emit_phase(ph_fin, engines=("sync",))

    return nc


def _pack_T(Wm):
    return np.ascontiguousarray(
        Wm.reshape(8, 128, 8, 128).transpose(3, 2, 0, 1).reshape(128, 8192))


def _pack_N(Wm):
    return np.ascontiguousarray(
        Wm.reshape(8, 128, 8, 128).transpose(1, 0, 2, 3).reshape(128, 8192))


def _prep_inputs(inputs):
    def f(k):
        return np.asarray(inputs[k], dtype=np.float32)
    x = np.ascontiguousarray(f("L0_attn_x").reshape(-1, D))
    com = {}
    com["wpk_lin"] = _pack_T(f("L0_attn_W_lin"))
    com["wpk_conv"] = _pack_T(f("L0_attn_W_conv"))
    com["wpk_wa"] = _pack_T(f("L0_attn_Wa"))
    com["wpk_wx"] = _pack_T(f("L0_attn_Wx"))
    com["wpk_lin2"] = _pack_T(f("L0_attn_W_lin2"))
    com["wpk_w1"] = _pack_T(f("L0_attn_W1"))
    com["wpk_w2"] = _pack_T(f("L0_attn_W2"))
    com["wpk_w3"] = _pack_T(f("L0_attn_W3"))
    ew = f("L0_ffn_expert_w")
    for e in range(8):
        com[f"wpk_e{e}"] = _pack_N(ew[e])
    com["bias1"] = np.concatenate(
        [f("L0_attn_b_lin"), f("L0_attn_b_conv"),
         f("L0_attn_ba"), f("L0_attn_bx")]).reshape(1, 4096)
    com["bias2"] = np.concatenate(
        [f("L0_attn_b_lin2"), f("L0_attn_b1"),
         f("L0_attn_b2"), f("L0_attn_b3")]).reshape(1, 4096)
    com["exb"] = np.ascontiguousarray(f("L0_ffn_expert_b"))
    com["rw"] = np.ascontiguousarray(
        f("L0_ffn_router_w").reshape(8, 128, 8).transpose(1, 0, 2).reshape(128, 64))
    com["rb"] = f("L0_ffn_router_b").reshape(1, 8)
    com["lam"] = np.ascontiguousarray(f("L0_attn_Lam").reshape(8, 128).T)
    com["g1c"] = np.ascontiguousarray(f("L0_attn_g1").reshape(8, 128).T)
    com["g2c"] = np.ascontiguousarray(f("L0_attn_g2").reshape(8, 128).T)
    oh = np.zeros((8, D), dtype=np.float32)
    for e in range(8):
        oh[e, e * 128:(e + 1) * 128] = 1.0
    com["oh"] = oh
    com["ident"] = np.eye(128, dtype=np.float32)
    in_maps = []
    for k in range(NCORES):
        m = dict(com)
        m["x"] = np.ascontiguousarray(x[k * TOK:(k + 1) * TOK, :])
        r = k % 4
        sel = np.zeros((128, 8), dtype=np.float32)
        sel[:, :r] = 1.0
        m["sel"] = sel
        m["isel"] = np.ascontiguousarray(1.0 - sel)
        in_maps.append(m)
    return in_maps


def kernel(**inputs):
    in_maps = _prep_inputs(inputs)
    nc = build(debug=False)
    res = run_bass_kernel_spmd(nc, in_maps, core_ids=list(range(NCORES)))
    outs = [res.results[k]["out"] for k in range(NCORES)]
    full = np.concatenate(outs, axis=0).reshape(2, 8192, 1024)
    return np.ascontiguousarray(full.astype(np.float32))
